# version 19
# baseline (speedup 1.0000x reference)
from contextlib import ExitStack

import numpy as np
import ml_dtypes

import concourse.bass as bass
import concourse.mybir as mybir
from concourse.bass_utils import run_bass_kernel_spmd

F32 = mybir.dt.float32
BF16 = mybir.dt.bfloat16
I32 = mybir.dt.int32
U8 = mybir.dt.uint8
Alu = mybir.AluOpType
Act = mybir.ActivationFunctionType
AxX = mybir.AxisListType.X

B, T, D, H, DK, DV, RR, DFF = 2, 8192, 1024, 8, 64, 64, 64, 4096
TSL = T // 4
CH = 64
W = 4
NPOS = CH // W
GRAN = 512
NGR = T // GRAN
CPG = GRAN // CH
WPG = GRAN // W
DECAY, THR = 0.9, 0.5
NT = TSL // 128

bfd = ml_dtypes.bfloat16



def _c_of(tau):
    return THR * DECAY ** (-float(tau))


def _build_slots(pos):
    cs = [_c_of(4 * pos + i + 1) for i in range(W)]
    paths = [(0.0, [(None, +1)])]
    slots = []
    for i in range(W):
        new_paths = []
        for (o, terms) in paths:
            shift = (i, o)
            spiked = []
            for (ms, sg) in terms:
                nms = (shift,) if ms is None else tuple(sorted(set(ms) | {shift}))
                spiked.append((nms, sg))
                slots.append((nms, sg, i + 1))
            new_paths.append((o + cs[i], spiked))
            new_paths.append((o, terms + [(m, -sg) for (m, sg) in spiked]))
        paths = new_paths
    shifts, seen = [], set()
    for (ms, sg, lv) in slots:
        for sh in ms:
            if sh not in seen:
                seen.add(sh)
                shifts.append(sh)
    return slots, shifts


_SLOTS = [_build_slots(p) for p in range(NPOS)]
K_SLOTS = len(_SLOTS[0][0])
NSHIFT = len(_SLOTS[0][1])
LEVEL_SIZES = [1, 3, 9, 27]
SBLK = 1 + K_SLOTS


def _host_repl(v):
    return np.tile(np.asarray(v, np.float32)[None, :], (128, 1))


def host_win_weights():
    out = np.zeros((128, NPOS * SBLK), np.float32)
    for p in range(NPOS):
        slots, _ = _SLOTS[p]
        c0 = p * SBLK
        out[:, c0] = 1.0
        for j, (ms, sg, lv) in enumerate(slots):
            out[:, c0 + 1 + j] = np.float32(sg) * np.float32(_c_of(4 * p + lv))
    return out


def host_colvec(v, rows=128):
    v = np.asarray(v, np.float32)
    n = v.shape[0]
    nch = (n + rows - 1) // rows
    out = np.zeros((rows, nch), np.float32)
    for c in range(nch):
        seg = v[c * rows:(c + 1) * rows]
        out[:len(seg), c] = seg
    return out



def make_A(nc, io, nt_lim=None, do_conv=True, do_au=True):
    xs, xh = io["xs"], io["xh"]
    nw_c, cb_c, cw_c = io["nw_c"], io["cb_c"], io["cw_c"]
    g2_c, b2_c = io["g2_c"], io["b2_c"]
    wau_i, bau_c, ident = io["wau_i"], io["bau_c"], io["ident"]
    h_au, xn_fm_o, ln2_fm_o = io["h_au"], io["xn_fm_o"], io["ln2_fm_o"]

    RSTR = TSL + 4
    es = ExitStack()
    def _sb(n, sh, dt):
        return es.enter_context(nc.sbuf_tensor(n, sh, dt))
    def _ps(n, sh, dt):
        return es.enter_context(nc.psum_tensor(n, sh, dt))
    def _sem(n):
        return es.enter_context(nc.semaphore(n))
    xt = _sb("xt", [128, D], F32)
    xnt = _sb("xnt", [128, D], F32)
    sq = _sb("sq", [128, D], F32)
    tt1 = _sb("tt1", [128, D], BF16)
    tt2 = _sb("tt2", [128, D], BF16)
    tt0 = _sb("tt0", [128, D], BF16)
    st = _sb("st", [128, 16], F32)
    ids = _sb("a_ids", [128, 128], BF16)
    nw = _sb("nw", [128, 8], F32)
    cb = _sb("cb", [128, 8], F32)
    cw = _sb("cw", [128, 32], F32)
    g2 = _sb("g2", [128, 8], F32)
    b2 = _sb("b2", [128, 8], F32)
    bau = _sb("bau", [64, 1], F32)
    wauf = _sb("wauf", [128, 8 * RR], F32)
    waus = _sb("waus", [128, 8 * RR], BF16)
    xnfm = _sb("xnfm", [128, 8 * TSL], BF16)
    rnfm = _sb("rnfm", [128, 8 * RSTR], BF16)
    l2fm = _sb("l2fm", [128, 8 * TSL], BF16)
    hfm = _sb("hfm", [128, 8 * TSL], BF16)
    aufm = _sb("aufm", [64, TSL], BF16)
    cacc = _sb("cacc", [128, TSL], F32)
    pstA = _ps("pstA", [128, 512], BF16)
    pstB = _ps("pstB", [128, 512], BF16)
    psa = _ps("psa", [64, 512], F32)
    ld = _sem("ldA")
    sv = _sem("sv")
    ss = _sem("ss")
    stp = _sem("stp")
    sev = _sem("sev")
    svc = _sem("svc")
    shh = _sem("shh")
    sau = _sem("sau")
    sd = _sem("sdA")
    if True:
        NTT = (nt_lim if nt_lim is not None else NT) + 1

        def fn_sync(sy):
            sy.dma_start(out=nw[:], in_=nw_c[:]).then_inc(ld, 16)
            sy.dma_start(out=cb[:], in_=cb_c[:]).then_inc(ld, 16)
            sy.dma_start(out=cw[:], in_=cw_c[:]).then_inc(ld, 16)
            sy.dma_start(out=g2[:], in_=g2_c[:]).then_inc(ld, 16)
            sy.dma_start(out=b2[:], in_=b2_c[:]).then_inc(ld, 16)
            sy.dma_start(out=bau[:], in_=bau_c[:]).then_inc(ld, 16)
            sy.dma_start(out=ids[:], in_=ident[:]).then_inc(ld, 16)
            for kc in range(8):
                sy.dma_start(out=wauf[:, kc * RR:(kc + 1) * RR],
                             in_=wau_i[kc * 128:(kc + 1) * 128, :]).then_inc(ld, 16)
            for i in range(NTT):
                if i >= 1:
                    sy.wait_ge(ss, 10 * (i - 1) + 4)
                src = xh[:] if i == 0 else xs[(i - 1) * 128:i * 128, :]
                sy.dma_start(out=xt[:], in_=src).then_inc(ld, 16)

        def fn_vector(v):
            v.wait_ge(ld, 15 * 16)
            v.tensor_copy(waus[:], wauf[:])
            for i in range(NTT):
                v.wait_ge(ld, 16 * (15 + i + 1))
                v.tensor_reduce(st[:, 0:1], xt[:], AxX, Alu.add)
                v.drain()
                v.tensor_scalar(st[:, 0:1], st[:, 0:1], -1.0 / D, None,
                                Alu.mult).then_inc(sv, 1)
                v.wait_ge(ss, 10 * i + 1)
                v.tensor_reduce(st[:, 1:2], sq[:], AxX, Alu.add)
                v.drain()
                v.tensor_scalar(st[:, 1:2], st[:, 1:2], 1.0 / D, 1e-5, Alu.mult,
                                Alu.add).then_inc(sv, 1)
                v.wait_ge(ss, 10 * i + 2)
                v.reciprocal(st[:, 2:3], st[:, 10:11])
                v.drain()
                v.tensor_tensor(st[:, 3:4], st[:, 0:1], st[:, 2:3],
                                Alu.mult).then_inc(sv, 1)
                v.wait_ge(ss, 10 * i + 4)
                v.tensor_reduce(st[:, 4:5], sq[:], AxX, Alu.add)
                v.drain()
                v.tensor_scalar(st[:, 4:5], st[:, 4:5], 1.0 / D, 1e-6, Alu.mult,
                                Alu.add).then_inc(sv, 1)
                v.wait_ge(ss, 10 * i + 5)
                v.reciprocal(st[:, 5:6], st[:, 11:12]).then_inc(sv, 1)
                v.tensor_reduce(st[:, 6:7], xnt[:], AxX, Alu.add)
                v.drain()
                v.tensor_scalar(st[:, 6:7], st[:, 6:7], -1.0 / D, None,
                                Alu.mult).then_inc(sv, 1)
                v.wait_ge(ss, 10 * i + 7)
                v.tensor_reduce(st[:, 7:8], sq[:], AxX, Alu.add)
                v.drain()
                v.tensor_scalar(st[:, 7:8], st[:, 7:8], 1.0 / D, 1e-5, Alu.mult,
                                Alu.add).then_inc(sv, 1)
                v.wait_ge(ss, 10 * i + 8)
                v.reciprocal(st[:, 8:9], st[:, 12:13])
                v.drain()
                v.tensor_tensor(st[:, 9:10], st[:, 6:7], st[:, 8:9],
                                Alu.mult).then_inc(sv, 1)
            if not do_conv:
                return
            v.wait_ge(sev, 24 * NTT)
            for fc in range(8):
                if fc >= 1:
                    v.wait_ge(shh, fc)
                base = fc * RSTR
                v.tensor_scalar(cacc[:, 0:TSL], rnfm[:, base:base + TSL],
                                cw[:, fc * 4:fc * 4 + 1], None, Alu.mult)
                v.drain()
                last = None
                for j in range(1, 4):
                    last = v.scalar_tensor_tensor(
                        cacc[:, 0:TSL], rnfm[:, base + j:base + j + TSL],
                        cw[:, fc * 4 + j:fc * 4 + j + 1],
                        cacc[:, 0:TSL], Alu.mult, Alu.add)
                    if j < 3:
                        v.drain()
                last.then_inc(svc, 1)

        def fn_scalar(s):
            for i in range(NTT):
                s.wait_ge(sv, 8 * i + 1)
                s.activation(xnt[:], xt[:], Act.Identity, bias=st[:, 0:1], scale=1.0)
                s.activation(sq[:], xnt[:], Act.Square).then_inc(ss, 1)
                s.wait_ge(sv, 8 * i + 2)
                s.activation(st[:, 10:11], st[:, 1:2], Act.Sqrt).then_inc(ss, 1)
                s.wait_ge(sv, 8 * i + 3)
                s.activation(xnt[:], xt[:], Act.Identity, bias=st[:, 3:4], scale=st[:, 2:3])
                s.activation(sq[:], xnt[:], Act.Square).then_inc(ss, 2)
                s.wait_ge(sv, 8 * i + 4)
                s.activation(st[:, 11:12], st[:, 4:5], Act.Sqrt).then_inc(ss, 1)
                s.wait_ge(sv, 8 * i + 5)
                s.activation(tt1[:], xnt[:], Act.Identity, scale=st[:, 5:6]).then_inc(ss, 1)
                s.wait_ge(sv, 8 * i + 6)
                s.activation(sq[:], xnt[:], Act.Identity, bias=st[:, 6:7])
                s.activation(sq[:], sq[:], Act.Square).then_inc(ss, 1)
                s.wait_ge(sv, 8 * i + 7)
                s.activation(st[:, 12:13], st[:, 7:8], Act.Sqrt).then_inc(ss, 1)
                s.wait_ge(sv, 8 * i + 8)
                s.activation(tt2[:], xnt[:], Act.Identity, bias=st[:, 9:10], scale=st[:, 8:9])
                s.activation(tt0[:], xnt[:], Act.Copy).then_inc(ss, 2)
                for c in range(8):
                    ps = pstA if c % 2 == 0 else pstB
                    s.wait_ge(stp, 24 * i + 3 * c + 1)
                    if i == 0:
                        s.activation(rnfm[:, c * RSTR:c * RSTR + 3],
                                     ps[0:128, 125:128], Act.Identity,
                                     scale=nw[:, c:c + 1]).then_inc(sev, 1)
                    else:
                        s.activation(rnfm[:, c * RSTR + 3 + (i - 1) * 128:
                                          c * RSTR + 3 + i * 128],
                                     ps[0:128, 0:128], Act.Identity,
                                     scale=nw[:, c:c + 1]).then_inc(sev, 1)
                    s.wait_ge(stp, 24 * i + 3 * c + 2)
                    if i >= 1:
                        s.activation(l2fm[:, c * TSL + (i - 1) * 128:c * TSL + i * 128],
                                     ps[0:128, 128:256], Act.Identity,
                                     bias=b2[:, c:c + 1],
                                     scale=g2[:, c:c + 1]).then_inc(sev, 1)
                    else:
                        s.activation(cacc[:, 0:128], ps[0:128, 128:256],
                                     Act.Identity).then_inc(sev, 1)
                    s.wait_ge(stp, 24 * i + 3 * c + 3)
                    if i >= 1:
                        s.copy(xnfm[:, c * TSL + (i - 1) * 128:c * TSL + i * 128],
                               ps[0:128, 256:384]).then_inc(sev, 1)
                    else:
                        s.copy(cacc[:, 0:128], ps[0:128, 256:384]).then_inc(sev, 1)

            if not do_conv:
                return
            for fc in range(8):
                s.wait_ge(svc, fc + 1)
                s.activation(hfm[:, fc * TSL:(fc + 1) * TSL], cacc[:, 0:TSL],
                             Act.Silu, bias=cb[:, fc:fc + 1], scale=1.0).then_inc(shh, 1)

            if not do_au:
                return
            for nb in range(4):
                s.wait_ge(stp, 24 * NTT + 8 * (nb + 1))
                s.activation(aufm[:, nb * 512:(nb + 1) * 512], psa[:],
                             Act.Silu, bias=bau[:], scale=1.0).then_inc(sau, 1)


        def fn_tensor(t):
            for i in range(NTT):
                t.wait_ge(ss, 10 * i + 10)
                for c in range(8):
                    ps = pstA if c % 2 == 0 else pstB
                    if 24 * i + 3 * c - 3 > 0:
                        t.wait_ge(sev, 24 * i + 3 * c - 3)
                    t.transpose(ps[0:128, 0:128], tt1[:, c * 128:(c + 1) * 128], ids[:]).then_inc(stp, 1)

                    t.transpose(ps[0:128, 128:256], tt2[:, c * 128:(c + 1) * 128], ids[:]).then_inc(stp, 1)

                    t.transpose(ps[0:128, 256:384], tt0[:, c * 128:(c + 1) * 128], ids[:]).then_inc(stp, 1)

            if not do_au:
                return
            t.wait_ge(sev, 24 * NTT)
            for nb in range(4):
                if nb >= 1:
                    t.wait_ge(sau, nb)
                for kc in range(8):
                    t.matmul(psa[:, 0:512], waus[:, kc * RR:(kc + 1) * RR],
                             xnfm[:, kc * TSL + nb * 512:kc * TSL + (nb + 1) * 512],
                             start=(kc == 0), stop=(kc == 7)).then_inc(stp, 1)


        def fn_gpsimd(g):
            if do_conv:
                g.wait_ge(shh, 8)
            if do_au:
                g.wait_ge(sau, 4)
            g.wait_ge(sev, 24 * NTT)
            n = 0
            for fc in range(8):
                if do_conv:
                    g.dma_start(out=h_au[fc * 128:(fc + 1) * 128, :],
                                in_=hfm[:, fc * TSL:(fc + 1) * TSL]).then_inc(sd, 16)
                    n += 1
                g.dma_start(out=xn_fm_o[fc * 128:(fc + 1) * 128, :],
                            in_=xnfm[:, fc * TSL:(fc + 1) * TSL]).then_inc(sd, 16)
                g.dma_start(out=ln2_fm_o[fc * 128:(fc + 1) * 128, :],
                            in_=l2fm[:, fc * TSL:(fc + 1) * TSL]).then_inc(sd, 16)
                n += 2
            if do_au:
                g.dma_start(out=h_au[D:D + RR, :], in_=aufm[:]).then_inc(sd, 16)
                n += 1
            g.wait_ge(sd, n * 16)
    n_st = ((8 if do_conv else 0) + 16 + (1 if do_au else 0)) * 16
    fns = dict(sync=fn_sync, vector=fn_vector, scalar=fn_scalar,
               tensor=fn_tensor, gpsimd=fn_gpsimd)
    return fns, es, (sd, n_st)


def build_A(nt_lim=None, do_conv=True, do_au=True):
    nc = bass.Bass("TRN2", target_bir_lowering=False, detect_race_conditions=False)
    io = dict(
        xs=nc.dram_tensor("xs", [TSL, D], F32, kind="ExternalInput").ap(),
        xh=nc.dram_tensor("xh", [128, D], F32, kind="ExternalInput").ap(),
        nw_c=nc.dram_tensor("nw_c", [128, 8], F32, kind="ExternalInput").ap(),
        cb_c=nc.dram_tensor("cb_c", [128, 8], F32, kind="ExternalInput").ap(),
        cw_c=nc.dram_tensor("cw_c", [128, 32], F32, kind="ExternalInput").ap(),
        g2_c=nc.dram_tensor("g2_c", [128, 8], F32, kind="ExternalInput").ap(),
        b2_c=nc.dram_tensor("b2_c", [128, 8], F32, kind="ExternalInput").ap(),
        wau_i=nc.dram_tensor("wau", [D, RR], F32, kind="ExternalInput").ap(),
        bau_c=nc.dram_tensor("bau_c", [64, 1], F32, kind="ExternalInput").ap(),
        ident=nc.dram_tensor("ident", [128, 128], BF16, kind="ExternalInput").ap(),
        h_au=nc.dram_tensor("h_au", [D + RR, TSL], BF16, kind="ExternalOutput").ap(),
        xn_fm_o=nc.dram_tensor("xn_fm", [D, TSL], BF16, kind="ExternalOutput").ap(),
        ln2_fm_o=nc.dram_tensor("ln2h_fm", [D, TSL], BF16, kind="ExternalOutput").ap(),
    )
    fns, es, _ = make_A(nc, io, nt_lim, do_conv, do_au)
    blk = es.enter_context(nc.Block())
    with es:
        blk.sync(fns["sync"])
        blk.vector(fns["vector"])
        blk.scalar(fns["scalar"])
        blk.tensor(fns["tensor"])
        blk.gpsimd(fns["gpsimd"])
    return nc


def stage_A_inputs(inputs):
    x = np.ascontiguousarray(inputs["L0_attn_x"])
    nw_c = host_colvec(inputs["L0_attn_norm_in_w"])
    cb_c = host_colvec(inputs["L0_attn_conv_b"])
    cwm = np.ascontiguousarray(inputs["L0_attn_conv_w"][:, 0, :])
    cw_c = np.zeros((128, 32), np.float32)
    for fc in range(8):
        cw_c[:, fc * 4:(fc + 1) * 4] = cwm[fc * 128:(fc + 1) * 128, :]
    g2_c = host_colvec(inputs["L0_ffn_ln_gamma"])
    b2_c = host_colvec(inputs["L0_ffn_ln_beta"])
    wau = np.ascontiguousarray(inputs["L0_attn_Wau"], np.float32)
    bau_c = np.ascontiguousarray(inputs["L0_attn_bau"], np.float32)[:, None]
    ident = np.eye(128, dtype=bfd)
    maps = []
    for c in range(8):
        b, s = c // 4, c % 4
        xs = np.ascontiguousarray(x[b, s * TSL:(s + 1) * TSL, :], np.float32)
        if s == 0:
            xhh = np.zeros((128, D), np.float32)
        else:
            xhh = np.ascontiguousarray(x[b, s * TSL - 128:s * TSL, :], np.float32)
        maps.append(dict(xs=xs, xh=xhh, nw_c=nw_c, cb_c=cb_c, cw_c=cw_c,
                         g2_c=g2_c, b2_c=b2_c, wau=wau, bau_c=bau_c, ident=ident))
    return maps


def kernel(**inputs):
    raise NotImplementedError




def _min_plan():
    slots, shifts = _SLOTS[0]
    sh_index = {sh: i for i, sh in enumerate(shifts)}
    built = {}
    plan = []
    for j, (ms, sg, lv) in enumerate(slots):
        key = frozenset(ms)
        if key in built:
            plan.append(('copy', built[key][1]))
        elif len(ms) == 1:
            plan.append(('ref', sh_index[ms[0]]))
            built[key] = ('sl', j)
        else:
            parent = None
            for sh in ms:
                pk = frozenset(ms) - {sh}
                if pk in built:
                    parent = (built[pk], sh_index[sh])
                    break
            assert parent is not None, f"no parent for slot {j}: {ms}"
            plan.append(('min', parent[0], parent[1]))
            built[key] = ('sl', j)
    return plan


MIN_PLAN = _min_plan()
for _p in range(1, NPOS):
    assert [(tuple(tr for tr, _ in ms), sg, lv) for ms, sg, lv in _SLOTS[_p][0]] == \
           [(tuple(tr for tr, _ in ms), sg, lv) for ms, sg, lv in _SLOTS[0][0]]
    assert [tr for tr, _ in _SLOTS[_p][1]] == [tr for tr, _ in _SLOTS[0][1]]
SLOT_TR = [_SLOTS[0][1][i][0] for i in range(NSHIFT)]


def host_shift_exp():
    out = np.zeros((128, NSHIFT * 128), np.float32)
    for si in range(NSHIFT):
        offs = np.array([_SLOTS[p][1][si][1] for p in range(NPOS)], np.float32)
        out[:, si * 128:(si + 1) * 128] = np.tile(offs, CPG)[None, :]
    return out


def host_invc():
    return np.tile(np.array([1.0 / _c_of(t + 1) for t in range(CH)], np.float32),
                   (128, GRAN // CH))


def host_ipow():
    return np.tile(np.array([DECAY ** (-(t + 1.0)) for t in range(CH)], np.float32),
                   (128, GRAN // CH))


def host_hpow():
    return np.tile(np.array([THR * DECAY ** (-(t + 1.0)) for t in range(CH)],
                            np.float32), (128, GRAN // CH))


class _EngProxy:

    def __init__(self, e):
        object.__setattr__(self, "_e", e)
        object.__setattr__(self, "_last", [None])

    def __getattr__(self, name):
        a = getattr(self._e, name)
        if not callable(a):
            return a
        def w(*args, **kw):
            r = a(*args, **kw)
            if r is not None and hasattr(r, "then_inc"):
                self._last[0] = r
            return r
        return w


class Eng:

    def __init__(self, sem):
        self.ops = []
        self.sem = sem
        self.n = 0

    def wait(self, other, val):
        if val > 0:
            self.ops.append(("w", lambda e, o=other.sem, v=val: e.wait_ge(o, v)))

    def wait_sem(self, sem, val):
        self.ops.append(("w", lambda e, o=sem, v=val: e.wait_ge(o, v)))

    def do(self, fn):
        self.ops.append(("d", fn))

    def tick(self, k=1):
        self.n += k
        self.ops.append(("t", k))
        return self.n

    def run(self, e):
        p = _EngProxy(e)
        for kind, f in self.ops:
            if kind == "t":
                li = p._last[0]
                assert li is not None, "tick with no preceding instruction"
                li.then_inc(self.sem, f)
            else:
                f(p)


class ECtx:

    def __init__(self, nc, es):
        self.ld = es.enter_context(nc.semaphore("ld"))
        self.ldg = es.enter_context(nc.semaphore("ldg"))
        self.tks = es.enter_context(nc.semaphore("tk"))
        self.sks = es.enter_context(nc.semaphore("sk"))
        self.vks = es.enter_context(nc.semaphore("vk"))
        self.gks = es.enter_context(nc.semaphore("gk"))
        self.TE, self.SE = Eng(self.tks), Eng(self.sks)
        self.VE, self.GE = Eng(self.vks), Eng(self.gks)
        self.dma_n = [0]
        self.dmas = []
        self.gdma_n = [0]

    def dma(self, dst, src, wait=None):
        self.dmas.append((wait, lambda sy, d=dst, s=src: sy.dma_start(out=d, in_=s)))
        self.dma_n[0] += 16
        return self.dma_n[0]


def emit_B(nc, E, io, Ttot=T, dbg=False, fused=False):
    ngr = Ttot // GRAN
    wq_i, wk_i, wv_i, wsp_i = io["wq_i"], io["wk_i"], io["wv_i"], io["wsp_i"]
    wad_i, bad_i, wbe_i, bbe_i = io["wad_i"], io["bad_i"], io["wbe_i"], io["bbe_i"]
    asp_i, bsp_i, hnw_i = io["asp_i"], io["bsp_i"], io["hnw_i"]
    oexp_i, wts_i = io["oexp_i"], io["wts_i"]
    invc_i, ipow_i, hpow_i = io["invc_i"], io["ipow_i"], io["hpow_i"]
    ident_i, mau_i, mui_i = io["ident_i"], io["mau_i"], io["mui_i"]
    if fused:
        hall, yrs, hsel_i = io["hall"], io["yrs"], io["hsel_i"]
        def hsl(r0, nr, gb0, w_):
            s_ = gb0 // TSL
            c0 = gb0 % TSL
            return hall[s_ * (D + RR) + r0:s_ * (D + RR) + r0 + nr, c0:c0 + w_]
    else:
        hin, yout, gout = io["hin"], io["yout"], io["gout"]
        def hsl(r0, nr, gb0, w_):
            return hin[r0:r0 + nr, gb0:gb0 + w_]

    es = ExitStack()
    def _sb(n, sh, dt):
        return es.enter_context(nc.sbuf_tensor(n, sh, dt))
    def _ps(n, sh, dt):
        return es.enter_context(nc.psum_tensor(n, sh, dt))
    wqs = _sb("wqs", [128, 8 * 128], BF16)
    wks = _sb("wks", [128, 8 * 128], BF16)
    wvs = _sb("wvs", [128, 8 * 128], BF16)
    wsps = _sb("wsps", [128, 8 * 128], BF16)
    wads = _sb("wads", [64, 128], BF16)
    wbes = _sb("wbes", [128, 16], BF16)
    badc = _sb("badc", [128, 1], F32)
    bbec = _sb("bbec", [128, 2], F32)
    aspc = _sb("aspc", [128, 1], F32)
    bspc = _sb("bspc", [128, 2], BF16)
    hnwt = _sb("hnwt", [128, 64], F32)
    oexp = _sb("oexp_s", [128, NSHIFT * 128], F32)
    wts = _sb("wts_s", [128, NPOS * SBLK], F32)
    invc = _sb("invc", [128, GRAN], F32)
    ipow = _sb("ipow", [128, GRAN], F32)
    hpow = _sb("hpow", [128, GRAN], F32)
    ids = _sb("ids", [128, 128], BF16)
    mUS = _sb("mUS", [128, 64], U8)
    mUI = _sb("mUI", [128, 64], U8)
    zt64 = _sb("zt64", [128, 64], F32)
    onesg = _sb("onesg", [128, GRAN], F32)
    o9g = _sb("o9g", [128, GRAN], F32)
    ones64 = _sb("ones64", [128, 64], BF16)
    ones64f = _sb("ones64f", [128, 64], F32)
    hgP = [_sb("hg0", [128, 8 * GRAN], BF16), _sb("hg1", [128, 8 * GRAN], BF16)]
    agbP = [_sb("agb0", [64, GRAN], BF16), _sb("agb1", [64, GRAN], BF16)]
    qraw = _sb("qraw", [128, GRAN], F32)
    kraw = _sb("kraw", [128, GRAN], F32)
    qn = _sb("qn", [128, GRAN], BF16)
    kn = _sb("kn", [128, GRAN], BF16)
    dg = _sb("dg", [128, GRAN], F32)
    abg = _sb("abg", [128, GRAN], F32)
    sqb = _sb("sqb", [128, GRAN], F32)
    nrm = _sb("nrm", [128, GRAN], F32)
    vts = _sb("vts", [128, 8 * 64], F32)
    bbt = _sb("bbt", [128, 8], F32)
    bdt = _sb("bdt", [128, 8], F32)
    gst = _sb("gst", [128, 8], F32)
    bcol = _sb("bcol", [128, 8], F32)
    nbcol = _sb("nbcol", [128, 8], F32)
    gsum = _sb("gsum", [128, GRAN], F32)
    gfm = _sb("gfm", [128, GRAN], BF16)
    gaf = _sb("gaf", [128, GRAN], F32)
    argf = _sb("argf", [128, GRAN], F32)
    laf = _sb("laf", [128, GRAN], F32)
    actf = _sb("actf", [128, GRAN], F32)
    Lg = _sb("Lg", [128, GRAN], F32)
    E1g = _sb("E1g", [128, GRAN], F32)
    E3g = _sb("E3g", [128, GRAN], F32)
    E4g = _sb("E4g", [128, GRAN], F32)
    ktb = _sb("ktb", [128, GRAN], BF16)
    ktpb = _sb("ktpb", [128, GRAN], BF16)
    khpb = _sb("khpb", [128, GRAN], BF16)
    qtb = _sb("qtb", [128, GRAN], BF16)
    qtpb = _sb("qtpb", [128, GRAN], BF16)
    kbarb = _sb("kbarb", [128, GRAN], BF16)
    ecol = _sb("ecol", [128, 16], F32)
    Egb = _sb("Egb", [128, GRAN], F32)
    thg = _sb("thg", [128, GRAN], F32)
    thr = _sb("thr", [128, WPG * 51], F32)
    wnd = _sb("wnd", [128, WPG * SBLK], F32)
    gb_ = _sb("gb", [128, 64], F32)
    AbarP = [_sb("Abar0", [128, 64], BF16), _sb("Abar1", [128, 64], BF16)]
    AorP = [_sb("Aor0", [128, 64], BF16), _sb("Aor1", [128, 64], BF16)]
    Atmpb = _sb("Atmpb", [128, 64], BF16)
    Atmpo = _sb("Atmpo", [128, 64], BF16)
    Rb = _sb("Rb", [128, 64], BF16)
    Yb = _sb("Yb", [128, 64], BF16)
    Ub = _sb("Ub", [128, 64], BF16)
    kbtmP = [_sb("kbtm0", [128, 64], BF16), _sb("kbtm1", [128, 64], BF16)]
    Sst = _sb("Sst", [128, 64], BF16)
    sscb = _sb("ssc", [128, 64], BF16)
    wmbP = [_sb("wmb0", [128, 64], BF16), _sb("wmb1", [128, 64], BF16)]
    wvt = _sb("wvt", [128, 64], F32)
    ytmp = _sb("ytmp", [128, 64], F32)
    sqy = _sb("sqy", [128, 64], F32)
    ynm = _sb("ynm", [128, 64], BF16)
    ytm = _sb("ytm", [128, 64], BF16)
    yfm = _sb("yfm", [128, GRAN], BF16)
    styc = _sb("styc", [128, 4], F32)
    pproj = _ps("pproj", [128, 512], F32)
    pv = _ps("pv", [128, 128], F32)
    pb1 = _ps("pb1", [128, 64], F32)
    pb2 = _ps("pb2", [128, 64], F32)
    pY = _ps("pY", [128, 64], F32)
    pSO = _ps("pSO", [128, 128], F32)
    ptr = _ps("ptr", [128, 128], BF16)
    pbc = _ps("pbc", [128, 512], F32)
    ysc = _sb("ysc", [128, 4 * GRAN], BF16) if fused else None
    hselc = _sb("hselc", [128, 4], F32) if fused else None

    if True:
        TE, SE, VE, GE = E.TE, E.SE, E.VE, E.GE
        ld, ldg = E.ld, E.ldg
        tks, sks, vks, gks = E.tks, E.sks, E.vks, E.gks
        dma, dma_n, dmas = E.dma, E.dma_n, E.dmas

        sd_n = [0]
        lastuseB = {}

        def bank_waitB(eng, bank):
            lu = lastuseB.get(id(bank))
            if lu is not None:
                eng.wait(lu[0], lu[1])

        cl = [(badc[:], bad_i[:]), (bbec[:], bbe_i[:]),
              (aspc[:], asp_i[:]), (hnwt[:], hnw_i[:]),
              (oexp[:], oexp_i[:]), (wts[:], wts_i[:]), (invc[:], invc_i[:]),
              (ipow[:], ipow_i[:]), (hpow[:], hpow_i[:]), (ids[:], ident_i[:]),
              (mUS[:], mau_i[:]), (mUI[:], mui_i[:])]
        if fused:
            cl.append((hselc[:], hsel_i[:]))
        for ci, (dst, src) in enumerate(cl):
            dma(dst, src, io.get("const_wait") if ci == 0 else None)
        gload = []
        for kc in range(8):
            gload.append((wqs[:, kc * 128:(kc + 1) * 128],
                          wq_i[kc * 128:(kc + 1) * 128, :]))
            gload.append((wks[:, kc * 128:(kc + 1) * 128],
                          wk_i[kc * 128:(kc + 1) * 128, :]))
            gload.append((wvs[:, kc * 128:(kc + 1) * 128],
                          wv_i[kc * 128:(kc + 1) * 128, :]))
            gload.append((wsps[:, kc * 128:(kc + 1) * 128],
                          wsp_i[kc * 128:(kc + 1) * 128, :]))
            gload.append((wbes[:, kc * 2:kc * 2 + 2],
                          wbe_i[kc * 128:(kc + 1) * 128, :]))
        gload.append((wads[:], wad_i[:]))
        gload.append((bspc[:], bsp_i[:]))
        def fgl(g_):
            for dst, src in gload:
                g_.dma_start(out=dst, in_=src).then_inc(ldg, 16)
        GE.do(fgl)
        ngl = E.gdma_n[0] + 16 * len(gload)
        E.gdma_n[0] = ngl
        if fused:
            cw_ = io["const_wait"]
            VE.wait_sem(cw_[0], cw_[1])
            SE.wait_sem(cw_[0], cw_[1])
            TE.wait_sem(cw_[0], cw_[1])
        VE.do(lambda v: v.memset(onesg[:], 1.0))
        VE.do(lambda v: v.memset(o9g[:], DECAY))
        VE.do(lambda v: v.memset(zt64[:], 0.0))
        VE.do(lambda v: v.memset(ones64[:], 1.0))
        VE.do(lambda v: v.memset(ones64f[:], 1.0))
        VE.do(lambda v: v.memset(wnd[:, 0:1], 0.0))
        VE.do(lambda v: v.memset(Sst[:], 0.0))
        vinit = VE.tick()
        TE.wait(VE, vinit)
        TE.wait_sem(ld, dma_n[0])
        TE.wait_sem(ldg, ngl)
        SE.wait(VE, vinit)
        GE.wait(VE, vinit)

        BND2 = float(THR)
        BNDM = float(DECAY ** 64)

        hv_hist = []
        for g in range(ngr):
            gb0 = g * GRAN
            g0w = io.get("g0_wait") if fused else None
            hg = hgP[g % 2]
            agb = agbP[g % 2]
            hwait = (tks, hv_hist[g - 2]) if g >= 2 else (g0w if g == 0 else None)
            for kc in range(8):
                dma(hg[:, kc * GRAN:(kc + 1) * GRAN],
                    hsl(kc * 128, 128, gb0, GRAN), hwait)
            dma(agb[:], hsl(D, RR, gb0, GRAN), hwait)
            TE.wait_sem(ld, dma_n[0])

            def proj(wtile, dst, bias=None):
                def ft(t, w=wtile, hh=hg):
                    for kc in range(8):
                        t.matmul(pproj[:, 0:GRAN], w[:, kc * 128:(kc + 1) * 128],
                                 hh[:, kc * GRAN:(kc + 1) * GRAN],
                                 start=(kc == 0), stop=(kc == 7), skip_group_check=True)
                TE.do(ft)
                tv = TE.tick()
                SE.wait(TE, tv)
                def fs(s, d=dst, b=bias):
                    if b is not None:
                        s.activation(d, pproj[:, 0:GRAN], Act.Identity, bias=b)
                    else:
                        s.activation(d, pproj[:, 0:GRAN], Act.Identity)
                SE.do(fs)
                sv = SE.tick()
                TE.wait(SE, sv)
                return sv

            proj(wqs, qraw[:])
            proj(wks, kraw[:])
            sv_d = proj(wsps, dg[:])
            def fab(t, ab_=agb):
                t.matmul(pproj[:, 0:GRAN], wads[:], ab_[:], start=True, stop=True, skip_group_check=True)
            TE.do(fab)
            tv = TE.tick()
            SE.wait(TE, tv)
            SE.do(lambda s: s.activation(abg[:], pproj[:, 0:GRAN], Act.Identity,
                                         bias=badc[:]))
            sv_ab = SE.tick()
            TE.wait(SE, sv_ab)

            bank_waitB(TE, pv)
            for tb in range(4):
                def fv(t, tb=tb, hh=hg):
                    for kc in range(8):
                        t.matmul(pv[:, 0:128],
                                 hh[:, kc * GRAN + tb * 128:kc * GRAN + (tb + 1) * 128],
                                 wvs[:, kc * 128:(kc + 1) * 128],
                                 start=(kc == 0), stop=(kc == 7), skip_group_check=True)
                TE.do(fv)
                tv = TE.tick()
                SE.wait(TE, tv)
                def fsv(s, tb=tb):
                    for half in range(2):
                        ch = 2 * tb + half
                        rows = slice(half * 64, half * 64 + 64)
                        s.copy(vts[0:64, ch * 64:(ch + 1) * 64], pv[rows, 0:64])
                        s.copy(vts[64:128, ch * 64:(ch + 1) * 64], pv[rows, 64:128])
                SE.do(fsv)
                sv = SE.tick()
                TE.wait(SE, sv)
                def fbb(t, tb=tb, hh=hg):
                    for kc in range(8):
                        t.matmul(pv[:, 0:2],
                                 hh[:, kc * GRAN + tb * 128:kc * GRAN + (tb + 1) * 128],
                                 wbes[:, kc * 2:kc * 2 + 2],
                                 start=(kc == 0), stop=(kc == 7), skip_group_check=True)
                TE.do(fbb)
                tv = TE.tick()
                SE.wait(TE, tv)
                def fsb(s, tb=tb):
                    for half in range(2):
                        ch = 2 * tb + half
                        rows = slice(half * 64, half * 64 + 64)
                        s.activation(bbt[0:64, ch:ch + 1], pv[rows, 0:1],
                                     Act.Identity, bias=bbec[0:64, 0:1])
                        s.activation(bbt[64:128, ch:ch + 1], pv[rows, 1:2],
                                     Act.Identity, bias=bbec[64:128, 1:2])
                SE.do(fsb)
                sv = SE.tick()
                TE.wait(SE, sv)
                lastuseB[id(pv)] = (SE, sv)
            hv_hist.append(TE.n)

            def qknorm(raw, out_bf):
                SE.wait(VE, VE.n)
                SE.do(lambda s, r=raw: s.activation(sqb[:], r, Act.Square))
                sv1 = SE.tick()
                TE.wait(SE, sv1)
                bank_waitB(TE, pbc)
                def fsum(t):
                    t.matmul(pbc[0:64, 0:GRAN], ones64f[0:64, 0:64], sqb[0:64, :],
                             start=True, stop=True, skip_group_check=True)
                    t.matmul(pbc[64:128, 0:GRAN], ones64f[64:128, 0:64],
                             sqb[64:128, :], start=True, stop=True,
                             tile_position=(64, 64), skip_group_check=True)
                TE.do(fsum)
                tv1 = TE.tick()
                SE.wait(TE, tv1)
                SE.do(lambda s: s.activation(nrm[:], pbc[:, 0:GRAN], Act.Sqrt))
                sv2 = SE.tick()
                lastuseB[id(pbc)] = (SE, sv2)
                TE.wait(SE, sv2)
                VE.wait(SE, sv2)
                def fnrm(v, r=raw, o=out_bf):
                    v.tensor_scalar(nrm[:], nrm[:], 1e-6, None, Alu.add)
                    v.drain()
                    v.reciprocal(nrm[:], nrm[:])
                    v.drain()
                    v.tensor_tensor(o, r, nrm[:], Alu.mult)
                VE.do(fnrm)
                return VE.tick()

            gv_q = qknorm(qraw[:], qn[:])
            gv_k = qknorm(kraw[:], kn[:])

            VE.wait(SE, sv_d)
            def fscan(v):
                for ch in range(CPG):
                    sl = slice(ch * CH, (ch + 1) * CH)
                    v.tensor_tensor_scan(Egb[:, sl], o9g[:, sl], dg[:, sl], 0.0,
                                         Alu.mult, Alu.add)
                v.drain()
                v.tensor_tensor(thg[:], Egb[:], ipow[:], Alu.mult)
                v.drain()
                v.tensor_tensor(thg[:], thg[:], hpow[:], Alu.subtract)
                v.drain()
            VE.do(fscan)
            vv_th = VE.tick()

            scr_map, nscr, ref_dest = {}, 0, {}
            for j, op in enumerate(MIN_PLAN):
                if op[0] == 'ref':
                    ref_dest[op[1]] = j + 1
            ref_dest = {}
            for j, op in enumerate(MIN_PLAN):
                if op[0] == 'ref':
                    ref_dest[op[1]] = j
            for si in range(NSHIFT):
                if si not in ref_dest:
                    scr_map[si] = 40 + nscr
                    nscr += 1
            assert nscr <= 11

            def sh_dest(si):
                return ref_dest[si] if si in ref_dest else scr_map[si]

            def fshift(v):
                for si in range(NSHIFT):
                    tr = SLOT_TR[si]
                    v.tensor_tensor(thr[:, sh_dest(si)::51], thg[:, tr::4],
                                    oexp[:, si * 128:(si + 1) * 128],
                                    Alu.subtract)
                v.drain()
            VE.do(fshift)

            emit_ops = []
            for j, op in enumerate(MIN_PLAN):
                if op[0] == 'ref':
                    continue
                if op[0] == 'copy':
                    emit_ops.append((j, {op[1]}, op))
                else:
                    src0 = op[1]
                    s0 = src0[1] if src0[0] == 'sl' else sh_dest(src0[1])
                    emit_ops.append((j, {s0, sh_dest(op[2])}, op))

            def fmins(v):
                for k, (j, _, op) in enumerate(emit_ops):
                    if op[0] == 'copy':
                        v.tensor_copy(thr[:, j::51], thr[:, op[1]::51])
                    else:
                        src0 = op[1]
                        a_ap = thr[:, src0[1]::51] if src0[0] == 'sl' else \
                            thr[:, sh_dest(src0[1])::51]
                        v.tensor_tensor(thr[:, j::51], a_ap,
                                        thr[:, sh_dest(op[2])::51], Alu.min)
                    if any(j in emit_ops[k2][1]
                           for k2 in range(k + 1, min(k + 4, len(emit_ops)))):
                        v.drain()
                v.drain()
            VE.do(fmins)

            def fchain(v):
                for w_ in range(WPG):
                    ch, p = divmod(w_, NPOS)
                    wb = w_ * SBLK
                    nxt = ((w_ + 1) % WPG) * SBLK
                    v.tensor_scalar(wnd[:, wb + 1:wb + SBLK],
                                    thr[:, w_ * 51:w_ * 51 + 40],
                                    wnd[:, wb:wb + 1], None, Alu.is_gt)
                    v.drain()
                    v.scalar_tensor_tensor(
                        wnd[:, wb:wb + SBLK], wnd[:, wb:wb + SBLK], 1.0,
                        wts[:, p * SBLK:(p + 1) * SBLK], Alu.mult, Alu.mult,
                        accum_out=wnd[:, nxt:nxt + 1])
                    v.drain()
                    if p == NPOS - 1:
                        v.tensor_scalar(wnd[:, nxt:nxt + 1], wnd[:, nxt:nxt + 1],
                                        thg[:, ch * CH + CH - 1:ch * CH + CH],
                                        BNDM, Alu.subtract, Alu.mult)
                        v.drain()
                        v.tensor_scalar(wnd[:, nxt:nxt + 1], wnd[:, nxt:nxt + 1],
                                        BND2, None, Alu.subtract)
                        v.drain()
            VE.do(fchain)
            VE.tick()

            def fextr(v):
                base = 1
                for lv in range(W):
                    seg = LEVEL_SIZES[lv]
                    if seg == 1:
                        v.tensor_copy(gsum[:, lv::4], wnd[:, base::SBLK])
                    else:
                        in_ap = wnd[:].rearrange("p (w s) -> p w s", s=SBLK)[:, :, base:base + seg]
                        v.tensor_reduce(gsum[:, lv::4], in_ap, AxX, Alu.add)
                    base += seg
                v.drain()
                v.tensor_tensor(gfm[:], gsum[:], invc[:], Alu.mult)
            VE.do(fextr)
            vv_g = VE.tick()

            SE.wait(VE, vv_g)
            SE.do(lambda s: s.activation(gaf[:], gfm[:], Act.Identity, scale=aspc[:]))
            sv_ga = SE.tick()
            VE.wait(SE, sv_ga)
            VE.do(lambda v: v.tensor_tensor(argf[:], abg[:], gaf[:], Alu.add))
            vv_arg = VE.tick()
            TE.wait(VE, vv_g)
            bank_waitB(TE, pbc)
            def factive(t):
                t.matmul(pbc[0:64, 0:GRAN], ones64[0:64, 0:64], gfm[0:64, :],
                         start=True, stop=True, skip_group_check=True)
                t.matmul(pbc[64:128, 0:GRAN], ones64[64:128, 0:64], gfm[64:128, :],
                         start=True, stop=True, tile_position=(64, 64), skip_group_check=True)
            TE.do(factive)
            tv_act = TE.tick()
            VE.wait(TE, tv_act)
            VE.do(lambda v: v.tensor_scalar(actf[:], pbc[:, 0:GRAN], 0.5, None,
                                            Alu.is_gt))
            vv_actf = VE.tick()
            lastuseB[id(pbc)] = (VE, vv_actf)
            TE.wait(VE, vv_actf)
            SE.wait(VE, vv_arg)
            SE.do(lambda s: s.activation(argf[:], argf[:], Act.Sigmoid))
            sv_sg = SE.tick()
            VE.wait(SE, sv_sg)
            VE.do(lambda v: v.tensor_scalar(argf[:], argf[:], 1e-30, None, Alu.max))
            vv_gd = VE.tick()
            SE.wait(VE, vv_gd)
            SE.do(lambda s: s.activation(argf[:], argf[:], Act.Ln))
            sv_ln = SE.tick()
            VE.wait(SE, sv_ln)
            VE.do(lambda v: v.tensor_tensor(laf[:], argf[:], actf[:], Alu.mult))
            VE.do(lambda v: v.drain())
            for ch in range(CPG):
                def flsc1(v, ch=ch):
                    sl = slice(ch * CH, (ch + 1) * CH)
                    v.tensor_tensor_scan(Lg[:, sl], onesg[:, sl], laf[:, sl],
                                         0.0, Alu.mult, Alu.add)
                VE.do(flsc1)
                vv_c = VE.tick()
                SE.wait(VE, vv_c)
                def fexp1(s, ch=ch):
                    sl = slice(ch * CH, (ch + 1) * CH)
                    mid = ch * CH + CH // 2 - 1
                    end = ch * CH + CH - 1
                    s.activation(E1g[:, sl], Lg[:, sl], Act.Exp)
                    s.activation(E3g[:, sl], Lg[:, sl], Act.Exp, scale=-1.0,
                                 bias=Lg[:, mid:mid + 1])
                    s.activation(E4g[:, sl], Lg[:, sl], Act.Exp, scale=-1.0,
                                 bias=Lg[:, end:end + 1])
                    s.activation(ecol[:, 2 * ch:2 * ch + 1], Lg[:, mid:mid + 1],
                                 Act.Exp, scale=-1.0)
                    s.activation(ecol[:, 2 * ch + 1:2 * ch + 2],
                                 Lg[:, end:end + 1], Act.Exp)
                SE.do(fexp1)
                sv_exp = SE.tick()

            def fbd(t):
                for tb in range(4):
                    t.matmul(pv[:, 0:2], gfm[0:64, tb * 128:(tb + 1) * 128],
                             bspc[0:64, 0:2], start=True, stop=True, skip_group_check=True)
                    t.matmul(pv[:, 2:4], gfm[64:128, tb * 128:(tb + 1) * 128],
                             bspc[64:128, 0:2], start=True, stop=True,
                             tile_position=(64, 0), skip_group_check=True)
                    t.sem_inc(tks, 1)
                    t.wait_ge(sks, 0)
            TE.wait(VE, vv_g)
            bank_waitB(TE, pv)
            for tb in range(4):
                def fbd1(t, tb=tb):
                    t.matmul(pv[:, 0:2], gfm[0:64, tb * 128:(tb + 1) * 128],
                             bspc[0:64, 0:2], start=True, stop=True, skip_group_check=True)
                    t.matmul(pv[:, 2:4], gfm[64:128, tb * 128:(tb + 1) * 128],
                             bspc[64:128, 0:2], start=True, stop=True,
                             tile_position=(64, 0), skip_group_check=True)
                TE.do(fbd1)
                tv = TE.tick()
                SE.wait(TE, tv)
                def fbst(s, tb=tb):
                    for half in range(2):
                        ch = 2 * tb + half
                        rows = slice(half * 64, half * 64 + 64)
                        s.copy(bdt[0:64, ch:ch + 1], pv[rows, 0:1])
                        s.copy(bdt[64:128, ch:ch + 1], pv[rows, 2:3])
                        s.copy(gst[0:64, ch:ch + 1], pv[rows, 1:2])
                        s.copy(gst[64:128, ch:ch + 1], pv[rows, 3:4])
                SE.do(fbst)
                sv = SE.tick()
                TE.wait(SE, sv)
                lastuseB[id(pv)] = (SE, sv)
            VE.wait(SE, SE.n)
            def fbeta(v):
                v.tensor_tensor(bcol[:], bbt[:], bdt[:], Alu.add)
            VE.do(fbeta)
            vv_bc = VE.tick()
            SE.wait(VE, vv_bc)
            SE.do(lambda s: s.activation(bcol[:], bcol[:], Act.Sigmoid))
            sv_bs = SE.tick()
            VE.wait(SE, sv_bs)
            def fbfin(v):
                v.tensor_scalar(gst[:], gst[:], 0.5, None, Alu.is_gt)
                v.drain()
                v.tensor_tensor(bcol[:], bcol[:], gst[:], Alu.mult)
                v.drain()
                v.tensor_scalar(nbcol[:], bcol[:], -1.0, None, Alu.mult)
            VE.do(fbfin)
            vv_bf = VE.tick()

            VE.wait(SE, sv_exp)
            def fsc(v):
                v.tensor_tensor(ktb[:], kn[:], E1g[:], Alu.mult)
                v.tensor_tensor(qtb[:], qn[:], E1g[:], Alu.mult)
                v.tensor_tensor(khpb[:], kn[:], E3g[:], Alu.mult)
                v.tensor_tensor(kbarb[:], kn[:], E4g[:], Alu.mult)
            VE.do(fsc)
            gv_sc = VE.tick()
            SE.wait(VE, gv_sc)
            def fscp(s):
                for ch in range(CPG):
                    sl = slice(ch * CH, (ch + 1) * CH)
                    s.activation(ktpb[:, sl], ktb[:, sl], Act.Identity,
                                 scale=ecol[:, 2 * ch:2 * ch + 1])
                    s.activation(qtpb[:, sl], qtb[:, sl], Act.Identity,
                                 scale=ecol[:, 2 * ch:2 * ch + 1])
            SE.do(fscp)
            sv_scp = SE.tick()

            def emit_front(ch):
                sl = slice(ch * CH, (ch + 1) * CH)
                par = ch % 2
                Ab = AbarP[par][:]
                Ao = AorP[par][:]
                wmb_c = wmbP[par][:]
                kbtm_c = kbtmP[par][:]
                TE.wait(SE, sv_scp)
                TE.wait(VE, vv_bf)
                bank_waitB(TE, pbc)
                def fgt(t, sl=sl):
                    t.matmul(pbc[0:64, 0:64], khpb[0:64, sl], ktpb[0:64, sl],
                             start=True, stop=True, skip_group_check=True)
                    t.matmul(pbc[64:128, 0:64], khpb[64:128, sl],
                             ktpb[64:128, sl], start=True, stop=True,
                             tile_position=(64, 64), skip_group_check=True)
                TE.do(fgt)
                tv = TE.tick()
                SE.wait(TE, tv)
                bank_waitB(SE, gb_)
                def fgb(s, ch=ch):
                    s.activation(gb_[:], pbc[:, 0:64], Act.Identity,
                                 scale=nbcol[:, ch:ch + 1])
                SE.do(fgb)
                sv = SE.tick()
                lastuseB[id(pbc)] = (SE, sv)
                VE.wait(SE, sv)
                def fsel(v, Ab=Ab):
                    v.select(Ab, mUS[:], gb_[:], zt64[:])
                VE.do(fsel)
                vv = VE.tick()
                lastuseB[id(gb_)] = (VE, vv)
                TE.wait(VE, vv)
                bank_waitB(TE, ptr)
                def ftr(t, Ab=Ab):
                    t.transpose(ptr[0:64, 0:64], Ab[0:64, :], ids[0:64, 0:64])
                    t.transpose(ptr[64:128, 64:128], Ab[64:128, :],
                                ids[64:128, 64:128], tile_position=(64, 64))
                TE.do(ftr)
                tv = TE.tick()
                SE.wait(TE, tv)
                def fevtr(s, Ao=Ao):
                    s.copy(Ao[0:64, :], ptr[0:64, 0:64])
                    s.copy(Ao[64:128, :], ptr[64:128, 64:128])
                SE.do(fevtr)
                sv_ao = SE.tick()
                lastuseB[id(ptr)] = (SE, sv_ao)

            def emit_front2(ch):
                sl = slice(ch * CH, (ch + 1) * CH)
                par = ch % 2
                wmb_c = wmbP[par][:]
                kbtm_c = kbtmP[par][:]
                bank_waitB(TE, pv)
                def fw(t, sl=sl):
                    t.matmul(pv[0:64, 0:64], khpb[0:64, sl], qtpb[0:64, sl],
                             start=True, stop=True, skip_group_check=True)
                    t.matmul(pv[64:128, 0:64], khpb[64:128, sl],
                             qtpb[64:128, sl], start=True, stop=True,
                             tile_position=(64, 64), skip_group_check=True)
                TE.do(fw)
                tv = TE.tick()
                SE.wait(TE, tv)
                bank_waitB(SE, wvt)
                SE.do(lambda s: s.activation(wvt[:], pv[:, 0:64], Act.Identity))
                sv = SE.tick()
                lastuseB[id(pv)] = (SE, sv)
                VE.wait(SE, sv)
                def fselw(v, wmb_c=wmb_c):
                    v.select(wmb_c, mUI[:], wvt[:], zt64[:])
                VE.do(fselw)
                vv_w = VE.tick()
                lastuseB[id(wvt)] = (VE, vv_w)
                bank_waitB(TE, ptr)
                def fktr(t, sl=sl):
                    t.transpose(ptr[0:64, 0:64], kbarb[0:64, sl], ids[0:64, 0:64])
                    t.transpose(ptr[64:128, 64:128], kbarb[64:128, sl],
                                ids[64:128, 64:128], tile_position=(64, 64))
                TE.do(fktr)
                tv = TE.tick()
                SE.wait(TE, tv)
                def fekt(s, kbtm_c=kbtm_c):
                    s.copy(kbtm_c[0:64, :], ptr[0:64, 0:64])
                    s.copy(kbtm_c[64:128, :], ptr[64:128, 64:128])
                SE.do(fekt)
                sv_kt = SE.tick()
                lastuseB[id(ptr)] = (SE, sv_kt)

            def emit_back(ch):
                sl = slice(ch * CH, (ch + 1) * CH)
                par = ch % 2
                Ab = AbarP[par][:]
                Ao = AorP[par][:]
                wmb_c = wmbP[par][:]
                kbtm_c = kbtmP[par][:]
                def fssc(s, ch=ch):
                    s.activation(sscb[:], Sst[:], Act.Identity,
                                 scale=ecol[:, 2 * ch + 1:2 * ch + 2])
                SE.do(fssc)
                sv_ssc = SE.tick()
                bank_waitB(TE, pb1)
                def fr(t, sl=sl):
                    t.matmul(pb1[0:64, :], ktb[0:64, sl], Sst[0:64, :],
                             start=True, stop=True, skip_group_check=True)
                    t.matmul(pb1[64:128, :], ktb[64:128, sl], Sst[64:128, :],
                             start=True, stop=True, tile_position=(64, 64),
                             skip_group_check=True)
                TE.do(fr)
                tv = TE.tick()
                SE.wait(TE, tv)
                SE.do(lambda s: s.activation(sqy[:], pb1[:], Act.Identity,
                                             scale=-1.0))
                sv = SE.tick()
                lastuseB[id(pb1)] = (SE, sv)
                VE.wait(SE, sv)
                def frr(v, ch=ch):
                    v.tensor_tensor(Rb[:], vts[:, ch * 64:(ch + 1) * 64],
                                    sqy[:], Alu.add)
                VE.do(frr)
                gv_r = VE.tick()
                TE.wait(VE, gv_r)
                bank_waitB(TE, pY)
                def fy0(t):
                    t.matmul(pY[0:64, 0:64], ids[0:64, 0:64], Rb[0:64, :],
                             start=True, stop=False, skip_group_check=True)
                    t.matmul(pY[64:128, 0:64], ids[64:128, 64:128], Rb[64:128, :],
                             start=True, stop=False, tile_position=(64, 64),
                             skip_group_check=True)
                TE.do(fy0)
                apA, apO = Ab, Ao
                spare_b = Atmpb[:]
                spare_o = Atmpo[:]
                cur_y = Rb
                for lvl in range(6):
                    def fap2(t, a=apA, y=cur_y, last=(lvl == 5)):
                        t.matmul(pY[0:64, 0:64], a[0:64, :], y[0:64, :],
                                 start=False, stop=last, skip_group_check=True)
                        t.matmul(pY[64:128, 0:64], a[64:128, :], y[64:128, :],
                                 start=False, stop=last, tile_position=(64, 64),
                                 skip_group_check=True)
                    TE.do(fap2)
                    tv = TE.tick()
                    SE.wait(TE, tv)
                    if lvl == 5:
                        def fey(s, ch=ch):
                            s.activation(Ub[:], pY[:, 0:64], Act.Identity,
                                         scale=bcol[:, ch:ch + 1])
                        SE.do(fey)
                    else:
                        SE.do(lambda s: s.activation(Yb[:], pY[:, 0:64],
                                                     Act.Identity))
                    sv_y = SE.tick()
                    TE.wait(SE, sv_y)
                    cur_y = Yb
                    if lvl == 5:
                        lastuseB[id(pY)] = (SE, sv_y)
                    if lvl < 5:
                        bank_waitB(TE, pb2)
                        def fsq(t, ab=apA, ao=apO):
                            t.matmul(pb2[0:64, :], ao[0:64, :], ab[0:64, :],
                                     start=True, stop=True, skip_group_check=True)
                            t.matmul(pb2[64:128, :], ao[64:128, :], ab[64:128, :],
                                     start=True, stop=True, tile_position=(64, 64),
                                     skip_group_check=True)
                        TE.do(fsq)
                        if lvl < 4:
                            bank_waitB(TE, pb1)
                            def fsq2(t, ab=apA, ao=apO):
                                t.matmul(pb1[0:64, :], ab[0:64, :], ao[0:64, :],
                                         start=True, stop=True,
                                         skip_group_check=True)
                                t.matmul(pb1[64:128, :], ab[64:128, :],
                                         ao[64:128, :], start=True, stop=True,
                                         tile_position=(64, 64))
                            TE.do(fsq2)
                        tv2 = TE.tick()
                        SE.wait(TE, tv2)
                        def fesq(s, nb=spare_b, no=spare_o, last=(lvl == 4)):
                            s.copy(nb, pb2[:])
                            if not last:
                                s.copy(no, pb1[:])
                        SE.do(fesq)
                        sv_sq = SE.tick()
                        lastuseB[id(pb2)] = (SE, sv_sq)
                        if lvl < 4:
                            lastuseB[id(pb1)] = (SE, sv_sq)
                        TE.wait(SE, sv_sq)
                        apA, apO = spare_b, spare_o
                        spare_b, spare_o = (Ab, Ao)
                bank_waitB(TE, pSO)
                def fo1(t, sl=sl):
                    t.matmul(pSO[0:64, 64:128], qtb[0:64, sl], Sst[0:64, :],
                             start=True, stop=False, skip_group_check=True)
                    t.matmul(pSO[64:128, 64:128], qtb[64:128, sl], Sst[64:128, :],
                             start=True, stop=False, tile_position=(64, 64),
                             skip_group_check=True)
                TE.do(fo1)
                TE.wait(SE, sv_ssc)
                def fo2(t, wmb_c=wmb_c):
                    t.matmul(pSO[0:64, 64:128], wmb_c[0:64, :], Ub[0:64, :],
                             start=False, stop=True, skip_group_check=True)
                    t.matmul(pSO[64:128, 64:128], wmb_c[64:128, :], Ub[64:128, :],
                             start=False, stop=True, tile_position=(64, 64),
                             skip_group_check=True)
                TE.do(fo2)
                TE.tick()
                def fsu(t, kbtm_c=kbtm_c):
                    t.matmul(pSO[0:64, 0:64], kbtm_c[0:64, :], Ub[0:64, :],
                             start=True, stop=False, skip_group_check=True)
                    t.matmul(pSO[64:128, 0:64], kbtm_c[64:128, :], Ub[64:128, :],
                             start=True, stop=False, tile_position=(64, 64),
                             skip_group_check=True)
                    t.matmul(pSO[0:64, 0:64], ids[0:64, 0:64], sscb[0:64, :],
                             start=False, stop=True, skip_group_check=True)
                    t.matmul(pSO[64:128, 0:64], ids[64:128, 64:128],
                             sscb[64:128, :], start=False, stop=True,
                             tile_position=(64, 64), skip_group_check=True)
                TE.do(fsu)
                tv_o2 = TE.tick()
                SE.wait(TE, tv_o2)
                def fes(s):
                    s.copy(Sst[:], pSO[:, 0:64])
                    s.activation(ytmp[:], pSO[:, 64:128], Act.Identity)
                    s.activation(sqy[:], ytmp[:], Act.Square)
                SE.do(fes)
                sv_es = SE.tick()
                lastuseB[id(pSO)] = (SE, sv_es)
                back_es[ch] = sv_es

            def emit_back2(ch):
                sv_es = back_es[ch]
                VE.wait(SE, sv_es)
                def fyn(v):
                    v.tensor_reduce(styc[:, 0:1], sqy[:], AxX, Alu.add)
                    v.drain()
                    v.tensor_scalar(styc[:, 0:1], styc[:, 0:1], 1.0 / 64, 1e-6,
                                    Alu.mult, Alu.add)
                VE.do(fyn)
                vv_yn = VE.tick()
                SE.wait(VE, vv_yn)
                SE.do(lambda s: s.activation(styc[:, 1:2], styc[:, 0:1], Act.Sqrt))
                sv_sy = SE.tick()
                VE.wait(SE, sv_sy)
                def fry(v):
                    v.reciprocal(styc[:, 2:3], styc[:, 1:2])
                VE.do(fry)
                vv_ry = VE.tick()
                SE.wait(VE, vv_ry)
                def fyscale(s):
                    s.activation(ynm[:], ytmp[:], Act.Identity, scale=styc[:, 2:3])
                SE.do(fyscale)
                sv_ynm = SE.tick()
                VE.wait(SE, sv_ynm)
                def fhw(v):
                    v.tensor_tensor(ytm[:], ynm[:], hnwt[:], Alu.mult)
                VE.do(fhw)
                gv_ytm = VE.tick()
                bank_waitB(TE, ptr)
                TE.wait(VE, gv_ytm)
                def fytr(t):
                    t.transpose(ptr[0:64, 0:64], ytm[0:64, :], ids[0:64, 0:64])
                    t.transpose(ptr[64:128, 64:128], ytm[64:128, :],
                                ids[64:128, 64:128], tile_position=(64, 64))
                TE.do(fytr)
                tv_ytr = TE.tick()
                SE.wait(TE, tv_ytr)
                def fyev(s, ch=ch):
                    s.copy(yfm[0:64, ch * 64:(ch + 1) * 64], ptr[0:64, 0:64])
                    s.copy(yfm[64:128, ch * 64:(ch + 1) * 64],
                           ptr[64:128, 64:128])
                SE.do(fyev)
                sv_yev = SE.tick()
                lastuseB[id(ptr)] = (SE, sv_yev)

            front_done = {}
            back_es = {}
            emit_front(0)
            emit_front2(0)
            for ch in range(1, CPG + 1):
                if ch < CPG:
                    emit_front(ch)
                emit_back(ch - 1)
                if ch < CPG:
                    emit_front2(ch)
                emit_back2(ch - 1)

            sv_fin = SE.n
            vv_fin = VE.n
            if fused:
                if sd_n[0] > 0:
                    SE.wait_sem(ld, sd_n[0])
                def fysel(s):
                    for j in range(4):
                        s.activation(ysc[:, j * GRAN:(j + 1) * GRAN], yfm[:],
                                     Act.Identity, scale=hselc[:, j:j + 1])
                SE.do(fysel)
                sv_ys = SE.tick()
                s_ = gb0 // TSL
                c0 = gb0 % TSL
                for j in range(4):
                    dma(yrs[s_ * 512 + j * 128:s_ * 512 + (j + 1) * 128,
                            c0:c0 + GRAN],
                        ysc[:, j * GRAN:(j + 1) * GRAN],
                        (sks, sv_ys) if j == 0 else None)
                sd_n[0] = dma_n[0]
            else:
                dma(yout[:, gb0:gb0 + GRAN], yfm[:], (sks, sv_fin))
                dma(gout[:, gb0:gb0 + GRAN], gfm[:], (vks, vv_g))

        if dbg:
            def fgfin(g_):
                g_.tensor_scalar(gb_[:], gb_[:], 0.0, None, Alu.add)
                g_.drain()
            GE.do(fgfin)
            gv_fin = GE.tick()
            dbg_list = [
                ("qraw", qraw, [128, GRAN], F32), ("kraw", kraw, [128, GRAN], F32),
                ("qn", qn, [128, GRAN], BF16), ("kn", kn, [128, GRAN], BF16),
                ("dg", dg, [128, GRAN], F32), ("abg", abg, [128, GRAN], F32),
                ("vts", vts, [128, 8 * 64], F32), ("bbt", bbt, [128, 8], F32),
                ("bdt", bdt, [128, 8], F32), ("gst", gst, [128, 8], F32),
                ("bcol", bcol, [128, 8], F32), ("nbcol", nbcol, [128, 8], F32),
                ("gaf", gaf, [128, GRAN], F32), ("argf", argf, [128, GRAN], F32),
                ("laf", laf, [128, GRAN], F32), ("actf", actf, [128, GRAN], F32),
                ("Lg", Lg, [128, GRAN], F32), ("E1g", E1g, [128, GRAN], F32),
                ("E3g", E3g, [128, GRAN], F32), ("E4g", E4g, [128, GRAN], F32),
                ("ktb", ktb, [128, GRAN], BF16), ("ktpb", ktpb, [128, GRAN], BF16),
                ("khpb", khpb, [128, GRAN], BF16), ("qtb", qtb, [128, GRAN], BF16),
                ("qtpb", qtpb, [128, GRAN], BF16), ("kbarb", kbarb, [128, GRAN], BF16),
                ("ecol", ecol, [128, 16], F32), ("Egb", Egb, [128, GRAN], F32),
                ("thg", thg, [128, GRAN], F32), ("Sst", Sst, [128, 64], BF16),
                ("styc", styc[:, 0:3], [128, 3], F32),
                ("hg", hgP[0], [128, 8 * GRAN], BF16),
                ("agb", agbP[0], [64, GRAN], BF16),
            ]
            waits = [(sks, SE.n), (vks, VE.n), (gks, gv_fin)]
            for di, (nm, tn, sh, dt) in enumerate(dbg_list):
                dap = nc.dram_tensor("dbg_" + nm, sh, dt,
                                     kind="ExternalOutput").ap()
                src = tn if isinstance(tn, bass.AP) else tn[:]
                dma(dap[:], src, waits[di] if di < len(waits) else None)
    return es


def finalize_engs(nc, E, blk):
    ld = E.ld
    dmas = E.dmas

    @blk.sync
    def _(sy):
        for wait, fn in dmas:
            if wait is not None:
                sy.wait_ge(wait[0], wait[1])
            fn(sy).then_inc(ld, 16)

    @blk.tensor
    def _(t):
        E.TE.run(t)

    @blk.scalar
    def _(s):
        E.SE.run(s)

    @blk.vector
    def _(v):
        E.VE.run(v)

    @blk.gpsimd
    def _(g_):
        E.GE.run(g_)


def build_B(Ttot=T, dbg=False):
    nc = bass.Bass("TRN2", target_bir_lowering=False, detect_race_conditions=False)
    io = dict(
        hin=nc.dram_tensor("hin", [D + RR, Ttot], BF16, kind="ExternalInput").ap(),
        wq_i=nc.dram_tensor("wq_s", [D, 128], F32, kind="ExternalInput").ap(),
        wk_i=nc.dram_tensor("wk_s", [D, 128], F32, kind="ExternalInput").ap(),
        wv_i=nc.dram_tensor("wv_s", [D, 128], F32, kind="ExternalInput").ap(),
        wsp_i=nc.dram_tensor("wsp_s", [D, 128], F32, kind="ExternalInput").ap(),
        wad_i=nc.dram_tensor("wad_s", [RR, 128], F32, kind="ExternalInput").ap(),
        bad_i=nc.dram_tensor("bad_s", [128, 1], F32, kind="ExternalInput").ap(),
        wbe_i=nc.dram_tensor("wbe_s", [D, 2], F32, kind="ExternalInput").ap(),
        bbe_i=nc.dram_tensor("bbe_s", [128, 2], F32, kind="ExternalInput").ap(),
        asp_i=nc.dram_tensor("asp_c", [128, 1], F32, kind="ExternalInput").ap(),
        bsp_i=nc.dram_tensor("bsp_c", [128, 2], F32, kind="ExternalInput").ap(),
        hnw_i=nc.dram_tensor("hnw_t", [128, 64], F32, kind="ExternalInput").ap(),
        oexp_i=nc.dram_tensor("oexp", [128, NSHIFT * 128], F32, kind="ExternalInput").ap(),
        wts_i=nc.dram_tensor("wts", [128, NPOS * SBLK], F32, kind="ExternalInput").ap(),
        invc_i=nc.dram_tensor("invc512", [128, GRAN], F32, kind="ExternalInput").ap(),
        ipow_i=nc.dram_tensor("ipow512", [128, GRAN], F32, kind="ExternalInput").ap(),
        hpow_i=nc.dram_tensor("hpow512", [128, GRAN], F32, kind="ExternalInput").ap(),
        ident_i=nc.dram_tensor("ident", [128, 128], BF16, kind="ExternalInput").ap(),
        mau_i=nc.dram_tensor("maskUS", [128, 64], U8, kind="ExternalInput").ap(),
        mui_i=nc.dram_tensor("maskUI", [128, 64], U8, kind="ExternalInput").ap(),
        yout=nc.dram_tensor("yout", [128, Ttot], BF16, kind="ExternalOutput").ap(),
        gout=nc.dram_tensor("gout", [128, Ttot], BF16, kind="ExternalOutput").ap(),
    )
    es0 = ExitStack()
    E = ECtx(nc, es0)
    es_b = emit_B(nc, E, io, Ttot=Ttot, dbg=dbg, fused=False)
    E.GE.wait_sem(E.ld, E.dma_n[0])
    blk = es0.enter_context(nc.Block())
    with es0, es_b:
        finalize_engs(nc, E, blk)
    return nc


def host_masks():
    tri = np.triu(np.ones((64, 64), np.uint8), 1)
    trii = np.triu(np.ones((64, 64), np.uint8), 0)
    mus = np.concatenate([tri, tri], axis=0)
    mui = np.concatenate([trii, trii], axis=0)
    return mus, mui


def stage_B_inputs(inputs, A_out, Ttot=T):
    maps = []
    mus, mui = host_masks()
    oexp = host_shift_exp()
    wtsh = host_win_weights()
    invc = host_invc()
    ipow = host_ipow()
    hpow = host_hpow()
    ident = np.eye(128, dtype=bfd)
    nsl = Ttot // TSL if Ttot >= TSL else 1
    for c in range(8):
        b, hp = c // 4, c % 4
        h0, h1 = 2 * hp, 2 * hp + 1
        cols = slice(h0 * 64, h0 * 64 + 128)
        if Ttot == T:
            hin = np.concatenate([A_out[b * 4 + s]["h_au"] for s in range(4)],
                                 axis=1)
        else:
            hin = A_out[b * 4]["h_au"][:, :Ttot]
        wq_s = np.ascontiguousarray(inputs["L0_attn_Wq"][:, cols], np.float32)
        wk_s = np.ascontiguousarray(inputs["L0_attn_Wk"][:, cols], np.float32)
        wv_s = np.ascontiguousarray(inputs["L0_attn_Wv"][:, cols], np.float32)
        wsp_s = np.ascontiguousarray(inputs["L0_attn_Wspike"][:, cols], np.float32)
        wad_s = np.ascontiguousarray(inputs["L0_attn_Wad"][:, cols], np.float32)
        bad_s = np.ascontiguousarray(inputs["L0_attn_bad"][cols], np.float32)[:, None]
        wbe_s = np.ascontiguousarray(inputs["L0_attn_Wbeta"][:, h0:h0 + 2], np.float32)
        bbe = np.zeros((128, 2), np.float32)
        bbe[:, 0] = inputs["L0_attn_bbeta"][h0]
        bbe[:, 1] = inputs["L0_attn_bbeta"][h1]
        asp = np.concatenate([inputs["L0_attn_alpha_spike"][0, h0],
                              inputs["L0_attn_alpha_spike"][0, h1]]).astype(np.float32)[:, None]
        bsp = np.zeros((128, 2), np.float32)
        bsp[0:64, 0] = inputs["L0_attn_beta_spike"][0, h0]
        bsp[64:128, 0] = inputs["L0_attn_beta_spike"][0, h1]
        bsp[:, 1] = 1.0
        hnw = np.zeros((128, 64), np.float32)
        hnw[0:64] = inputs["L0_attn_head_norm_w"][h0][None, :]
        hnw[64:128] = inputs["L0_attn_head_norm_w"][h1][None, :]
        maps.append(dict(hin=hin, wq_s=wq_s, wk_s=wk_s, wv_s=wv_s, wsp_s=wsp_s,
                         wad_s=wad_s, bad_s=bad_s, wbe_s=wbe_s, bbe_s=bbe,
                         asp_c=asp, bsp_c=bsp, hnw_t=hnw, oexp=oexp, wts=wtsh,
                         invc512=invc, ipow512=ipow, hpow512=hpow, ident=ident,
                         maskUS=mus, maskUI=mui))
    return maps



def emit_C(nc, E, io, fused=False):
    xn_i, l2_i, y_i, xs_i = io["xn_i"], io["l2_i"], io["y_i"], io["xs_i"]
    wu1_i, bu1_i, wu2_i, bu2_i = io["wu1_i"], io["bu1_i"], io["wu2_i"], io["bu2_i"]
    wo_i, fnw_i = io["wo_i"], io["fnw_i"]
    wf1_i, wf3_i, wf2_i = io["wf1_i"], io["wf3_i"], io["wf2_i"]
    q1_i, s1_i, b1_i = io["q1_i"], io["s1_i"], io["b1_i"]
    q2_i, s2_i, b2c_i = io["q2_i"], io["s2_i"], io["b2c_i"]
    idf_i, out_o = io["idf_i"], io["out_o"]

    SLAB = 512
    NSLAB = DFF // SLAB
    MTS = SLAB // 128
    TH2 = TSL // 2
    NB2 = TH2 // 512
    es = ExitStack()
    def _sb(n, sh, dt):
        return es.enter_context(nc.sbuf_tensor(n, sh, dt))
    def _ps(n, sh, dt):
        return es.enter_context(nc.psum_tensor(n, sh, dt))
    def _sem(n):
        return es.enter_context(nc.semaphore(n))
    xnb = _sb("xnb", [128, 8 * TH2], BF16)
    l2b = _sb("l2b", [128, 8 * TH2], BF16)
    zb = _sb("zb", [128, 8 * TH2], BF16)
    accb = _sb("accb", [128, 8 * TH2], F32)
    g1b = _sb("g1b", [64, TH2], BF16)
    gatet = _sb("gatet", [128, 512], BF16)
    wu1s = _sb("wu1s", [128, 8 * RR], BF16)
    wu2s = _sb("wu2s", [64, D], BF16)
    wos = _sb("wos", [128, 4 * D], BF16)
    bu1c = _sb("bu1c", [64, 1], F32)
    bu2c = _sb("bu2c", [128, 8], F32)
    fnwc = _sb("fnwc", [128, 8], F32)
    b1c = _sb("b1c", [128, 32], F32)
    b2cc = _sb("b2cc", [128, 8], F32)
    idf = _sb("idf_s", [128, 128], F32)
    wslabA = _sb("wslabA", [128, 8 * SLAB], BF16)
    wslabB = _sb("wslabB", [128, 8 * SLAB], BF16)
    w2slab = _sb("w2slab", [128, MTS * D], BF16)
    qslab = _sb("qslab", [128, 8 * SLAB], BF16)
    sslab = _sb("sslab", [128, 8 * SLAB], BF16)
    uslab = _sb("uslab", [128, MTS * TH2], BF16)
    xtile = _sb("xtile", [128, D], F32)
    sq_ = _sb("sqc", [128, 1536], F32)
    onesf = _sb("onesfC", [128, 128], F32)
    pm = _ps("pm", [128, 512], F32)
    pmB = _ps("pmB", [128, 512], F32)
    pm2 = _ps("pm2", [128, 512], F32)
    pm2B = _ps("pm2B", [128, 512], F32)
    pbc = _ps("pbcC", [128, 512], F32)
    ptrp = _ps("ptrp", [128, 128], F32)
    ptrpB = _ps("ptrpB", [128, 128], F32)
    gatetB = _sb("gatetB", [128, 512], BF16)

    if True:
        TE, SE, VE, GE = E.TE, E.SE, E.VE, E.GE
        ld, ldg = E.ld, E.ldg
        tks, sks, vks, gks = E.tks, E.sks, E.vks, E.gks
        dma, dma_n, dmas = E.dma, E.dma_n, E.dmas
        gdma_n = E.gdma_n

        def gdma(dst, src, wait=None):
            if wait is not None:
                GE.wait_sem(wait[0], wait[1])
            GE.do(lambda g_, d=dst, s=src: g_.dma_start(out=d, in_=s).then_inc(ldg, 16))
            gdma_n[0] += 16
            return gdma_n[0]

        cl = [(bu1c[:], bu1_i[:]), (bu2c[:], bu2_i[:]), (fnwc[:], fnw_i[:]),
              (b1c[:], b1_i[:]), (b2cc[:], b2c_i[:]), (idf[:], idf_i[:])]
        for ci, (dst, src) in enumerate(cl):
            dma(dst, src, io.get("const_wait") if ci == 0 else None)
        if fused:
            cw_ = io["const_wait"]
            VE.wait_sem(cw_[0], cw_[1])
            SE.wait_sem(cw_[0], cw_[1])
            TE.wait_sem(cw_[0], cw_[1])
        VE.do(lambda v: v.memset(onesf[:], 1.0))
        vv_ones = VE.tick()
        TE.wait(VE, vv_ones)
        for kc in range(8):
            gdma(wu1s[:, kc * RR:(kc + 1) * RR], wu1_i[kc * 128:(kc + 1) * 128, :])
        gdma(wu2s[:], wu2_i[:])
        for kc in range(4):
            gdma(wos[:, kc * D:(kc + 1) * D], wo_i[kc * 128:(kc + 1) * 128, :])
        SE.wait_sem(ld, dma_n[0])
        SE.do(lambda s: s.activation(b2cc[:], b2cc[:], Act.Identity, scale=0.5))
        sv_b2h = SE.tick()

        lastuse = {}

        def bank_wait(eng, bank):
            lu = lastuse.get(id(bank))
            if lu is not None:
                eng.wait(lu[0], lu[1])

        for half in range(2):
            hb = half * TH2
            wv_h = TE.n
            vv_h = VE.n
            for kc in range(8):
                dma(xnb[:, kc * TH2:(kc + 1) * TH2],
                    xn_i[kc * 128:(kc + 1) * 128, hb:hb + TH2], (tks, wv_h))
                dma(l2b[:, kc * TH2:(kc + 1) * TH2],
                    l2_i[kc * 128:(kc + 1) * 128, hb:hb + TH2], (tks, wv_h))
            for kc in range(4):
                dma(zb[:, kc * TH2:(kc + 1) * TH2],
                    y_i[kc * 128:(kc + 1) * 128, hb:hb + TH2], (tks, wv_h))
            TE.wait_sem(ld, dma_n[0])
            TE.wait_sem(ldg, gdma_n[0])
            TE.wait(VE, vv_h)

            for nb in range(NB2):
                def fg1(t, nb=nb):
                    for kc in range(8):
                        t.matmul(pm[0:64, 0:512], wu1s[:, kc * RR:(kc + 1) * RR],
                                 xnb[:, kc * TH2 + nb * 512:kc * TH2 + (nb + 1) * 512],
                                 start=(kc == 0), stop=(kc == 7), skip_group_check=True)
                TE.do(fg1)
                tv = TE.tick()
                SE.wait(TE, tv)
                def fs1(s, nb=nb):
                    s.activation(g1b[:, nb * 512:(nb + 1) * 512], pm[0:64, 0:512],
                                 Act.Silu, bias=bu1c[:], scale=1.0)
                SE.do(fs1)
                sv = SE.tick()
                TE.wait(SE, sv)

            it_g = 0
            for mc in range(8):
                for nb in range(NB2):
                    bnk = it_g % 2
                    pg = pm2 if bnk == 0 else pm2B
                    pw = pm if bnk == 0 else pmB
                    gt = gatet if bnk == 0 else gatetB
                    bank_wait(TE, pg)
                    def fg2(t, mc=mc, nb=nb, pg=pg):
                        t.matmul(pg[:, 0:512], wu2s[:, mc * 128:(mc + 1) * 128],
                                 g1b[:, nb * 512:(nb + 1) * 512],
                                 start=True, stop=True, skip_group_check=True)
                    TE.do(fg2)
                    tv = TE.tick()
                    SE.wait(TE, tv)
                    bank_wait(SE, gt)
                    def fs2(s, mc=mc, pg=pg, gt=gt):
                        s.activation(gt[:], pg[:, 0:512], Act.Sigmoid,
                                     bias=bu2c[:, mc:mc + 1], scale=1.0)
                    SE.do(fs2)
                    sv = SE.tick()
                    lastuse[id(pg)] = (SE, sv)
                    bank_wait(TE, pw)
                    def fwo(t, mc=mc, nb=nb, pw=pw):
                        for kc in range(4):
                            t.matmul(pw[:, 0:512],
                                     wos[:, kc * D + mc * 128:kc * D + (mc + 1) * 128],
                                     zb[:, kc * TH2 + nb * 512:kc * TH2 + (nb + 1) * 512],
                                     start=(kc == 0), stop=(kc == 3),
                                     skip_group_check=True)
                    TE.do(fwo)
                    tv = TE.tick()
                    VE.wait(TE, tv)
                    VE.wait(SE, sv)
                    def fya(v, mc=mc, nb=nb, pw=pw, gt=gt):
                        sl = slice(mc * TH2 + nb * 512, mc * TH2 + (nb + 1) * 512)
                        v.tensor_tensor(accb[:, sl], pw[:, 0:512], gt[:], Alu.mult)
                        v.drain()
                        v.scalar_tensor_tensor(accb[:, sl], xnb[:, sl], 2.0,
                                               accb[:, sl], Alu.mult, Alu.add)
                    VE.do(fya)
                    vv = VE.tick()
                    lastuse[id(pw)] = (VE, vv)
                    lastuse[id(gt)] = (VE, vv)
                    it_g += 1

            for nb in range(NB2):
                VE.wait(SE, SE.n)
                for kc in range(8):
                    def fyasq(v, kc=kc, nb=nb):
                        sl = slice(kc * TH2 + nb * 512, kc * TH2 + (nb + 1) * 512)
                        v.tensor_tensor(sq_[:, 0:512], accb[:, sl], xnb[:, sl],
                                        Alu.subtract)
                    VE.do(fyasq)
                    vv = VE.tick()
                    SE.wait(VE, vv)
                    SE.do(lambda s: s.activation(sq_[:, 512:1024], sq_[:, 0:512],
                                                 Act.Square))
                    sv = SE.tick()
                    TE.wait(SE, sv)
                    def fsum(t, kc=kc):
                        t.matmul(pbc[0:1, 0:512], onesf[:, 0:1], sq_[:, 512:1024],
                                 start=(kc == 0), stop=(kc == 7),
                                 skip_group_check=True)
                    TE.do(fsum)
                    tv = TE.tick()
                    VE.wait(TE, tv)
                SE.wait(TE, TE.n)
                SE.do(lambda s: s.copy(sq_[0:1, 512:1024], pbc[0:1, 0:512]))
                sv_cp = SE.tick()
                VE.wait(SE, sv_cp)
                VE.do(lambda v: v.tensor_scalar(sq_[0:1, 512:1024],
                                                sq_[0:1, 512:1024], 1.0 / D, 1e-6,
                                                Alu.mult, Alu.add))
                vv_eps = VE.tick()
                SE.wait(VE, vv_eps)
                SE.do(lambda s: s.activation(sq_[0:1, 0:512], sq_[0:1, 512:1024],
                                             Act.Sqrt))
                sv_sq = SE.tick()
                TE.wait(SE, sv_sq)
                def fbc(t):
                    t.matmul(pbc[:, 0:512], onesf[0:1, 0:128], sq_[0:1, 0:512],
                             start=True, stop=True, skip_group_check=True)
                TE.do(fbc)
                tv_bc = TE.tick()
                SE.wait(TE, tv_bc)
                SE.do(lambda s: s.copy(sq_[:, 1024:1536], pbc[:, 0:512]))
                gv_z0 = SE.tick()
                VE.wait(SE, gv_z0)
                VE.do(lambda v: v.reciprocal(sq_[:, 1024:1536], sq_[:, 1024:1536]))
                sv_rz = VE.tick()
                GE.wait(VE, sv_rz)
                TE.wait(VE, sv_rz)
                for kc in range(8):
                    GE.wait(SE, SE.n)
                    def fz(g_, kc=kc, nb=nb):
                        sl = slice(kc * TH2 + nb * 512, kc * TH2 + (nb + 1) * 512)
                        g_.tensor_tensor(sq_[:, 0:512], accb[:, sl], xnb[:, sl],
                                         Alu.subtract)
                        g_.drain()
                        g_.tensor_tensor(sq_[:, 0:512], sq_[:, 0:512],
                                         sq_[:, 1024:1536], Alu.mult)
                        g_.drain()
                    GE.do(fz)
                    gv = GE.tick()
                    SE.wait(GE, gv)
                    def fzw(s, kc=kc, nb=nb):
                        sl = slice(kc * TH2 + nb * 512, kc * TH2 + (nb + 1) * 512)
                        s.activation(zb[:, sl], sq_[:, 0:512], Act.Identity,
                                     scale=fnwc[:, kc:kc + 1])
                    SE.do(fzw)
                    sv = SE.tick()
                    GE.wait(SE, sv)

            TE.wait(SE, SE.n)
            for sb_ in range(NSLAB):
                wv_ = TE.n
                for kc in range(8):
                    gdma(wslabA[:, kc * SLAB:(kc + 1) * SLAB],
                         wf1_i[kc * 128:(kc + 1) * 128, sb_ * SLAB:(sb_ + 1) * SLAB],
                         (tks, wv_))
                    gdma(wslabB[:, kc * SLAB:(kc + 1) * SLAB],
                         wf3_i[kc * 128:(kc + 1) * 128, sb_ * SLAB:(sb_ + 1) * SLAB],
                         (tks, wv_))
                for kc in range(MTS):
                    gdma(w2slab[:, kc * D:(kc + 1) * D],
                         wf2_i[sb_ * SLAB + kc * 128:sb_ * SLAB + (kc + 1) * 128, :],
                         (tks, wv_))
                TE.wait_sem(ldg, gdma_n[0])
                it_f = 0
                for mt in range(MTS):
                    for nb in range(NB2):
                        p1 = pm if it_f % 2 == 0 else pmB
                        p3 = pm2 if it_f % 2 == 0 else pm2B
                        bank_wait(TE, p1)
                        def ff1(t, mt=mt, nb=nb, p1=p1):
                            for kc in range(8):
                                t.matmul(p1[:, 0:512],
                                         wslabA[:, kc * SLAB + mt * 128:kc * SLAB + (mt + 1) * 128],
                                         zb[:, kc * TH2 + nb * 512:kc * TH2 + (nb + 1) * 512],
                                         start=(kc == 0), stop=(kc == 7),
                                         skip_group_check=True)
                        TE.do(ff1)
                        tv = TE.tick()
                        SE.wait(TE, tv)
                        def fsl(s, mt=mt, nb=nb, p1=p1):
                            s.activation(uslab[:, mt * TH2 + nb * 512:mt * TH2 + (nb + 1) * 512],
                                         p1[:, 0:512], Act.Silu)
                        SE.do(fsl)
                        sv = SE.tick()
                        lastuse[id(p1)] = (SE, sv)
                        bank_wait(TE, p3)
                        def ff3(t, mt=mt, nb=nb, p3=p3):
                            for kc in range(8):
                                t.matmul(p3[:, 0:512],
                                         wslabB[:, kc * SLAB + mt * 128:kc * SLAB + (mt + 1) * 128],
                                         zb[:, kc * TH2 + nb * 512:kc * TH2 + (nb + 1) * 512],
                                         start=(kc == 0), stop=(kc == 7),
                                         skip_group_check=True)
                        TE.do(ff3)
                        tv3 = TE.tick()
                        VE.wait(TE, tv3)
                        VE.wait(SE, sv)
                        def fmu(v, mt=mt, nb=nb, p3=p3):
                            sl = slice(mt * TH2 + nb * 512, mt * TH2 + (nb + 1) * 512)
                            v.tensor_tensor(uslab[:, sl], p3[:, 0:512], uslab[:, sl],
                                            Alu.mult)
                        VE.do(fmu)
                        vv = VE.tick()
                        lastuse[id(p3)] = (VE, vv)
                        it_f += 1
                TE.wait(VE, VE.n)
                it_f = 0
                for mc in range(8):
                    for nb in range(NB2):
                        p2 = pm if it_f % 2 == 0 else pmB
                        bank_wait(TE, p2)
                        def ff2(t, mc=mc, nb=nb, p2=p2):
                            for kc in range(MTS):
                                t.matmul(p2[:, 0:512],
                                         w2slab[:, kc * D + mc * 128:kc * D + (mc + 1) * 128],
                                         uslab[:, kc * TH2 + nb * 512:kc * TH2 + (nb + 1) * 512],
                                         start=(kc == 0), stop=(kc == MTS - 1),
                                         skip_group_check=True)
                        TE.do(ff2)
                        tv = TE.tick()
                        VE.wait(TE, tv)
                        def facc(v, mc=mc, nb=nb, p2=p2):
                            sl = slice(mc * TH2 + nb * 512, mc * TH2 + (nb + 1) * 512)
                            v.tensor_tensor(accb[:, sl], accb[:, sl], p2[:, 0:512],
                                            Alu.add)
                        VE.do(facc)
                        vv = VE.tick()
                        lastuse[id(p2)] = (VE, vv)
                        it_f += 1

            for sb_ in range(NSLAB):
                wv_ = TE.n
                for kc in range(8):
                    gdma(qslab[:, kc * SLAB:(kc + 1) * SLAB],
                         q1_i[kc * 128:(kc + 1) * 128, sb_ * SLAB:(sb_ + 1) * SLAB],
                         (tks, wv_))
                    gdma(sslab[:, kc * SLAB:(kc + 1) * SLAB],
                         s1_i[kc * 128:(kc + 1) * 128, sb_ * SLAB:(sb_ + 1) * SLAB],
                         (tks, wv_))
                VE.wait_sem(ldg, gdma_n[0])
                VE.wait(TE, wv_)
                def fdq(v):
                    v.tensor_scalar(wslabA[:], qslab[:], -128.0, None, Alu.add)
                    v.drain()
                    v.tensor_tensor(wslabA[:], wslabA[:], sslab[:], Alu.mult)
                VE.do(fdq)
                vv_dq = VE.tick()
                TE.wait(VE, vv_dq)
                for kc in range(MTS):
                    gdma(qslab[:, kc * D:(kc + 1) * D],
                         q2_i[sb_ * SLAB + kc * 128:sb_ * SLAB + (kc + 1) * 128, :],
                         (vks, vv_dq))
                    gdma(sslab[:, kc * D:(kc + 1) * D],
                         s2_i[(sb_ * MTS + kc) * 128:(sb_ * MTS + kc + 1) * 128, :],
                         (vks, vv_dq))
                VE.wait_sem(ldg, gdma_n[0])
                def fdq2(v):
                    nw2 = MTS * D
                    v.tensor_scalar(w2slab[:, 0:nw2], qslab[:, 0:nw2], -128.0,
                                    None, Alu.add)
                    v.drain()
                    v.tensor_tensor(w2slab[:, 0:nw2], w2slab[:, 0:nw2],
                                    sslab[:, 0:nw2], Alu.mult)
                VE.do(fdq2)
                vv_dq2 = VE.tick()
                TE.wait(VE, vv_dq2)
                it_f = 0
                for mt in range(MTS):
                    for nb in range(NB2):
                        p1 = pm if it_f % 2 == 0 else pmB
                        bank_wait(TE, p1)
                        def fq1(t, mt=mt, nb=nb, p1=p1):
                            for kc in range(8):
                                t.matmul(p1[:, 0:512],
                                         wslabA[:, kc * SLAB + mt * 128:kc * SLAB + (mt + 1) * 128],
                                         l2b[:, kc * TH2 + nb * 512:kc * TH2 + (nb + 1) * 512],
                                         start=(kc == 0), stop=(kc == 7),
                                         skip_group_check=True)
                        TE.do(fq1)
                        tv = TE.tick()
                        SE.wait(TE, tv)
                        def fsq1(s, mt=mt, nb=nb, sb_=sb_, p1=p1):
                            col = (sb_ * SLAB + mt * 128) // 128
                            s.activation(uslab[:, mt * TH2 + nb * 512:mt * TH2 + (nb + 1) * 512],
                                         p1[:, 0:512], Act.Silu,
                                         bias=b1c[:, col:col + 1], scale=1.0)
                        SE.do(fsq1)
                        sv = SE.tick()
                        lastuse[id(p1)] = (SE, sv)
                        it_f += 1
                TE.wait(SE, SE.n)
                it_f = 0
                for mc in range(8):
                    for nb in range(NB2):
                        p2 = pm if it_f % 2 == 0 else pmB
                        bank_wait(TE, p2)
                        def fq2(t, mc=mc, nb=nb, p2=p2):
                            for kc in range(MTS):
                                t.matmul(p2[:, 0:512],
                                         w2slab[:, kc * D + mc * 128:kc * D + (mc + 1) * 128],
                                         uslab[:, kc * TH2 + nb * 512:kc * TH2 + (nb + 1) * 512],
                                         start=(kc == 0), stop=(kc == MTS - 1),
                                         skip_group_check=True)
                        TE.do(fq2)
                        tv = TE.tick()
                        VE.wait(TE, tv)
                        def facc2(v, mc=mc, nb=nb, p2=p2):
                            sl = slice(mc * TH2 + nb * 512, mc * TH2 + (nb + 1) * 512)
                            v.scalar_tensor_tensor(accb[:, sl], p2[:, 0:512], 0.5,
                                                   accb[:, sl], Alu.mult, Alu.add)
                        VE.do(facc2)
                        vv = VE.tick()
                        lastuse[id(p2)] = (VE, vv)
                        it_f += 1

            VE.wait(SE, sv_b2h)
            for mc in range(8):
                def fbias(v, mc=mc):
                    v.tensor_scalar(accb[:, mc * TH2:(mc + 1) * TH2],
                                    accb[:, mc * TH2:(mc + 1) * TH2],
                                    b2cc[:, mc:mc + 1], None, Alu.add)
                VE.do(fbias)
            vv_b = VE.tick()
            TE.wait(VE, vv_b)

            prev_vv = VE.n
            prev_store = dma_n[0]
            for it in range(TH2 // 128):
                dma(xtile[:], xs_i[hb + it * 128:hb + (it + 1) * 128, :],
                    (vks, prev_vv))
                ldx = dma_n[0]
                VE.wait_sem(ld, ldx)
                for kc in range(8):
                    pt = ptrp if kc % 2 == 0 else ptrpB
                    bank_wait(TE, pt)
                    def ftr(t, it=it, kc=kc, pt=pt):
                        t.transpose(pt[:, 0:128],
                                    accb[:, kc * TH2 + it * 128:kc * TH2 + (it + 1) * 128],
                                    idf[:])
                    TE.do(ftr)
                    tv = TE.tick()
                    VE.wait(TE, tv)
                    def fadd(v, kc=kc, pt=pt):
                        v.tensor_tensor(sq_[:, kc * 128:(kc + 1) * 128], pt[:, 0:128],
                                        xtile[:, kc * 128:(kc + 1) * 128], Alu.add)
                    VE.do(fadd)
                    vv = VE.tick()
                    lastuse[id(pt)] = (VE, vv)
                prev_vv = VE.n
                dma(out_o[hb + it * 128:hb + (it + 1) * 128, :], sq_[:, 0:1024],
                    (vks, prev_vv))
                prev_store = dma_n[0]
                VE.wait_sem(ld, prev_store - 16)
    return es


def build_C():
    nc = bass.Bass("TRN2", target_bir_lowering=False, detect_race_conditions=False)
    io = dict(
        xn_i=nc.dram_tensor("xn_fm", [D, TSL], BF16, kind="ExternalInput").ap(),
        l2_i=nc.dram_tensor("ln2h_fm", [D, TSL], BF16, kind="ExternalInput").ap(),
        y_i=nc.dram_tensor("yin", [H * DV, TSL], BF16, kind="ExternalInput").ap(),
        xs_i=nc.dram_tensor("xs", [TSL, D], F32, kind="ExternalInput").ap(),
        wu1_i=nc.dram_tensor("wu1", [D, RR], F32, kind="ExternalInput").ap(),
        bu1_i=nc.dram_tensor("bu1_c", [64, 1], F32, kind="ExternalInput").ap(),
        wu2_i=nc.dram_tensor("wu2", [RR, D], F32, kind="ExternalInput").ap(),
        bu2_i=nc.dram_tensor("bu2_c", [128, 8], F32, kind="ExternalInput").ap(),
        wo_i=nc.dram_tensor("wo", [H * DV, D], F32, kind="ExternalInput").ap(),
        fnw_i=nc.dram_tensor("fnw_c", [128, 8], F32, kind="ExternalInput").ap(),
        wf1_i=nc.dram_tensor("wff1", [D, DFF], F32, kind="ExternalInput").ap(),
        wf3_i=nc.dram_tensor("wff3", [D, DFF], F32, kind="ExternalInput").ap(),
        wf2_i=nc.dram_tensor("wff2", [DFF, D], F32, kind="ExternalInput").ap(),
        q1_i=nc.dram_tensor("fc1q", [D, DFF], I32, kind="ExternalInput").ap(),
        s1_i=nc.dram_tensor("fc1s_bc", [8 * 128, DFF], F32, kind="ExternalInput").ap(),
        b1_i=nc.dram_tensor("fc1b_c", [128, 32], F32, kind="ExternalInput").ap(),
        q2_i=nc.dram_tensor("fc2q", [DFF, D], I32, kind="ExternalInput").ap(),
        s2_i=nc.dram_tensor("fc2s_bc", [32 * 128, D], F32, kind="ExternalInput").ap(),
        b2c_i=nc.dram_tensor("fc2b_c", [128, 8], F32, kind="ExternalInput").ap(),
        idf_i=nc.dram_tensor("identf", [128, 128], F32, kind="ExternalInput").ap(),
        out_o=nc.dram_tensor("outp", [TSL, D], F32, kind="ExternalOutput").ap(),
    )
    es0 = ExitStack()
    E = ECtx(nc, es0)
    es_c = emit_C(nc, E, io, fused=False)
    E.GE.wait_sem(E.ld, E.dma_n[0])
    blk = es0.enter_context(nc.Block())
    with es0, es_c:
        finalize_engs(nc, E, blk)
    return nc


def stage_C_inputs(inputs, A_out, B_out):
    maps = []
    identf = np.eye(128, dtype=np.float32)
    x = np.ascontiguousarray(inputs["L0_attn_x"])
    s1bc = np.repeat(np.ascontiguousarray(inputs["L0_ffn_fc1_scales"], np.float32),
                     128, axis=0)
    s2bc = np.repeat(np.ascontiguousarray(inputs["L0_ffn_fc2_scales"], np.float32),
                     128, axis=0)
    b1c = host_colvec(inputs["L0_ffn_fc1_bias"])
    b2c = host_colvec(inputs["L0_ffn_fc2_bias"])
    fnw = host_colvec(inputs["L0_attn_ff_norm_w"])
    bu2 = host_colvec(inputs["L0_attn_bu2"])
    for c in range(8):
        b, sl_i = c // 4, c % 4
        sl = slice(sl_i * TSL, (sl_i + 1) * TSL)
        yin = np.concatenate([B_out[b * 4 + hp]["yout"][:, sl] for hp in range(4)],
                             axis=0)
        maps.append(dict(
            xn_fm=A_out[c]["xn_fm"], ln2h_fm=A_out[c]["ln2h_fm"], yin=yin,
            xs=np.ascontiguousarray(x[b, sl], np.float32),
            wu1=np.ascontiguousarray(inputs["L0_attn_Wu1"], np.float32),
            bu1_c=np.ascontiguousarray(inputs["L0_attn_bu1"], np.float32)[:, None],
            wu2=np.ascontiguousarray(inputs["L0_attn_Wu2"], np.float32),
            bu2_c=bu2, wo=np.ascontiguousarray(inputs["L0_attn_Wo"], np.float32),
            fnw_c=fnw,
            wff1=np.ascontiguousarray(inputs["L0_attn_Wff1"], np.float32),
            wff3=np.ascontiguousarray(inputs["L0_attn_Wff3"], np.float32),
            wff2=np.ascontiguousarray(inputs["L0_attn_Wff2"], np.float32),
            fc1q=np.ascontiguousarray(inputs["L0_ffn_fc1_q"], np.int32),
            fc1s_bc=s1bc, fc1b_c=b1c,
            fc2q=np.ascontiguousarray(inputs["L0_ffn_fc2_q"], np.int32),
            fc2s_bc=s2bc, fc2b_c=b2c, identf=identf))
    return maps



RG4 = [[0, 1, 2, 3], [4, 5, 6, 7]]


def build_F():
    nc = bass.Bass("TRN2", target_bir_lowering=False, detect_race_conditions=False)

    def ein(n, sh, dt):
        return nc.dram_tensor(n, sh, dt, kind="ExternalInput").ap()

    def loc(n, sh, dt):
        return nc.dram_tensor(n, sh, dt, kind="Internal", addr_space="Local").ap()

    xs_i = ein("xs", [TSL, D], F32)
    ident_i = ein("ident", [128, 128], BF16)
    ioA = dict(
        xs=xs_i, xh=ein("xh", [128, D], F32), nw_c=ein("nw_c", [128, 8], F32),
        cb_c=ein("cb_c", [128, 8], F32), cw_c=ein("cw_c", [128, 32], F32),
        g2_c=ein("g2_c", [128, 8], F32), b2_c=ein("b2_c", [128, 8], F32),
        wau_i=ein("wau", [D, RR], F32), bau_c=ein("bau_c", [64, 1], F32),
        ident=ident_i,
        h_au=loc("hau_loc", [D + RR, TSL], BF16),
        xn_fm_o=loc("xnf_loc", [D, TSL], BF16),
        ln2_fm_o=loc("l2f_loc", [D, TSL], BF16),
    )
    hall = loc("hall", [4 * (D + RR), TSL], BF16)
    yrs = loc("yrs", [4 * 512, TSL], BF16)
    ysl = loc("ysl", [512, TSL], BF16)

    es0 = ExitStack()
    E = ECtx(nc, es0)
    cgs = es0.enter_context(nc.semaphore("cgs"))

    fnsA, esA, (sdA, nstA) = make_A(nc, ioA)
    esA.close()

    E.GE.wait_sem(sdA, nstA)
    E.GE.do(lambda g: g.collective_compute(
        "AllGather", Alu.bypass, ins=[ioA["h_au"]], outs=[hall],
        replica_groups=RG4).then_inc(cgs, 1))

    ioB = dict(
        wq_i=ein("wq_s", [D, 128], F32), wk_i=ein("wk_s", [D, 128], F32),
        wv_i=ein("wv_s", [D, 128], F32), wsp_i=ein("wsp_s", [D, 128], F32),
        wad_i=ein("wad_s", [RR, 128], F32), bad_i=ein("bad_s", [128, 1], F32),
        wbe_i=ein("wbe_s", [D, 2], F32), bbe_i=ein("bbe_s", [128, 2], F32),
        asp_i=ein("asp_c", [128, 1], F32), bsp_i=ein("bsp_c", [128, 2], F32),
        hnw_i=ein("hnw_t", [128, 64], F32),
        oexp_i=ein("oexp", [128, NSHIFT * 128], F32),
        wts_i=ein("wts", [128, NPOS * SBLK], F32),
        invc_i=ein("invc512", [128, GRAN], F32),
        ipow_i=ein("ipow512", [128, GRAN], F32),
        hpow_i=ein("hpow512", [128, GRAN], F32),
        ident_i=ident_i, mau_i=ein("maskUS", [128, 64], U8),
        mui_i=ein("maskUI", [128, 64], U8),
        hall=hall, yrs=yrs, hsel_i=ein("hsel", [128, 4], F32),
        const_wait=(sdA, nstA), g0_wait=(cgs, 1),
    )
    esB = emit_B(nc, E, ioB, Ttot=T, fused=True)
    esB.close()

    E.GE.wait_sem(E.ld, E.dma_n[0])
    E.GE.do(lambda g: g.collective_compute(
        "ReduceScatter", Alu.add, ins=[yrs], outs=[ysl],
        replica_groups=RG4).then_inc(cgs, 1))

    ioC = dict(
        xn_i=ioA["xn_fm_o"], l2_i=ioA["ln2_fm_o"], y_i=ysl, xs_i=xs_i,
        wu1_i=ein("wu1", [D, RR], F32), bu1_i=ein("bu1_c", [64, 1], F32),
        wu2_i=ein("wu2", [RR, D], F32), bu2_i=ein("bu2_c", [128, 8], F32),
        wo_i=ein("wo", [H * DV, D], F32), fnw_i=ein("fnw_c", [128, 8], F32),
        wf1_i=ein("wff1", [D, DFF], F32), wf3_i=ein("wff3", [D, DFF], F32),
        wf2_i=ein("wff2", [DFF, D], F32),
        q1_i=ein("fc1q", [D, DFF], I32),
        s1_i=ein("fc1s_bc", [8 * 128, DFF], F32),
        b1_i=ein("fc1b_c", [128, 32], F32),
        q2_i=ein("fc2q", [DFF, D], I32),
        s2_i=ein("fc2s_bc", [32 * 128, D], F32),
        b2c_i=ein("fc2b_c", [128, 8], F32),
        idf_i=ein("identf", [128, 128], F32),
        out_o=nc.dram_tensor("outp", [TSL, D], F32, kind="ExternalOutput").ap(),
        const_wait=(cgs, 2),
    )
    esC = emit_C(nc, E, ioC, fused=True)
    E.GE.wait_sem(E.ld, E.dma_n[0])

    blk = es0.enter_context(nc.Block())
    with es0, esC:
        @blk.sync
        def _(sy):
            fnsA["sync"](sy)
            for wait, fn in E.dmas:
                if wait is not None:
                    sy.wait_ge(wait[0], wait[1])
                fn(sy).then_inc(E.ld, 16)

        @blk.tensor
        def _(t):
            fnsA["tensor"](t)
            E.TE.run(t)

        @blk.scalar
        def _(s):
            fnsA["scalar"](s)
            E.SE.run(s)

        @blk.vector
        def _(v):
            fnsA["vector"](v)
            E.VE.run(v)

        @blk.gpsimd
        def _(g_):
            fnsA["gpsimd"](g_)
            E.GE.run(g_)
    return nc


def stage_F_inputs(inputs):
    mapsA = stage_A_inputs(inputs)
    mus, mui = host_masks()
    oexp = host_shift_exp()
    wtsh = host_win_weights()
    invc = host_invc()
    ipow = host_ipow()
    hpow = host_hpow()
    identf = np.eye(128, dtype=np.float32)
    x = np.ascontiguousarray(inputs["L0_attn_x"])
    s1bc = np.repeat(np.ascontiguousarray(inputs["L0_ffn_fc1_scales"], np.float32),
                     128, axis=0)
    s2bc = np.repeat(np.ascontiguousarray(inputs["L0_ffn_fc2_scales"], np.float32),
                     128, axis=0)
    b1c = host_colvec(inputs["L0_ffn_fc1_bias"])
    b2c = host_colvec(inputs["L0_ffn_fc2_bias"])
    fnw = host_colvec(inputs["L0_attn_ff_norm_w"])
    bu2 = host_colvec(inputs["L0_attn_bu2"])
    maps = []
    for c in range(8):
        b, hp = c // 4, c % 4
        h0, h1 = 2 * hp, 2 * hp + 1
        cols = slice(h0 * 64, h0 * 64 + 128)
        m = dict(mapsA[c])
        wbe_s = np.ascontiguousarray(inputs["L0_attn_Wbeta"][:, h0:h0 + 2],
                                     np.float32)
        bbe = np.zeros((128, 2), np.float32)
        bbe[:, 0] = inputs["L0_attn_bbeta"][h0]
        bbe[:, 1] = inputs["L0_attn_bbeta"][h1]
        asp = np.concatenate([inputs["L0_attn_alpha_spike"][0, h0],
                              inputs["L0_attn_alpha_spike"][0, h1]]
                             ).astype(np.float32)[:, None]
        bsp = np.zeros((128, 2), np.float32)
        bsp[0:64, 0] = inputs["L0_attn_beta_spike"][0, h0]
        bsp[64:128, 0] = inputs["L0_attn_beta_spike"][0, h1]
        bsp[:, 1] = 1.0
        hnw = np.zeros((128, 64), np.float32)
        hnw[0:64] = inputs["L0_attn_head_norm_w"][h0][None, :]
        hnw[64:128] = inputs["L0_attn_head_norm_w"][h1][None, :]
        hsel = np.zeros((128, 4), np.float32)
        hsel[:, hp] = 1.0
        m.update(
            wq_s=np.ascontiguousarray(inputs["L0_attn_Wq"][:, cols], np.float32),
            wk_s=np.ascontiguousarray(inputs["L0_attn_Wk"][:, cols], np.float32),
            wv_s=np.ascontiguousarray(inputs["L0_attn_Wv"][:, cols], np.float32),
            wsp_s=np.ascontiguousarray(inputs["L0_attn_Wspike"][:, cols], np.float32),
            wad_s=np.ascontiguousarray(inputs["L0_attn_Wad"][:, cols], np.float32),
            bad_s=np.ascontiguousarray(inputs["L0_attn_bad"][cols],
                                       np.float32)[:, None],
            wbe_s=wbe_s, bbe_s=bbe, asp_c=asp, bsp_c=bsp, hnw_t=hnw,
            oexp=oexp, wts=wtsh, invc512=invc, ipow512=ipow, hpow512=hpow,
            maskUS=mus, maskUI=mui, hsel=hsel,
            wu1=np.ascontiguousarray(inputs["L0_attn_Wu1"], np.float32),
            bu1_c=np.ascontiguousarray(inputs["L0_attn_bu1"], np.float32)[:, None],
            wu2=np.ascontiguousarray(inputs["L0_attn_Wu2"], np.float32),
            bu2_c=bu2,
            wo=np.ascontiguousarray(inputs["L0_attn_Wo"], np.float32),
            fnw_c=fnw,
            wff1=np.ascontiguousarray(inputs["L0_attn_Wff1"], np.float32),
            wff3=np.ascontiguousarray(inputs["L0_attn_Wff3"], np.float32),
            wff2=np.ascontiguousarray(inputs["L0_attn_Wff2"], np.float32),
            fc1q=np.ascontiguousarray(inputs["L0_ffn_fc1_q"], np.int32),
            fc1s_bc=s1bc, fc1b_c=b1c,
            fc2q=np.ascontiguousarray(inputs["L0_ffn_fc2_q"], np.int32),
            fc2s_bc=s2bc, fc2b_c=b2c, identf=identf)
        maps.append(m)
    return maps


def kernel(**inputs):
    nc = build_F()
    maps = stage_F_inputs(inputs)
    res = run_bass_kernel_spmd(nc, maps, core_ids=list(range(8)))
    out = np.empty((B, T, D), np.float32)
    for c in range(8):
        b, s = c // 4, c % 4
        out[b, s * TSL:(s + 1) * TSL] = res.results[c]["outp"]
    return out


def kernel3(**inputs):
    nc_a = build_A()
    maps_a = stage_A_inputs(inputs)
    res_a = run_bass_kernel_spmd(nc_a, maps_a, core_ids=list(range(8)))
    A_out = res_a.results

    nc_b = build_B(T)
    maps_b = stage_B_inputs(inputs, A_out, Ttot=T)
    res_b = run_bass_kernel_spmd(nc_b, maps_b, core_ids=list(range(8)))
    B_out = res_b.results

    nc_c = build_C()
    maps_c = stage_C_inputs(inputs, A_out, B_out)
    res_c = run_bass_kernel_spmd(nc_c, maps_c, core_ids=list(range(8)))

    out = np.empty((B, T, D), np.float32)
    for c in range(8):
        b, s = c // 4, c % 4
        out[b, s * TSL:(s + 1) * TSL] = res_c.results[c]["outp"]
    return out


# version 20
# speedup vs baseline: 1.0733x; 1.0733x over previous
from contextlib import ExitStack

import numpy as np
import ml_dtypes

import concourse.bass as bass
import concourse.mybir as mybir
from concourse.bass_utils import run_bass_kernel_spmd

F32 = mybir.dt.float32
BF16 = mybir.dt.bfloat16
I32 = mybir.dt.int32
U8 = mybir.dt.uint8
Alu = mybir.AluOpType
Act = mybir.ActivationFunctionType
AxX = mybir.AxisListType.X

B, T, D, H, DK, DV, RR, DFF = 2, 8192, 1024, 8, 64, 64, 64, 4096
TSL = T // 4
CH = 64
W = 4
NPOS = CH // W
GRAN = 512
NGR = T // GRAN
CPG = GRAN // CH
WPG = GRAN // W
DECAY, THR = 0.9, 0.5
NT = TSL // 128

bfd = ml_dtypes.bfloat16



def _c_of(tau):
    return THR * DECAY ** (-float(tau))


def _build_slots(pos):
    cs = [_c_of(4 * pos + i + 1) for i in range(W)]
    paths = [(0.0, [(None, +1)])]
    slots = []
    for i in range(W):
        new_paths = []
        for (o, terms) in paths:
            shift = (i, o)
            spiked = []
            for (ms, sg) in terms:
                nms = (shift,) if ms is None else tuple(sorted(set(ms) | {shift}))
                spiked.append((nms, sg))
                slots.append((nms, sg, i + 1))
            new_paths.append((o + cs[i], spiked))
            new_paths.append((o, terms + [(m, -sg) for (m, sg) in spiked]))
        paths = new_paths
    shifts, seen = [], set()
    for (ms, sg, lv) in slots:
        for sh in ms:
            if sh not in seen:
                seen.add(sh)
                shifts.append(sh)
    return slots, shifts


_SLOTS = [_build_slots(p) for p in range(NPOS)]
K_SLOTS = len(_SLOTS[0][0])
NSHIFT = len(_SLOTS[0][1])
LEVEL_SIZES = [1, 3, 9, 27]
SBLK = 1 + K_SLOTS


def _host_repl(v):
    return np.tile(np.asarray(v, np.float32)[None, :], (128, 1))


def host_win_weights():
    out = np.zeros((128, NPOS * SBLK), np.float32)
    for p in range(NPOS):
        slots, _ = _SLOTS[p]
        c0 = p * SBLK
        out[:, c0] = 1.0
        for j, (ms, sg, lv) in enumerate(slots):
            out[:, c0 + 1 + j] = np.float32(sg) * np.float32(_c_of(4 * p + lv))
    return out


def host_colvec(v, rows=128):
    v = np.asarray(v, np.float32)
    n = v.shape[0]
    nch = (n + rows - 1) // rows
    out = np.zeros((rows, nch), np.float32)
    for c in range(nch):
        seg = v[c * rows:(c + 1) * rows]
        out[:len(seg), c] = seg
    return out



def make_A(nc, io, nt_lim=None, do_conv=True, do_au=True):
    xs, xh = io["xs"], io["xh"]
    nw_c, cb_c, cw_c = io["nw_c"], io["cb_c"], io["cw_c"]
    g2_c, b2_c = io["g2_c"], io["b2_c"]
    wau_i, bau_c, ident = io["wau_i"], io["bau_c"], io["ident"]
    h_au, xn_fm_o, ln2_fm_o = io["h_au"], io["xn_fm_o"], io["ln2_fm_o"]

    RSTR = TSL + 4
    es = ExitStack()
    def _sb(n, sh, dt):
        return es.enter_context(nc.sbuf_tensor(n, sh, dt))
    def _ps(n, sh, dt):
        return es.enter_context(nc.psum_tensor(n, sh, dt))
    def _sem(n):
        return es.enter_context(nc.semaphore(n))
    xt = _sb("xt", [128, D], F32)
    xnt = _sb("xnt", [128, D], F32)
    sq = _sb("sq", [128, D], F32)
    tt1 = _sb("tt1", [128, D], BF16)
    tt2 = _sb("tt2", [128, D], BF16)
    tt0 = _sb("tt0", [128, D], BF16)
    st = _sb("st", [128, 16], F32)
    ids = _sb("a_ids", [128, 128], BF16)
    nw = _sb("nw", [128, 8], F32)
    cb = _sb("cb", [128, 8], F32)
    cw = _sb("cw", [128, 32], F32)
    g2 = _sb("g2", [128, 8], F32)
    b2 = _sb("b2", [128, 8], F32)
    bau = _sb("bau", [64, 1], F32)
    wauf = _sb("wauf", [128, 8 * RR], F32)
    waus = _sb("waus", [128, 8 * RR], BF16)
    xnfm = _sb("xnfm", [128, 8 * TSL], BF16)
    rnfm = _sb("rnfm", [128, 8 * RSTR], BF16)
    l2fm = _sb("l2fm", [128, 8 * TSL], BF16)
    hfm = _sb("hfm", [128, 8 * TSL], BF16)
    aufm = _sb("aufm", [64, TSL], BF16)
    cacc = _sb("cacc", [128, TSL], F32)
    pstA = _ps("pstA", [128, 512], BF16)
    pstB = _ps("pstB", [128, 512], BF16)
    psa = _ps("psa", [64, 512], F32)
    ld = _sem("ldA")
    sv = _sem("sv")
    ss = _sem("ss")
    stp = _sem("stp")
    sev = _sem("sev")
    svc = _sem("svc")
    shh = _sem("shh")
    sau = _sem("sau")
    sd = _sem("sdA")
    if True:
        NTT = (nt_lim if nt_lim is not None else NT) + 1

        def fn_sync(sy):
            sy.dma_start(out=nw[:], in_=nw_c[:]).then_inc(ld, 16)
            sy.dma_start(out=cb[:], in_=cb_c[:]).then_inc(ld, 16)
            sy.dma_start(out=cw[:], in_=cw_c[:]).then_inc(ld, 16)
            sy.dma_start(out=g2[:], in_=g2_c[:]).then_inc(ld, 16)
            sy.dma_start(out=b2[:], in_=b2_c[:]).then_inc(ld, 16)
            sy.dma_start(out=bau[:], in_=bau_c[:]).then_inc(ld, 16)
            sy.dma_start(out=ids[:], in_=ident[:]).then_inc(ld, 16)
            for kc in range(8):
                sy.dma_start(out=wauf[:, kc * RR:(kc + 1) * RR],
                             in_=wau_i[kc * 128:(kc + 1) * 128, :]).then_inc(ld, 16)
            for i in range(NTT):
                if i >= 1:
                    sy.wait_ge(ss, 10 * (i - 1) + 4)
                src = xh[:] if i == 0 else xs[(i - 1) * 128:i * 128, :]
                sy.dma_start(out=xt[:], in_=src).then_inc(ld, 16)

        def fn_vector(v):
            v.wait_ge(ld, 15 * 16)
            v.tensor_copy(waus[:], wauf[:])
            for i in range(NTT):
                v.wait_ge(ld, 16 * (15 + i + 1))
                v.tensor_reduce(st[:, 0:1], xt[:], AxX, Alu.add)
                v.drain()
                v.tensor_scalar(st[:, 0:1], st[:, 0:1], -1.0 / D, None,
                                Alu.mult).then_inc(sv, 1)
                v.wait_ge(ss, 10 * i + 1)
                v.tensor_reduce(st[:, 1:2], sq[:], AxX, Alu.add)
                v.drain()
                v.tensor_scalar(st[:, 1:2], st[:, 1:2], 1.0 / D, 1e-5, Alu.mult,
                                Alu.add).then_inc(sv, 1)
                v.wait_ge(ss, 10 * i + 2)
                v.reciprocal(st[:, 2:3], st[:, 10:11])
                v.drain()
                v.tensor_tensor(st[:, 3:4], st[:, 0:1], st[:, 2:3],
                                Alu.mult).then_inc(sv, 1)
                v.wait_ge(ss, 10 * i + 4)
                v.tensor_reduce(st[:, 4:5], sq[:], AxX, Alu.add)
                v.drain()
                v.tensor_scalar(st[:, 4:5], st[:, 4:5], 1.0 / D, 1e-6, Alu.mult,
                                Alu.add).then_inc(sv, 1)
                v.wait_ge(ss, 10 * i + 5)
                v.reciprocal(st[:, 5:6], st[:, 11:12]).then_inc(sv, 1)
                v.tensor_reduce(st[:, 6:7], xnt[:], AxX, Alu.add)
                v.drain()
                v.tensor_scalar(st[:, 6:7], st[:, 6:7], -1.0 / D, None,
                                Alu.mult).then_inc(sv, 1)
                v.wait_ge(ss, 10 * i + 7)
                v.tensor_reduce(st[:, 7:8], sq[:], AxX, Alu.add)
                v.drain()
                v.tensor_scalar(st[:, 7:8], st[:, 7:8], 1.0 / D, 1e-5, Alu.mult,
                                Alu.add).then_inc(sv, 1)
                v.wait_ge(ss, 10 * i + 8)
                v.reciprocal(st[:, 8:9], st[:, 12:13])
                v.drain()
                v.tensor_tensor(st[:, 9:10], st[:, 6:7], st[:, 8:9],
                                Alu.mult).then_inc(sv, 1)
            if not do_conv:
                return
            v.wait_ge(sev, 24 * NTT)
            for fc in range(8):
                if fc >= 1:
                    v.wait_ge(shh, fc)
                base = fc * RSTR
                v.tensor_scalar(cacc[:, 0:TSL], rnfm[:, base:base + TSL],
                                cw[:, fc * 4:fc * 4 + 1], None, Alu.mult)
                v.drain()
                last = None
                for j in range(1, 4):
                    last = v.scalar_tensor_tensor(
                        cacc[:, 0:TSL], rnfm[:, base + j:base + j + TSL],
                        cw[:, fc * 4 + j:fc * 4 + j + 1],
                        cacc[:, 0:TSL], Alu.mult, Alu.add)
                    if j < 3:
                        v.drain()
                last.then_inc(svc, 1)

        def fn_scalar(s):
            for i in range(NTT):
                s.wait_ge(sv, 8 * i + 1)
                s.activation(xnt[:], xt[:], Act.Identity, bias=st[:, 0:1], scale=1.0)
                s.activation(sq[:], xnt[:], Act.Square).then_inc(ss, 1)
                s.wait_ge(sv, 8 * i + 2)
                s.activation(st[:, 10:11], st[:, 1:2], Act.Sqrt).then_inc(ss, 1)
                s.wait_ge(sv, 8 * i + 3)
                s.activation(xnt[:], xt[:], Act.Identity, bias=st[:, 3:4], scale=st[:, 2:3])
                s.activation(sq[:], xnt[:], Act.Square).then_inc(ss, 2)
                s.wait_ge(sv, 8 * i + 4)
                s.activation(st[:, 11:12], st[:, 4:5], Act.Sqrt).then_inc(ss, 1)
                s.wait_ge(sv, 8 * i + 5)
                s.activation(tt1[:], xnt[:], Act.Identity, scale=st[:, 5:6]).then_inc(ss, 1)
                s.wait_ge(sv, 8 * i + 6)
                s.activation(sq[:], xnt[:], Act.Identity, bias=st[:, 6:7])
                s.activation(sq[:], sq[:], Act.Square).then_inc(ss, 1)
                s.wait_ge(sv, 8 * i + 7)
                s.activation(st[:, 12:13], st[:, 7:8], Act.Sqrt).then_inc(ss, 1)
                s.wait_ge(sv, 8 * i + 8)
                s.activation(tt2[:], xnt[:], Act.Identity, bias=st[:, 9:10], scale=st[:, 8:9])
                s.activation(tt0[:], xnt[:], Act.Copy).then_inc(ss, 2)
                for c in range(8):
                    ps = pstA if c % 2 == 0 else pstB
                    s.wait_ge(stp, 24 * i + 3 * c + 1)
                    if i == 0:
                        s.activation(rnfm[:, c * RSTR:c * RSTR + 3],
                                     ps[0:128, 125:128], Act.Identity,
                                     scale=nw[:, c:c + 1]).then_inc(sev, 1)
                    else:
                        s.activation(rnfm[:, c * RSTR + 3 + (i - 1) * 128:
                                          c * RSTR + 3 + i * 128],
                                     ps[0:128, 0:128], Act.Identity,
                                     scale=nw[:, c:c + 1]).then_inc(sev, 1)
                    s.wait_ge(stp, 24 * i + 3 * c + 2)
                    if i >= 1:
                        s.activation(l2fm[:, c * TSL + (i - 1) * 128:c * TSL + i * 128],
                                     ps[0:128, 128:256], Act.Identity,
                                     bias=b2[:, c:c + 1],
                                     scale=g2[:, c:c + 1]).then_inc(sev, 1)
                    else:
                        s.activation(cacc[:, 0:128], ps[0:128, 128:256],
                                     Act.Identity).then_inc(sev, 1)
                    s.wait_ge(stp, 24 * i + 3 * c + 3)
                    if i >= 1:
                        s.copy(xnfm[:, c * TSL + (i - 1) * 128:c * TSL + i * 128],
                               ps[0:128, 256:384]).then_inc(sev, 1)
                    else:
                        s.copy(cacc[:, 0:128], ps[0:128, 256:384]).then_inc(sev, 1)

            if not do_conv:
                return
            for fc in range(8):
                s.wait_ge(svc, fc + 1)
                s.activation(hfm[:, fc * TSL:(fc + 1) * TSL], cacc[:, 0:TSL],
                             Act.Silu, bias=cb[:, fc:fc + 1], scale=1.0).then_inc(shh, 1)

            if not do_au:
                return
            for nb in range(4):
                s.wait_ge(stp, 24 * NTT + 8 * (nb + 1))
                s.activation(aufm[:, nb * 512:(nb + 1) * 512], psa[:],
                             Act.Silu, bias=bau[:], scale=1.0).then_inc(sau, 1)


        def fn_tensor(t):
            for i in range(NTT):
                t.wait_ge(ss, 10 * i + 10)
                for c in range(8):
                    ps = pstA if c % 2 == 0 else pstB
                    if 24 * i + 3 * c - 3 > 0:
                        t.wait_ge(sev, 24 * i + 3 * c - 3)
                    t.transpose(ps[0:128, 0:128], tt1[:, c * 128:(c + 1) * 128], ids[:]).then_inc(stp, 1)

                    t.transpose(ps[0:128, 128:256], tt2[:, c * 128:(c + 1) * 128], ids[:]).then_inc(stp, 1)

                    t.transpose(ps[0:128, 256:384], tt0[:, c * 128:(c + 1) * 128], ids[:]).then_inc(stp, 1)

            if not do_au:
                return
            t.wait_ge(sev, 24 * NTT)
            for nb in range(4):
                if nb >= 1:
                    t.wait_ge(sau, nb)
                for kc in range(8):
                    t.matmul(psa[:, 0:512], waus[:, kc * RR:(kc + 1) * RR],
                             xnfm[:, kc * TSL + nb * 512:kc * TSL + (nb + 1) * 512],
                             start=(kc == 0), stop=(kc == 7)).then_inc(stp, 1)


        def fn_gpsimd(g):
            if do_conv:
                g.wait_ge(shh, 8)
            if do_au:
                g.wait_ge(sau, 4)
            g.wait_ge(sev, 24 * NTT)
            n = 0
            for fc in range(8):
                if do_conv:
                    g.dma_start(out=h_au[fc * 128:(fc + 1) * 128, :],
                                in_=hfm[:, fc * TSL:(fc + 1) * TSL]).then_inc(sd, 16)
                    n += 1
                g.dma_start(out=xn_fm_o[fc * 128:(fc + 1) * 128, :],
                            in_=xnfm[:, fc * TSL:(fc + 1) * TSL]).then_inc(sd, 16)
                g.dma_start(out=ln2_fm_o[fc * 128:(fc + 1) * 128, :],
                            in_=l2fm[:, fc * TSL:(fc + 1) * TSL]).then_inc(sd, 16)
                n += 2
            if do_au:
                g.dma_start(out=h_au[D:D + RR, :], in_=aufm[:]).then_inc(sd, 16)
                n += 1
            g.wait_ge(sd, n * 16)
    n_st = ((8 if do_conv else 0) + 16 + (1 if do_au else 0)) * 16
    fns = dict(sync=fn_sync, vector=fn_vector, scalar=fn_scalar,
               tensor=fn_tensor, gpsimd=fn_gpsimd)
    return fns, es, (sd, n_st)


def build_A(nt_lim=None, do_conv=True, do_au=True):
    nc = bass.Bass("TRN2", target_bir_lowering=False, detect_race_conditions=False)
    io = dict(
        xs=nc.dram_tensor("xs", [TSL, D], F32, kind="ExternalInput").ap(),
        xh=nc.dram_tensor("xh", [128, D], F32, kind="ExternalInput").ap(),
        nw_c=nc.dram_tensor("nw_c", [128, 8], F32, kind="ExternalInput").ap(),
        cb_c=nc.dram_tensor("cb_c", [128, 8], F32, kind="ExternalInput").ap(),
        cw_c=nc.dram_tensor("cw_c", [128, 32], F32, kind="ExternalInput").ap(),
        g2_c=nc.dram_tensor("g2_c", [128, 8], F32, kind="ExternalInput").ap(),
        b2_c=nc.dram_tensor("b2_c", [128, 8], F32, kind="ExternalInput").ap(),
        wau_i=nc.dram_tensor("wau", [D, RR], F32, kind="ExternalInput").ap(),
        bau_c=nc.dram_tensor("bau_c", [64, 1], F32, kind="ExternalInput").ap(),
        ident=nc.dram_tensor("ident", [128, 128], BF16, kind="ExternalInput").ap(),
        h_au=nc.dram_tensor("h_au", [D + RR, TSL], BF16, kind="ExternalOutput").ap(),
        xn_fm_o=nc.dram_tensor("xn_fm", [D, TSL], BF16, kind="ExternalOutput").ap(),
        ln2_fm_o=nc.dram_tensor("ln2h_fm", [D, TSL], BF16, kind="ExternalOutput").ap(),
    )
    fns, es, _ = make_A(nc, io, nt_lim, do_conv, do_au)
    blk = es.enter_context(nc.Block())
    with es:
        blk.sync(fns["sync"])
        blk.vector(fns["vector"])
        blk.scalar(fns["scalar"])
        blk.tensor(fns["tensor"])
        blk.gpsimd(fns["gpsimd"])
    return nc


def stage_A_inputs(inputs):
    x = np.ascontiguousarray(inputs["L0_attn_x"])
    nw_c = host_colvec(inputs["L0_attn_norm_in_w"])
    cb_c = host_colvec(inputs["L0_attn_conv_b"])
    cwm = np.ascontiguousarray(inputs["L0_attn_conv_w"][:, 0, :])
    cw_c = np.zeros((128, 32), np.float32)
    for fc in range(8):
        cw_c[:, fc * 4:(fc + 1) * 4] = cwm[fc * 128:(fc + 1) * 128, :]
    g2_c = host_colvec(inputs["L0_ffn_ln_gamma"])
    b2_c = host_colvec(inputs["L0_ffn_ln_beta"])
    wau = np.ascontiguousarray(inputs["L0_attn_Wau"], np.float32)
    bau_c = np.ascontiguousarray(inputs["L0_attn_bau"], np.float32)[:, None]
    ident = np.eye(128, dtype=bfd)
    maps = []
    for c in range(8):
        b, s = c // 4, c % 4
        xs = np.ascontiguousarray(x[b, s * TSL:(s + 1) * TSL, :], np.float32)
        if s == 0:
            xhh = np.zeros((128, D), np.float32)
        else:
            xhh = np.ascontiguousarray(x[b, s * TSL - 128:s * TSL, :], np.float32)
        maps.append(dict(xs=xs, xh=xhh, nw_c=nw_c, cb_c=cb_c, cw_c=cw_c,
                         g2_c=g2_c, b2_c=b2_c, wau=wau, bau_c=bau_c, ident=ident))
    return maps


def kernel(**inputs):
    raise NotImplementedError




def _min_plan():
    slots, shifts = _SLOTS[0]
    sh_index = {sh: i for i, sh in enumerate(shifts)}
    built = {}
    plan = []
    for j, (ms, sg, lv) in enumerate(slots):
        key = frozenset(ms)
        if key in built:
            plan.append(('copy', built[key][1]))
        elif len(ms) == 1:
            plan.append(('ref', sh_index[ms[0]]))
            built[key] = ('sl', j)
        else:
            parent = None
            for sh in ms:
                pk = frozenset(ms) - {sh}
                if pk in built:
                    parent = (built[pk], sh_index[sh])
                    break
            assert parent is not None, f"no parent for slot {j}: {ms}"
            plan.append(('min', parent[0], parent[1]))
            built[key] = ('sl', j)
    return plan


MIN_PLAN = _min_plan()
for _p in range(1, NPOS):
    assert [(tuple(tr for tr, _ in ms), sg, lv) for ms, sg, lv in _SLOTS[_p][0]] == \
           [(tuple(tr for tr, _ in ms), sg, lv) for ms, sg, lv in _SLOTS[0][0]]
    assert [tr for tr, _ in _SLOTS[_p][1]] == [tr for tr, _ in _SLOTS[0][1]]
SLOT_TR = [_SLOTS[0][1][i][0] for i in range(NSHIFT)]


def host_shift_exp():
    out = np.zeros((128, NSHIFT * 128), np.float32)
    for si in range(NSHIFT):
        offs = np.array([_SLOTS[p][1][si][1] for p in range(NPOS)], np.float32)
        out[:, si * 128:(si + 1) * 128] = np.tile(offs, CPG)[None, :]
    return out


def host_invc():
    return np.tile(np.array([1.0 / _c_of(t + 1) for t in range(CH)], np.float32),
                   (128, GRAN // CH))


def host_ipow():
    return np.tile(np.array([DECAY ** (-(t + 1.0)) for t in range(CH)], np.float32),
                   (128, GRAN // CH))


def host_hpow():
    return np.tile(np.array([THR * DECAY ** (-(t + 1.0)) for t in range(CH)],
                            np.float32), (128, GRAN // CH))


class _EngProxy:

    def __init__(self, e):
        object.__setattr__(self, "_e", e)
        object.__setattr__(self, "_last", [None])

    def __getattr__(self, name):
        a = getattr(self._e, name)
        if not callable(a):
            return a
        def w(*args, **kw):
            r = a(*args, **kw)
            if r is not None and hasattr(r, "then_inc"):
                self._last[0] = r
            return r
        return w


class Eng:

    def __init__(self, sem):
        self.ops = []
        self.sem = sem
        self.n = 0

    def wait(self, other, val):
        if val > 0:
            self.ops.append(("w", lambda e, o=other.sem, v=val: e.wait_ge(o, v)))

    def wait_sem(self, sem, val):
        self.ops.append(("w", lambda e, o=sem, v=val: e.wait_ge(o, v)))

    def do(self, fn):
        self.ops.append(("d", fn))

    def tick(self, k=1):
        self.n += k
        self.ops.append(("t", k))
        return self.n

    def run(self, e):
        p = _EngProxy(e)
        for kind, f in self.ops:
            if kind == "t":
                li = p._last[0]
                assert li is not None, "tick with no preceding instruction"
                li.then_inc(self.sem, f)
            else:
                f(p)


class ECtx:

    def __init__(self, nc, es):
        self.ld = es.enter_context(nc.semaphore("ld"))
        self.ldg = es.enter_context(nc.semaphore("ldg"))
        self.tks = es.enter_context(nc.semaphore("tk"))
        self.sks = es.enter_context(nc.semaphore("sk"))
        self.vks = es.enter_context(nc.semaphore("vk"))
        self.gks = es.enter_context(nc.semaphore("gk"))
        self.TE, self.SE = Eng(self.tks), Eng(self.sks)
        self.VE, self.GE = Eng(self.vks), Eng(self.gks)
        self.dma_n = [0]
        self.dmas = []
        self.gdma_n = [0]

    def dma(self, dst, src, wait=None):
        self.dmas.append((wait, lambda sy, d=dst, s=src: sy.dma_start(out=d, in_=s)))
        self.dma_n[0] += 16
        return self.dma_n[0]


def emit_B(nc, E, io, Ttot=T, dbg=False, fused=False):
    ngr = Ttot // GRAN
    wq_i, wk_i, wv_i, wsp_i = io["wq_i"], io["wk_i"], io["wv_i"], io["wsp_i"]
    wad_i, bad_i, wbe_i, bbe_i = io["wad_i"], io["bad_i"], io["wbe_i"], io["bbe_i"]
    asp_i, bsp_i, hnw_i = io["asp_i"], io["bsp_i"], io["hnw_i"]
    oexp_i, wts_i = io["oexp_i"], io["wts_i"]
    invc_i, ipow_i, hpow_i = io["invc_i"], io["ipow_i"], io["hpow_i"]
    ident_i, mau_i, mui_i = io["ident_i"], io["mau_i"], io["mui_i"]
    if fused:
        hall, yrs, hsel_i = io["hall"], io["yrs"], io["hsel_i"]
        def hsl(r0, nr, gb0, w_):
            s_ = gb0 // TSL
            c0 = gb0 % TSL
            return hall[s_ * (D + RR) + r0:s_ * (D + RR) + r0 + nr, c0:c0 + w_]
    else:
        hin, yout, gout = io["hin"], io["yout"], io["gout"]
        def hsl(r0, nr, gb0, w_):
            return hin[r0:r0 + nr, gb0:gb0 + w_]

    es = ExitStack()
    def _sb(n, sh, dt):
        return es.enter_context(nc.sbuf_tensor(n, sh, dt))
    def _ps(n, sh, dt):
        return es.enter_context(nc.psum_tensor(n, sh, dt))
    wqs = _sb("wqs", [128, 8 * 128], BF16)
    wks = _sb("wks", [128, 8 * 128], BF16)
    wvs = _sb("wvs", [128, 8 * 128], BF16)
    wsps = _sb("wsps", [128, 8 * 128], BF16)
    wads = _sb("wads", [64, 128], BF16)
    wbes = _sb("wbes", [128, 16], BF16)
    badc = _sb("badc", [128, 1], F32)
    bbec = _sb("bbec", [128, 2], F32)
    aspc = _sb("aspc", [128, 1], F32)
    bspc = _sb("bspc", [128, 2], BF16)
    hnwt = _sb("hnwt", [128, 64], F32)
    oexp = _sb("oexp_s", [128, NSHIFT * 128], F32)
    wts = _sb("wts_s", [128, NPOS * SBLK], F32)
    invc = _sb("invc", [128, GRAN], F32)
    ipow = _sb("ipow", [128, GRAN], F32)
    hpow = _sb("hpow", [128, GRAN], F32)
    ids = _sb("ids", [128, 128], BF16)
    mUS = _sb("mUS", [128, 64], U8)
    mUI = _sb("mUI", [128, 64], U8)
    zt64 = _sb("zt64", [128, 64], F32)
    onesg = _sb("onesg", [128, GRAN], F32)
    o9g = _sb("o9g", [128, GRAN], F32)
    ones64 = _sb("ones64", [128, 64], BF16)
    ones64f = _sb("ones64f", [128, 64], F32)
    hgP = [_sb("hg0", [128, 8 * GRAN], BF16), _sb("hg1", [128, 8 * GRAN], BF16)]
    agbP = [_sb("agb0", [64, GRAN], BF16), _sb("agb1", [64, GRAN], BF16)]
    qraw = _sb("qraw", [128, GRAN], F32)
    kraw = _sb("kraw", [128, GRAN], F32)
    qn = _sb("qn", [128, GRAN], BF16)
    kn = _sb("kn", [128, GRAN], BF16)
    dg = _sb("dg", [128, GRAN], F32)
    abg = _sb("abg", [128, GRAN], F32)
    sqb = _sb("sqb", [128, GRAN], F32)
    nrm = _sb("nrm", [128, GRAN], F32)
    vts = _sb("vts", [128, 8 * 64], F32)
    bbt = _sb("bbt", [128, 8], F32)
    bdt = _sb("bdt", [128, 8], F32)
    gst = _sb("gst", [128, 8], F32)
    bcol = _sb("bcol", [128, 8], F32)
    nbcol = _sb("nbcol", [128, 8], F32)
    gsum = _sb("gsum", [128, GRAN], F32)
    gfm = _sb("gfm", [128, GRAN], BF16)
    gaf = _sb("gaf", [128, GRAN], F32)
    argf = _sb("argf", [128, GRAN], F32)
    laf = _sb("laf", [128, GRAN], F32)
    actf = _sb("actf", [128, GRAN], F32)
    Lg = _sb("Lg", [128, GRAN], F32)
    E1g = _sb("E1g", [128, GRAN], F32)
    E3g = _sb("E3g", [128, GRAN], F32)
    E4g = _sb("E4g", [128, GRAN], F32)
    ktb = _sb("ktb", [128, GRAN], BF16)
    ktpb = _sb("ktpb", [128, GRAN], BF16)
    khpb = _sb("khpb", [128, GRAN], BF16)
    qtb = _sb("qtb", [128, GRAN], BF16)
    qtpb = _sb("qtpb", [128, GRAN], BF16)
    kbarb = _sb("kbarb", [128, GRAN], BF16)
    ecol = _sb("ecol", [128, 16], F32)
    Egb = _sb("Egb", [128, GRAN], F32)
    thg = _sb("thg", [128, GRAN], F32)
    thr = _sb("thr", [128, WPG * 51], F32)
    wnd = _sb("wnd", [128, WPG * SBLK], F32)
    gb_ = _sb("gb", [128, 64], F32)
    AbarP = [_sb("Abar0", [128, 64], BF16), _sb("Abar1", [128, 64], BF16)]
    AorP = [_sb("Aor0", [128, 64], BF16), _sb("Aor1", [128, 64], BF16)]
    Atmpb = _sb("Atmpb", [128, 64], BF16)
    Atmpo = _sb("Atmpo", [128, 64], BF16)
    Rb = _sb("Rb", [128, 64], BF16)
    Yb = _sb("Yb", [128, 64], BF16)
    Ub = _sb("Ub", [128, 64], BF16)
    kbtmP = [_sb("kbtm0", [128, 64], BF16), _sb("kbtm1", [128, 64], BF16)]
    Sst = _sb("Sst", [128, 64], BF16)
    sscb = _sb("ssc", [128, 64], BF16)
    wmbP = [_sb("wmb0", [128, 64], BF16), _sb("wmb1", [128, 64], BF16)]
    wvt = _sb("wvt", [128, 64], F32)
    ytmp = _sb("ytmp", [128, 64], F32)
    sqy = _sb("sqy", [128, 64], F32)
    ynm = _sb("ynm", [128, 64], BF16)
    ytm = _sb("ytm", [128, 64], BF16)
    yfm = _sb("yfm", [128, GRAN], BF16)
    styc = _sb("styc", [128, 4], F32)
    pproj = _ps("pproj", [128, 512], F32)
    pv = _ps("pv", [128, 128], F32)
    pb1 = _ps("pb1", [128, 64], F32)
    pb2 = _ps("pb2", [128, 64], F32)
    pY = _ps("pY", [128, 64], F32)
    pSO = _ps("pSO", [128, 128], F32)
    ptr = _ps("ptr", [128, 128], BF16)
    pbc = _ps("pbc", [128, 512], F32)
    ysc = _sb("ysc", [128, 4 * GRAN], BF16) if fused else None
    hselc = _sb("hselc", [128, 4], F32) if fused else None

    if True:
        TE, SE, VE, GE = E.TE, E.SE, E.VE, E.GE
        ld, ldg = E.ld, E.ldg
        tks, sks, vks, gks = E.tks, E.sks, E.vks, E.gks
        dma, dma_n, dmas = E.dma, E.dma_n, E.dmas

        sd_n = [0]
        lastuseB = {}

        def bank_waitB(eng, bank):
            lu = lastuseB.get(id(bank))
            if lu is not None:
                eng.wait(lu[0], lu[1])

        cl = [(badc[:], bad_i[:]), (bbec[:], bbe_i[:]),
              (aspc[:], asp_i[:]), (hnwt[:], hnw_i[:]),
              (oexp[:], oexp_i[:]), (wts[:], wts_i[:]), (invc[:], invc_i[:]),
              (ipow[:], ipow_i[:]), (hpow[:], hpow_i[:]), (ids[:], ident_i[:]),
              (mUS[:], mau_i[:]), (mUI[:], mui_i[:])]
        if fused:
            cl.append((hselc[:], hsel_i[:]))
        for ci, (dst, src) in enumerate(cl):
            dma(dst, src, io.get("const_wait") if ci == 0 else None)
        gload = []
        for kc in range(8):
            gload.append((wqs[:, kc * 128:(kc + 1) * 128],
                          wq_i[kc * 128:(kc + 1) * 128, :]))
            gload.append((wks[:, kc * 128:(kc + 1) * 128],
                          wk_i[kc * 128:(kc + 1) * 128, :]))
            gload.append((wvs[:, kc * 128:(kc + 1) * 128],
                          wv_i[kc * 128:(kc + 1) * 128, :]))
            gload.append((wsps[:, kc * 128:(kc + 1) * 128],
                          wsp_i[kc * 128:(kc + 1) * 128, :]))
            gload.append((wbes[:, kc * 2:kc * 2 + 2],
                          wbe_i[kc * 128:(kc + 1) * 128, :]))
        gload.append((wads[:], wad_i[:]))
        gload.append((bspc[:], bsp_i[:]))
        def fgl(g_):
            for dst, src in gload:
                g_.dma_start(out=dst, in_=src).then_inc(ldg, 16)
        GE.do(fgl)
        ngl = E.gdma_n[0] + 16 * len(gload)
        E.gdma_n[0] = ngl
        if fused:
            cw_ = io["const_wait"]
            VE.wait_sem(cw_[0], cw_[1])
            SE.wait_sem(cw_[0], cw_[1])
            TE.wait_sem(cw_[0], cw_[1])
        VE.do(lambda v: v.memset(onesg[:], 1.0))
        VE.do(lambda v: v.memset(o9g[:], DECAY))
        VE.do(lambda v: v.memset(zt64[:], 0.0))
        VE.do(lambda v: v.memset(ones64[:], 1.0))
        VE.do(lambda v: v.memset(ones64f[:], 1.0))
        VE.do(lambda v: v.memset(wnd[:, 0:1], 0.0))
        VE.do(lambda v: v.memset(Sst[:], 0.0))
        vinit = VE.tick()
        TE.wait(VE, vinit)
        TE.wait_sem(ld, dma_n[0])
        TE.wait_sem(ldg, ngl)
        SE.wait(VE, vinit)
        GE.wait(VE, vinit)

        BND2 = float(THR)
        BNDM = float(DECAY ** 64)

        hv_hist = []
        for g in range(ngr):
            gb0 = g * GRAN
            g0w = io.get("g0_wait") if fused else None
            hg = hgP[g % 2]
            agb = agbP[g % 2]
            hwait = (tks, hv_hist[g - 2]) if g >= 2 else (g0w if g == 0 else None)
            for kc in range(8):
                dma(hg[:, kc * GRAN:(kc + 1) * GRAN],
                    hsl(kc * 128, 128, gb0, GRAN), hwait)
            dma(agb[:], hsl(D, RR, gb0, GRAN), hwait)
            TE.wait_sem(ld, dma_n[0])

            def proj(wtile, dst, bias=None):
                def ft(t, w=wtile, hh=hg):
                    for kc in range(8):
                        t.matmul(pproj[:, 0:GRAN], w[:, kc * 128:(kc + 1) * 128],
                                 hh[:, kc * GRAN:(kc + 1) * GRAN],
                                 start=(kc == 0), stop=(kc == 7), skip_group_check=True)
                TE.do(ft)
                tv = TE.tick()
                SE.wait(TE, tv)
                def fs(s, d=dst, b=bias):
                    if b is not None:
                        s.activation(d, pproj[:, 0:GRAN], Act.Identity, bias=b)
                    else:
                        s.activation(d, pproj[:, 0:GRAN], Act.Identity)
                SE.do(fs)
                sv = SE.tick()
                TE.wait(SE, sv)
                return sv

            proj(wqs, qraw[:])
            proj(wks, kraw[:])
            sv_d = proj(wsps, dg[:])
            def fab(t, ab_=agb):
                t.matmul(pproj[:, 0:GRAN], wads[:], ab_[:], start=True, stop=True, skip_group_check=True)
            TE.do(fab)
            tv = TE.tick()
            SE.wait(TE, tv)
            SE.do(lambda s: s.activation(abg[:], pproj[:, 0:GRAN], Act.Identity,
                                         bias=badc[:]))
            sv_ab = SE.tick()
            TE.wait(SE, sv_ab)

            bank_waitB(TE, pv)
            for tb in range(4):
                def fv(t, tb=tb, hh=hg):
                    for kc in range(8):
                        t.matmul(pv[:, 0:128],
                                 hh[:, kc * GRAN + tb * 128:kc * GRAN + (tb + 1) * 128],
                                 wvs[:, kc * 128:(kc + 1) * 128],
                                 start=(kc == 0), stop=(kc == 7), skip_group_check=True)
                TE.do(fv)
                tv = TE.tick()
                SE.wait(TE, tv)
                def fsv(s, tb=tb):
                    for half in range(2):
                        ch = 2 * tb + half
                        rows = slice(half * 64, half * 64 + 64)
                        s.copy(vts[0:64, ch * 64:(ch + 1) * 64], pv[rows, 0:64])
                        s.copy(vts[64:128, ch * 64:(ch + 1) * 64], pv[rows, 64:128])
                SE.do(fsv)
                sv = SE.tick()
                TE.wait(SE, sv)
                def fbb(t, tb=tb, hh=hg):
                    for kc in range(8):
                        t.matmul(pv[:, 0:2],
                                 hh[:, kc * GRAN + tb * 128:kc * GRAN + (tb + 1) * 128],
                                 wbes[:, kc * 2:kc * 2 + 2],
                                 start=(kc == 0), stop=(kc == 7), skip_group_check=True)
                TE.do(fbb)
                tv = TE.tick()
                SE.wait(TE, tv)
                def fsb(s, tb=tb):
                    for half in range(2):
                        ch = 2 * tb + half
                        rows = slice(half * 64, half * 64 + 64)
                        s.activation(bbt[0:64, ch:ch + 1], pv[rows, 0:1],
                                     Act.Identity, bias=bbec[0:64, 0:1])
                        s.activation(bbt[64:128, ch:ch + 1], pv[rows, 1:2],
                                     Act.Identity, bias=bbec[64:128, 1:2])
                SE.do(fsb)
                sv = SE.tick()
                TE.wait(SE, sv)
                lastuseB[id(pv)] = (SE, sv)
            hv_hist.append(TE.n)

            def qknorm(raw, out_bf):
                SE.wait(VE, VE.n)
                SE.do(lambda s, r=raw: s.activation(sqb[:], r, Act.Square))
                sv1 = SE.tick()
                TE.wait(SE, sv1)
                bank_waitB(TE, pbc)
                def fsum(t):
                    t.matmul(pbc[0:64, 0:GRAN], ones64f[0:64, 0:64], sqb[0:64, :],
                             start=True, stop=True, skip_group_check=True)
                    t.matmul(pbc[64:128, 0:GRAN], ones64f[64:128, 0:64],
                             sqb[64:128, :], start=True, stop=True,
                             tile_position=(64, 64), skip_group_check=True)
                TE.do(fsum)
                tv1 = TE.tick()
                SE.wait(TE, tv1)
                SE.do(lambda s: s.activation(nrm[:], pbc[:, 0:GRAN], Act.Sqrt))
                sv2 = SE.tick()
                lastuseB[id(pbc)] = (SE, sv2)
                TE.wait(SE, sv2)
                VE.wait(SE, sv2)
                def fnrm(v, r=raw, o=out_bf):
                    v.tensor_scalar(nrm[:], nrm[:], 1e-6, None, Alu.add)
                    v.drain()
                    v.reciprocal(nrm[:], nrm[:])
                    v.drain()
                    v.tensor_tensor(o, r, nrm[:], Alu.mult)
                VE.do(fnrm)
                return VE.tick()

            gv_q = qknorm(qraw[:], qn[:])
            gv_k = qknorm(kraw[:], kn[:])

            VE.wait(SE, sv_d)
            def fscan(v):
                for ch in range(CPG):
                    sl = slice(ch * CH, (ch + 1) * CH)
                    v.tensor_tensor_scan(Egb[:, sl], o9g[:, sl], dg[:, sl], 0.0,
                                         Alu.mult, Alu.add)
                v.drain()
                v.tensor_tensor(thg[:], Egb[:], ipow[:], Alu.mult)
                v.drain()
                v.tensor_tensor(thg[:], thg[:], hpow[:], Alu.subtract)
                v.drain()
            VE.do(fscan)
            vv_th = VE.tick()

            scr_map, nscr, ref_dest = {}, 0, {}
            for j, op in enumerate(MIN_PLAN):
                if op[0] == 'ref':
                    ref_dest[op[1]] = j + 1
            ref_dest = {}
            for j, op in enumerate(MIN_PLAN):
                if op[0] == 'ref':
                    ref_dest[op[1]] = j
            for si in range(NSHIFT):
                if si not in ref_dest:
                    scr_map[si] = 40 + nscr
                    nscr += 1
            assert nscr <= 11

            def sh_dest(si):
                return ref_dest[si] if si in ref_dest else scr_map[si]

            def fshift(v):
                for si in range(NSHIFT):
                    tr = SLOT_TR[si]
                    v.tensor_tensor(thr[:, sh_dest(si)::51], thg[:, tr::4],
                                    oexp[:, si * 128:(si + 1) * 128],
                                    Alu.subtract)
                v.drain()
            VE.do(fshift)

            emit_ops = []
            for j, op in enumerate(MIN_PLAN):
                if op[0] == 'ref':
                    continue
                if op[0] == 'copy':
                    emit_ops.append((j, {op[1]}, op))
                else:
                    src0 = op[1]
                    s0 = src0[1] if src0[0] == 'sl' else sh_dest(src0[1])
                    emit_ops.append((j, {s0, sh_dest(op[2])}, op))

            def fmins(v):
                for k, (j, _, op) in enumerate(emit_ops):
                    if op[0] == 'copy':
                        v.tensor_copy(thr[:, j::51], thr[:, op[1]::51])
                    else:
                        src0 = op[1]
                        a_ap = thr[:, src0[1]::51] if src0[0] == 'sl' else \
                            thr[:, sh_dest(src0[1])::51]
                        v.tensor_tensor(thr[:, j::51], a_ap,
                                        thr[:, sh_dest(op[2])::51], Alu.min)
                    if any(j in emit_ops[k2][1]
                           for k2 in range(k + 1, min(k + 4, len(emit_ops)))):
                        v.drain()
                v.drain()
            VE.do(fmins)

            def fchain(v):
                for w_ in range(WPG):
                    ch, p = divmod(w_, NPOS)
                    wb = w_ * SBLK
                    nxt = ((w_ + 1) % WPG) * SBLK
                    v.tensor_scalar(wnd[:, wb + 1:wb + SBLK],
                                    thr[:, w_ * 51:w_ * 51 + 40],
                                    wnd[:, wb:wb + 1], None, Alu.is_gt)
                    v.drain()
                    v.scalar_tensor_tensor(
                        wnd[:, wb:wb + SBLK], wnd[:, wb:wb + SBLK], 1.0,
                        wts[:, p * SBLK:(p + 1) * SBLK], Alu.mult, Alu.mult,
                        accum_out=wnd[:, nxt:nxt + 1])
                    v.drain()
                    if p == NPOS - 1:
                        v.tensor_scalar(wnd[:, nxt:nxt + 1], wnd[:, nxt:nxt + 1],
                                        thg[:, ch * CH + CH - 1:ch * CH + CH],
                                        BNDM, Alu.subtract, Alu.mult)
                        v.drain()
                        v.tensor_scalar(wnd[:, nxt:nxt + 1], wnd[:, nxt:nxt + 1],
                                        BND2, None, Alu.subtract)
                        v.drain()
            VE.do(fchain)
            VE.tick()

            def fextr(v):
                base = 1
                for lv in range(W):
                    seg = LEVEL_SIZES[lv]
                    if seg == 1:
                        v.tensor_copy(gsum[:, lv::4], wnd[:, base::SBLK])
                    else:
                        in_ap = wnd[:].rearrange("p (w s) -> p w s", s=SBLK)[:, :, base:base + seg]
                        v.tensor_reduce(gsum[:, lv::4], in_ap, AxX, Alu.add)
                    base += seg
                v.drain()
                v.tensor_tensor(gfm[:], gsum[:], invc[:], Alu.mult)
            VE.do(fextr)
            vv_g = VE.tick()

            SE.wait(VE, vv_g)
            SE.do(lambda s: s.activation(gaf[:], gfm[:], Act.Identity, scale=aspc[:]))
            sv_ga = SE.tick()
            VE.wait(SE, sv_ga)
            VE.do(lambda v: v.tensor_tensor(argf[:], abg[:], gaf[:], Alu.add))
            vv_arg = VE.tick()
            TE.wait(VE, vv_g)
            bank_waitB(TE, pbc)
            def factive(t):
                t.matmul(pbc[0:64, 0:GRAN], ones64[0:64, 0:64], gfm[0:64, :],
                         start=True, stop=True, skip_group_check=True)
                t.matmul(pbc[64:128, 0:GRAN], ones64[64:128, 0:64], gfm[64:128, :],
                         start=True, stop=True, tile_position=(64, 64), skip_group_check=True)
            TE.do(factive)
            tv_act = TE.tick()
            VE.wait(TE, tv_act)
            VE.do(lambda v: v.tensor_scalar(actf[:], pbc[:, 0:GRAN], 0.5, None,
                                            Alu.is_gt))
            vv_actf = VE.tick()
            lastuseB[id(pbc)] = (VE, vv_actf)
            TE.wait(VE, vv_actf)
            SE.wait(VE, vv_arg)
            SE.do(lambda s: s.activation(argf[:], argf[:], Act.Sigmoid))
            sv_sg = SE.tick()
            VE.wait(SE, sv_sg)
            VE.do(lambda v: v.tensor_scalar(argf[:], argf[:], 1e-30, None, Alu.max))
            vv_gd = VE.tick()
            SE.wait(VE, vv_gd)
            SE.do(lambda s: s.activation(argf[:], argf[:], Act.Ln))
            sv_ln = SE.tick()
            VE.wait(SE, sv_ln)
            VE.do(lambda v: v.tensor_tensor(laf[:], argf[:], actf[:], Alu.mult))
            VE.do(lambda v: v.drain())
            def flsc(v):
                for ch in range(CPG):
                    sl = slice(ch * CH, (ch + 1) * CH)
                    v.tensor_tensor_scan(Lg[:, sl], onesg[:, sl], laf[:, sl], 0.0,
                                         Alu.mult, Alu.add)
            VE.do(flsc)
            vv_L = VE.tick()

            def fbd(t):
                for tb in range(4):
                    t.matmul(pv[:, 0:2], gfm[0:64, tb * 128:(tb + 1) * 128],
                             bspc[0:64, 0:2], start=True, stop=True, skip_group_check=True)
                    t.matmul(pv[:, 2:4], gfm[64:128, tb * 128:(tb + 1) * 128],
                             bspc[64:128, 0:2], start=True, stop=True,
                             tile_position=(64, 0), skip_group_check=True)
                    t.sem_inc(tks, 1)
                    t.wait_ge(sks, 0)
            TE.wait(VE, vv_g)
            bank_waitB(TE, pv)
            for tb in range(4):
                def fbd1(t, tb=tb):
                    t.matmul(pv[:, 0:2], gfm[0:64, tb * 128:(tb + 1) * 128],
                             bspc[0:64, 0:2], start=True, stop=True, skip_group_check=True)
                    t.matmul(pv[:, 2:4], gfm[64:128, tb * 128:(tb + 1) * 128],
                             bspc[64:128, 0:2], start=True, stop=True,
                             tile_position=(64, 0), skip_group_check=True)
                TE.do(fbd1)
                tv = TE.tick()
                SE.wait(TE, tv)
                def fbst(s, tb=tb):
                    for half in range(2):
                        ch = 2 * tb + half
                        rows = slice(half * 64, half * 64 + 64)
                        s.copy(bdt[0:64, ch:ch + 1], pv[rows, 0:1])
                        s.copy(bdt[64:128, ch:ch + 1], pv[rows, 2:3])
                        s.copy(gst[0:64, ch:ch + 1], pv[rows, 1:2])
                        s.copy(gst[64:128, ch:ch + 1], pv[rows, 3:4])
                SE.do(fbst)
                sv = SE.tick()
                TE.wait(SE, sv)
                lastuseB[id(pv)] = (SE, sv)
            VE.wait(SE, SE.n)
            def fbeta(v):
                v.tensor_tensor(bcol[:], bbt[:], bdt[:], Alu.add)
            VE.do(fbeta)
            vv_bc = VE.tick()
            SE.wait(VE, vv_bc)
            SE.do(lambda s: s.activation(bcol[:], bcol[:], Act.Sigmoid))
            sv_bs = SE.tick()
            VE.wait(SE, sv_bs)
            def fbfin(v):
                v.tensor_scalar(gst[:], gst[:], 0.5, None, Alu.is_gt)
                v.drain()
                v.tensor_tensor(bcol[:], bcol[:], gst[:], Alu.mult)
                v.drain()
                v.tensor_scalar(nbcol[:], bcol[:], -1.0, None, Alu.mult)
            VE.do(fbfin)
            vv_bf = VE.tick()

            SE.wait(VE, vv_L)
            def fexp(s):
                for ch in range(CPG):
                    sl = slice(ch * CH, (ch + 1) * CH)
                    mid = ch * CH + CH // 2 - 1
                    end = ch * CH + CH - 1
                    s.activation(E1g[:, sl], Lg[:, sl], Act.Exp)
                    s.activation(E3g[:, sl], Lg[:, sl], Act.Exp, scale=-1.0,
                                 bias=Lg[:, mid:mid + 1])
                    s.activation(E4g[:, sl], Lg[:, sl], Act.Exp, scale=-1.0,
                                 bias=Lg[:, end:end + 1])
                    s.activation(ecol[:, 2 * ch:2 * ch + 1], Lg[:, mid:mid + 1],
                                 Act.Exp, scale=-1.0)
                    s.activation(ecol[:, 2 * ch + 1:2 * ch + 2], Lg[:, end:end + 1],
                                 Act.Exp)
            SE.do(fexp)
            sv_exp = SE.tick()
            VE.wait(SE, sv_exp)
            def fsc(v):
                v.tensor_tensor(ktb[:], kn[:], E1g[:], Alu.mult)
                v.tensor_tensor(qtb[:], qn[:], E1g[:], Alu.mult)
                v.tensor_tensor(khpb[:], kn[:], E3g[:], Alu.mult)
                v.tensor_tensor(kbarb[:], kn[:], E4g[:], Alu.mult)
            VE.do(fsc)
            gv_sc = VE.tick()
            SE.wait(VE, gv_sc)
            def fscp(s):
                for ch in range(CPG):
                    sl = slice(ch * CH, (ch + 1) * CH)
                    s.activation(ktpb[:, sl], ktb[:, sl], Act.Identity,
                                 scale=ecol[:, 2 * ch:2 * ch + 1])
                    s.activation(qtpb[:, sl], qtb[:, sl], Act.Identity,
                                 scale=ecol[:, 2 * ch:2 * ch + 1])
            SE.do(fscp)
            sv_scp = SE.tick()

            def emit_front(ch):
                sl = slice(ch * CH, (ch + 1) * CH)
                par = ch % 2
                Ab = AbarP[par][:]
                Ao = AorP[par][:]
                wmb_c = wmbP[par][:]
                kbtm_c = kbtmP[par][:]
                TE.wait(SE, sv_scp)
                TE.wait(VE, vv_bf)
                bank_waitB(TE, pbc)
                def fgt(t, sl=sl):
                    t.matmul(pbc[0:64, 0:64], khpb[0:64, sl], ktpb[0:64, sl],
                             start=True, stop=True, skip_group_check=True)
                    t.matmul(pbc[64:128, 0:64], khpb[64:128, sl],
                             ktpb[64:128, sl], start=True, stop=True,
                             tile_position=(64, 64), skip_group_check=True)
                TE.do(fgt)
                tv = TE.tick()
                SE.wait(TE, tv)
                bank_waitB(SE, gb_)
                def fgb(s, ch=ch):
                    s.activation(gb_[:], pbc[:, 0:64], Act.Identity,
                                 scale=nbcol[:, ch:ch + 1])
                SE.do(fgb)
                sv = SE.tick()
                lastuseB[id(pbc)] = (SE, sv)
                VE.wait(SE, sv)
                def fsel(v, Ab=Ab):
                    v.select(Ab, mUS[:], gb_[:], zt64[:])
                VE.do(fsel)
                vv = VE.tick()
                lastuseB[id(gb_)] = (VE, vv)
                TE.wait(VE, vv)
                bank_waitB(TE, ptr)
                def ftr(t, Ab=Ab):
                    t.transpose(ptr[0:64, 0:64], Ab[0:64, :], ids[0:64, 0:64])
                    t.transpose(ptr[64:128, 64:128], Ab[64:128, :],
                                ids[64:128, 64:128], tile_position=(64, 64))
                TE.do(ftr)
                tv = TE.tick()
                SE.wait(TE, tv)
                def fevtr(s, Ao=Ao):
                    s.copy(Ao[0:64, :], ptr[0:64, 0:64])
                    s.copy(Ao[64:128, :], ptr[64:128, 64:128])
                SE.do(fevtr)
                sv_ao = SE.tick()
                lastuseB[id(ptr)] = (SE, sv_ao)

            def emit_front2(ch):
                sl = slice(ch * CH, (ch + 1) * CH)
                par = ch % 2
                wmb_c = wmbP[par][:]
                kbtm_c = kbtmP[par][:]
                bank_waitB(TE, pv)
                def fw(t, sl=sl):
                    t.matmul(pv[0:64, 0:64], khpb[0:64, sl], qtpb[0:64, sl],
                             start=True, stop=True, skip_group_check=True)
                    t.matmul(pv[64:128, 0:64], khpb[64:128, sl],
                             qtpb[64:128, sl], start=True, stop=True,
                             tile_position=(64, 64), skip_group_check=True)
                TE.do(fw)
                tv = TE.tick()
                SE.wait(TE, tv)
                bank_waitB(SE, wvt)
                SE.do(lambda s: s.activation(wvt[:], pv[:, 0:64], Act.Identity))
                sv = SE.tick()
                lastuseB[id(pv)] = (SE, sv)
                VE.wait(SE, sv)
                def fselw(v, wmb_c=wmb_c):
                    v.select(wmb_c, mUI[:], wvt[:], zt64[:])
                VE.do(fselw)
                vv_w = VE.tick()
                lastuseB[id(wvt)] = (VE, vv_w)
                bank_waitB(TE, ptr)
                def fktr(t, sl=sl):
                    t.transpose(ptr[0:64, 0:64], kbarb[0:64, sl], ids[0:64, 0:64])
                    t.transpose(ptr[64:128, 64:128], kbarb[64:128, sl],
                                ids[64:128, 64:128], tile_position=(64, 64))
                TE.do(fktr)
                tv = TE.tick()
                SE.wait(TE, tv)
                def fekt(s, kbtm_c=kbtm_c):
                    s.copy(kbtm_c[0:64, :], ptr[0:64, 0:64])
                    s.copy(kbtm_c[64:128, :], ptr[64:128, 64:128])
                SE.do(fekt)
                sv_kt = SE.tick()
                lastuseB[id(ptr)] = (SE, sv_kt)

            def emit_back(ch):
                sl = slice(ch * CH, (ch + 1) * CH)
                par = ch % 2
                Ab = AbarP[par][:]
                Ao = AorP[par][:]
                wmb_c = wmbP[par][:]
                kbtm_c = kbtmP[par][:]
                def fssc(s, ch=ch):
                    s.activation(sscb[:], Sst[:], Act.Identity,
                                 scale=ecol[:, 2 * ch + 1:2 * ch + 2])
                SE.do(fssc)
                sv_ssc = SE.tick()
                bank_waitB(TE, pb1)
                def fr(t, sl=sl):
                    t.matmul(pb1[0:64, :], ktb[0:64, sl], Sst[0:64, :],
                             start=True, stop=True, skip_group_check=True)
                    t.matmul(pb1[64:128, :], ktb[64:128, sl], Sst[64:128, :],
                             start=True, stop=True, tile_position=(64, 64),
                             skip_group_check=True)
                TE.do(fr)
                tv = TE.tick()
                SE.wait(TE, tv)
                SE.do(lambda s: s.activation(sqy[:], pb1[:], Act.Identity,
                                             scale=-1.0))
                sv = SE.tick()
                lastuseB[id(pb1)] = (SE, sv)
                VE.wait(SE, sv)
                def frr(v, ch=ch):
                    v.tensor_tensor(Rb[:], vts[:, ch * 64:(ch + 1) * 64],
                                    sqy[:], Alu.add)
                VE.do(frr)
                gv_r = VE.tick()
                TE.wait(VE, gv_r)
                bank_waitB(TE, pY)
                def fy0(t):
                    t.matmul(pY[0:64, 0:64], ids[0:64, 0:64], Rb[0:64, :],
                             start=True, stop=False, skip_group_check=True)
                    t.matmul(pY[64:128, 0:64], ids[64:128, 64:128], Rb[64:128, :],
                             start=True, stop=False, tile_position=(64, 64),
                             skip_group_check=True)
                TE.do(fy0)
                apA, apO = Ab, Ao
                spare_b = Atmpb[:]
                spare_o = Atmpo[:]
                cur_y = Rb
                for lvl in range(6):
                    def fap2(t, a=apA, y=cur_y, last=(lvl == 5)):
                        t.matmul(pY[0:64, 0:64], a[0:64, :], y[0:64, :],
                                 start=False, stop=last, skip_group_check=True)
                        t.matmul(pY[64:128, 0:64], a[64:128, :], y[64:128, :],
                                 start=False, stop=last, tile_position=(64, 64),
                                 skip_group_check=True)
                    TE.do(fap2)
                    tv = TE.tick()
                    SE.wait(TE, tv)
                    if lvl == 5:
                        def fey(s, ch=ch):
                            s.activation(Ub[:], pY[:, 0:64], Act.Identity,
                                         scale=bcol[:, ch:ch + 1])
                        SE.do(fey)
                    else:
                        SE.do(lambda s: s.activation(Yb[:], pY[:, 0:64],
                                                     Act.Identity))
                    sv_y = SE.tick()
                    TE.wait(SE, sv_y)
                    cur_y = Yb
                    if lvl == 5:
                        lastuseB[id(pY)] = (SE, sv_y)
                    if lvl < 5:
                        bank_waitB(TE, pb2)
                        def fsq(t, ab=apA, ao=apO):
                            t.matmul(pb2[0:64, :], ao[0:64, :], ab[0:64, :],
                                     start=True, stop=True, skip_group_check=True)
                            t.matmul(pb2[64:128, :], ao[64:128, :], ab[64:128, :],
                                     start=True, stop=True, tile_position=(64, 64),
                                     skip_group_check=True)
                        TE.do(fsq)
                        if lvl < 4:
                            bank_waitB(TE, pb1)
                            def fsq2(t, ab=apA, ao=apO):
                                t.matmul(pb1[0:64, :], ab[0:64, :], ao[0:64, :],
                                         start=True, stop=True,
                                         skip_group_check=True)
                                t.matmul(pb1[64:128, :], ab[64:128, :],
                                         ao[64:128, :], start=True, stop=True,
                                         tile_position=(64, 64))
                            TE.do(fsq2)
                        tv2 = TE.tick()
                        SE.wait(TE, tv2)
                        def fesq(s, nb=spare_b, no=spare_o, last=(lvl == 4)):
                            s.copy(nb, pb2[:])
                            if not last:
                                s.copy(no, pb1[:])
                        SE.do(fesq)
                        sv_sq = SE.tick()
                        lastuseB[id(pb2)] = (SE, sv_sq)
                        if lvl < 4:
                            lastuseB[id(pb1)] = (SE, sv_sq)
                        TE.wait(SE, sv_sq)
                        apA, apO = spare_b, spare_o
                        spare_b, spare_o = (Ab, Ao)
                bank_waitB(TE, pSO)
                def fo1(t, sl=sl):
                    t.matmul(pSO[0:64, 64:128], qtb[0:64, sl], Sst[0:64, :],
                             start=True, stop=False, skip_group_check=True)
                    t.matmul(pSO[64:128, 64:128], qtb[64:128, sl], Sst[64:128, :],
                             start=True, stop=False, tile_position=(64, 64),
                             skip_group_check=True)
                TE.do(fo1)
                TE.wait(SE, sv_ssc)
                def fo2(t, wmb_c=wmb_c):
                    t.matmul(pSO[0:64, 64:128], wmb_c[0:64, :], Ub[0:64, :],
                             start=False, stop=True, skip_group_check=True)
                    t.matmul(pSO[64:128, 64:128], wmb_c[64:128, :], Ub[64:128, :],
                             start=False, stop=True, tile_position=(64, 64),
                             skip_group_check=True)
                TE.do(fo2)
                TE.tick()
                def fsu(t, kbtm_c=kbtm_c):
                    t.matmul(pSO[0:64, 0:64], kbtm_c[0:64, :], Ub[0:64, :],
                             start=True, stop=False, skip_group_check=True)
                    t.matmul(pSO[64:128, 0:64], kbtm_c[64:128, :], Ub[64:128, :],
                             start=True, stop=False, tile_position=(64, 64),
                             skip_group_check=True)
                    t.matmul(pSO[0:64, 0:64], ids[0:64, 0:64], sscb[0:64, :],
                             start=False, stop=True, skip_group_check=True)
                    t.matmul(pSO[64:128, 0:64], ids[64:128, 64:128],
                             sscb[64:128, :], start=False, stop=True,
                             tile_position=(64, 64), skip_group_check=True)
                TE.do(fsu)
                tv_o2 = TE.tick()
                SE.wait(TE, tv_o2)
                def fes(s):
                    s.copy(Sst[:], pSO[:, 0:64])
                    s.activation(ytmp[:], pSO[:, 64:128], Act.Identity)
                    s.activation(sqy[:], ytmp[:], Act.Square)
                SE.do(fes)
                sv_es = SE.tick()
                lastuseB[id(pSO)] = (SE, sv_es)
                back_es[ch] = sv_es

            def emit_back2(ch):
                sv_es = back_es[ch]
                VE.wait(SE, sv_es)
                def fyn(v):
                    v.tensor_reduce(styc[:, 0:1], sqy[:], AxX, Alu.add)
                    v.drain()
                    v.tensor_scalar(styc[:, 0:1], styc[:, 0:1], 1.0 / 64, 1e-6,
                                    Alu.mult, Alu.add)
                VE.do(fyn)
                vv_yn = VE.tick()
                SE.wait(VE, vv_yn)
                SE.do(lambda s: s.activation(styc[:, 1:2], styc[:, 0:1], Act.Sqrt))
                sv_sy = SE.tick()
                VE.wait(SE, sv_sy)
                def fry(v):
                    v.reciprocal(styc[:, 2:3], styc[:, 1:2])
                VE.do(fry)
                vv_ry = VE.tick()
                SE.wait(VE, vv_ry)
                def fyscale(s):
                    s.activation(ynm[:], ytmp[:], Act.Identity, scale=styc[:, 2:3])
                SE.do(fyscale)
                sv_ynm = SE.tick()
                VE.wait(SE, sv_ynm)
                def fhw(v):
                    v.tensor_tensor(ytm[:], ynm[:], hnwt[:], Alu.mult)
                VE.do(fhw)
                gv_ytm = VE.tick()
                bank_waitB(TE, ptr)
                TE.wait(VE, gv_ytm)
                def fytr(t):
                    t.transpose(ptr[0:64, 0:64], ytm[0:64, :], ids[0:64, 0:64])
                    t.transpose(ptr[64:128, 64:128], ytm[64:128, :],
                                ids[64:128, 64:128], tile_position=(64, 64))
                TE.do(fytr)
                tv_ytr = TE.tick()
                SE.wait(TE, tv_ytr)
                def fyev(s, ch=ch):
                    s.copy(yfm[0:64, ch * 64:(ch + 1) * 64], ptr[0:64, 0:64])
                    s.copy(yfm[64:128, ch * 64:(ch + 1) * 64],
                           ptr[64:128, 64:128])
                SE.do(fyev)
                sv_yev = SE.tick()
                lastuseB[id(ptr)] = (SE, sv_yev)

            front_done = {}
            back_es = {}
            emit_front(0)
            emit_front2(0)
            for ch in range(1, CPG + 1):
                if ch < CPG:
                    emit_front(ch)
                emit_back(ch - 1)
                if ch < CPG:
                    emit_front2(ch)
                emit_back2(ch - 1)

            sv_fin = SE.n
            vv_fin = VE.n
            if fused:
                if sd_n[0] > 0:
                    SE.wait_sem(ld, sd_n[0])
                def fysel(s):
                    for j in range(4):
                        s.activation(ysc[:, j * GRAN:(j + 1) * GRAN], yfm[:],
                                     Act.Identity, scale=hselc[:, j:j + 1])
                SE.do(fysel)
                sv_ys = SE.tick()
                s_ = gb0 // TSL
                c0 = gb0 % TSL
                for j in range(4):
                    dma(yrs[s_ * 512 + j * 128:s_ * 512 + (j + 1) * 128,
                            c0:c0 + GRAN],
                        ysc[:, j * GRAN:(j + 1) * GRAN],
                        (sks, sv_ys) if j == 0 else None)
                sd_n[0] = dma_n[0]
            else:
                dma(yout[:, gb0:gb0 + GRAN], yfm[:], (sks, sv_fin))
                dma(gout[:, gb0:gb0 + GRAN], gfm[:], (vks, vv_g))

        if dbg:
            def fgfin(g_):
                g_.tensor_scalar(gb_[:], gb_[:], 0.0, None, Alu.add)
                g_.drain()
            GE.do(fgfin)
            gv_fin = GE.tick()
            dbg_list = [
                ("qraw", qraw, [128, GRAN], F32), ("kraw", kraw, [128, GRAN], F32),
                ("qn", qn, [128, GRAN], BF16), ("kn", kn, [128, GRAN], BF16),
                ("dg", dg, [128, GRAN], F32), ("abg", abg, [128, GRAN], F32),
                ("vts", vts, [128, 8 * 64], F32), ("bbt", bbt, [128, 8], F32),
                ("bdt", bdt, [128, 8], F32), ("gst", gst, [128, 8], F32),
                ("bcol", bcol, [128, 8], F32), ("nbcol", nbcol, [128, 8], F32),
                ("gaf", gaf, [128, GRAN], F32), ("argf", argf, [128, GRAN], F32),
                ("laf", laf, [128, GRAN], F32), ("actf", actf, [128, GRAN], F32),
                ("Lg", Lg, [128, GRAN], F32), ("E1g", E1g, [128, GRAN], F32),
                ("E3g", E3g, [128, GRAN], F32), ("E4g", E4g, [128, GRAN], F32),
                ("ktb", ktb, [128, GRAN], BF16), ("ktpb", ktpb, [128, GRAN], BF16),
                ("khpb", khpb, [128, GRAN], BF16), ("qtb", qtb, [128, GRAN], BF16),
                ("qtpb", qtpb, [128, GRAN], BF16), ("kbarb", kbarb, [128, GRAN], BF16),
                ("ecol", ecol, [128, 16], F32), ("Egb", Egb, [128, GRAN], F32),
                ("thg", thg, [128, GRAN], F32), ("Sst", Sst, [128, 64], BF16),
                ("styc", styc[:, 0:3], [128, 3], F32),
                ("hg", hgP[0], [128, 8 * GRAN], BF16),
                ("agb", agbP[0], [64, GRAN], BF16),
            ]
            waits = [(sks, SE.n), (vks, VE.n), (gks, gv_fin)]
            for di, (nm, tn, sh, dt) in enumerate(dbg_list):
                dap = nc.dram_tensor("dbg_" + nm, sh, dt,
                                     kind="ExternalOutput").ap()
                src = tn if isinstance(tn, bass.AP) else tn[:]
                dma(dap[:], src, waits[di] if di < len(waits) else None)
    return es


def finalize_engs(nc, E, blk):
    ld = E.ld
    dmas = E.dmas

    @blk.sync
    def _(sy):
        for wait, fn in dmas:
            if wait is not None:
                sy.wait_ge(wait[0], wait[1])
            fn(sy).then_inc(ld, 16)

    @blk.tensor
    def _(t):
        E.TE.run(t)

    @blk.scalar
    def _(s):
        E.SE.run(s)

    @blk.vector
    def _(v):
        E.VE.run(v)

    @blk.gpsimd
    def _(g_):
        E.GE.run(g_)


def build_B(Ttot=T, dbg=False):
    nc = bass.Bass("TRN2", target_bir_lowering=False, detect_race_conditions=False)
    io = dict(
        hin=nc.dram_tensor("hin", [D + RR, Ttot], BF16, kind="ExternalInput").ap(),
        wq_i=nc.dram_tensor("wq_s", [D, 128], F32, kind="ExternalInput").ap(),
        wk_i=nc.dram_tensor("wk_s", [D, 128], F32, kind="ExternalInput").ap(),
        wv_i=nc.dram_tensor("wv_s", [D, 128], F32, kind="ExternalInput").ap(),
        wsp_i=nc.dram_tensor("wsp_s", [D, 128], F32, kind="ExternalInput").ap(),
        wad_i=nc.dram_tensor("wad_s", [RR, 128], F32, kind="ExternalInput").ap(),
        bad_i=nc.dram_tensor("bad_s", [128, 1], F32, kind="ExternalInput").ap(),
        wbe_i=nc.dram_tensor("wbe_s", [D, 2], F32, kind="ExternalInput").ap(),
        bbe_i=nc.dram_tensor("bbe_s", [128, 2], F32, kind="ExternalInput").ap(),
        asp_i=nc.dram_tensor("asp_c", [128, 1], F32, kind="ExternalInput").ap(),
        bsp_i=nc.dram_tensor("bsp_c", [128, 2], F32, kind="ExternalInput").ap(),
        hnw_i=nc.dram_tensor("hnw_t", [128, 64], F32, kind="ExternalInput").ap(),
        oexp_i=nc.dram_tensor("oexp", [128, NSHIFT * 128], F32, kind="ExternalInput").ap(),
        wts_i=nc.dram_tensor("wts", [128, NPOS * SBLK], F32, kind="ExternalInput").ap(),
        invc_i=nc.dram_tensor("invc512", [128, GRAN], F32, kind="ExternalInput").ap(),
        ipow_i=nc.dram_tensor("ipow512", [128, GRAN], F32, kind="ExternalInput").ap(),
        hpow_i=nc.dram_tensor("hpow512", [128, GRAN], F32, kind="ExternalInput").ap(),
        ident_i=nc.dram_tensor("ident", [128, 128], BF16, kind="ExternalInput").ap(),
        mau_i=nc.dram_tensor("maskUS", [128, 64], U8, kind="ExternalInput").ap(),
        mui_i=nc.dram_tensor("maskUI", [128, 64], U8, kind="ExternalInput").ap(),
        yout=nc.dram_tensor("yout", [128, Ttot], BF16, kind="ExternalOutput").ap(),
        gout=nc.dram_tensor("gout", [128, Ttot], BF16, kind="ExternalOutput").ap(),
    )
    es0 = ExitStack()
    E = ECtx(nc, es0)
    es_b = emit_B(nc, E, io, Ttot=Ttot, dbg=dbg, fused=False)
    E.GE.wait_sem(E.ld, E.dma_n[0])
    blk = es0.enter_context(nc.Block())
    with es0, es_b:
        finalize_engs(nc, E, blk)
    return nc


def host_masks():
    tri = np.triu(np.ones((64, 64), np.uint8), 1)
    trii = np.triu(np.ones((64, 64), np.uint8), 0)
    mus = np.concatenate([tri, tri], axis=0)
    mui = np.concatenate([trii, trii], axis=0)
    return mus, mui


def stage_B_inputs(inputs, A_out, Ttot=T):
    maps = []
    mus, mui = host_masks()
    oexp = host_shift_exp()
    wtsh = host_win_weights()
    invc = host_invc()
    ipow = host_ipow()
    hpow = host_hpow()
    ident = np.eye(128, dtype=bfd)
    nsl = Ttot // TSL if Ttot >= TSL else 1
    for c in range(8):
        b, hp = c // 4, c % 4
        h0, h1 = 2 * hp, 2 * hp + 1
        cols = slice(h0 * 64, h0 * 64 + 128)
        if Ttot == T:
            hin = np.concatenate([A_out[b * 4 + s]["h_au"] for s in range(4)],
                                 axis=1)
        else:
            hin = A_out[b * 4]["h_au"][:, :Ttot]
        wq_s = np.ascontiguousarray(inputs["L0_attn_Wq"][:, cols], np.float32)
        wk_s = np.ascontiguousarray(inputs["L0_attn_Wk"][:, cols], np.float32)
        wv_s = np.ascontiguousarray(inputs["L0_attn_Wv"][:, cols], np.float32)
        wsp_s = np.ascontiguousarray(inputs["L0_attn_Wspike"][:, cols], np.float32)
        wad_s = np.ascontiguousarray(inputs["L0_attn_Wad"][:, cols], np.float32)
        bad_s = np.ascontiguousarray(inputs["L0_attn_bad"][cols], np.float32)[:, None]
        wbe_s = np.ascontiguousarray(inputs["L0_attn_Wbeta"][:, h0:h0 + 2], np.float32)
        bbe = np.zeros((128, 2), np.float32)
        bbe[:, 0] = inputs["L0_attn_bbeta"][h0]
        bbe[:, 1] = inputs["L0_attn_bbeta"][h1]
        asp = np.concatenate([inputs["L0_attn_alpha_spike"][0, h0],
                              inputs["L0_attn_alpha_spike"][0, h1]]).astype(np.float32)[:, None]
        bsp = np.zeros((128, 2), np.float32)
        bsp[0:64, 0] = inputs["L0_attn_beta_spike"][0, h0]
        bsp[64:128, 0] = inputs["L0_attn_beta_spike"][0, h1]
        bsp[:, 1] = 1.0
        hnw = np.zeros((128, 64), np.float32)
        hnw[0:64] = inputs["L0_attn_head_norm_w"][h0][None, :]
        hnw[64:128] = inputs["L0_attn_head_norm_w"][h1][None, :]
        maps.append(dict(hin=hin, wq_s=wq_s, wk_s=wk_s, wv_s=wv_s, wsp_s=wsp_s,
                         wad_s=wad_s, bad_s=bad_s, wbe_s=wbe_s, bbe_s=bbe,
                         asp_c=asp, bsp_c=bsp, hnw_t=hnw, oexp=oexp, wts=wtsh,
                         invc512=invc, ipow512=ipow, hpow512=hpow, ident=ident,
                         maskUS=mus, maskUI=mui))
    return maps



def emit_C(nc, E, io, fused=False):
    xn_i, l2_i, y_i, xs_i = io["xn_i"], io["l2_i"], io["y_i"], io["xs_i"]
    wu1_i, bu1_i, wu2_i, bu2_i = io["wu1_i"], io["bu1_i"], io["wu2_i"], io["bu2_i"]
    wo_i, fnw_i = io["wo_i"], io["fnw_i"]
    wf1_i, wf3_i, wf2_i = io["wf1_i"], io["wf3_i"], io["wf2_i"]
    q1_i, s1_i, b1_i = io["q1_i"], io["s1_i"], io["b1_i"]
    q2_i, s2_i, b2c_i = io["q2_i"], io["s2_i"], io["b2c_i"]
    idf_i, out_o = io["idf_i"], io["out_o"]

    SLAB = 512
    NSLAB = DFF // SLAB
    MTS = SLAB // 128
    TH2 = TSL // 2
    NB2 = TH2 // 512
    es = ExitStack()
    def _sb(n, sh, dt):
        return es.enter_context(nc.sbuf_tensor(n, sh, dt))
    def _ps(n, sh, dt):
        return es.enter_context(nc.psum_tensor(n, sh, dt))
    def _sem(n):
        return es.enter_context(nc.semaphore(n))
    xnb = _sb("xnb", [128, 8 * TH2], BF16)
    l2b = _sb("l2b", [128, 8 * TH2], BF16)
    zb = _sb("zb", [128, 8 * TH2], BF16)
    accb = _sb("accb", [128, 8 * TH2], F32)
    g1b = _sb("g1b", [64, TH2], BF16)
    gatet = _sb("gatet", [128, 512], BF16)
    wu1s = _sb("wu1s", [128, 8 * RR], BF16)
    wu2s = _sb("wu2s", [64, D], BF16)
    wos = _sb("wos", [128, 4 * D], BF16)
    bu1c = _sb("bu1c", [64, 1], F32)
    bu2c = _sb("bu2c", [128, 8], F32)
    fnwc = _sb("fnwc", [128, 8], F32)
    b1c = _sb("b1c", [128, 32], F32)
    b2cc = _sb("b2cc", [128, 8], F32)
    idf = _sb("idf_s", [128, 128], F32)
    wslabA_P = [_sb("wslabA0", [128, 8 * SLAB], BF16),
                _sb("wslabA1", [128, 8 * SLAB], BF16)]
    wslabB_P = [_sb("wslabB0", [128, 8 * SLAB], BF16),
                _sb("wslabB1", [128, 8 * SLAB], BF16)]
    w2slab_P = [_sb("w2slab0", [128, MTS * D], BF16),
                _sb("w2slab1", [128, MTS * D], BF16)]
    qslab_P = [_sb("qslab0", [128, 8 * SLAB], BF16),
               _sb("qslab1", [128, 8 * SLAB], BF16)]
    sslab_P = [_sb("sslab0", [128, 8 * SLAB], BF16),
               _sb("sslab1", [128, 8 * SLAB], BF16)]
    uslab = _sb("uslab", [128, MTS * TH2], BF16)
    xtile = _sb("xtile", [128, D], F32)
    sq_ = _sb("sqc", [128, 1536], F32)
    onesf = _sb("onesfC", [128, 128], F32)
    pm = _ps("pm", [128, 512], F32)
    pmB = _ps("pmB", [128, 512], F32)
    pm2 = _ps("pm2", [128, 512], F32)
    pm2B = _ps("pm2B", [128, 512], F32)
    pbc = _ps("pbcC", [128, 512], F32)
    ptrp = _ps("ptrp", [128, 128], F32)
    ptrpB = _ps("ptrpB", [128, 128], F32)
    gatetB = _sb("gatetB", [128, 512], BF16)

    if True:
        TE, SE, VE, GE = E.TE, E.SE, E.VE, E.GE
        ld, ldg = E.ld, E.ldg
        tks, sks, vks, gks = E.tks, E.sks, E.vks, E.gks
        dma, dma_n, dmas = E.dma, E.dma_n, E.dmas
        gdma_n = E.gdma_n

        def gdma(dst, src, wait=None):
            if wait is not None:
                GE.wait_sem(wait[0], wait[1])
            GE.do(lambda g_, d=dst, s=src: g_.dma_start(out=d, in_=s).then_inc(ldg, 16))
            gdma_n[0] += 16
            return gdma_n[0]

        cl = [(bu1c[:], bu1_i[:]), (bu2c[:], bu2_i[:]), (fnwc[:], fnw_i[:]),
              (b1c[:], b1_i[:]), (b2cc[:], b2c_i[:]), (idf[:], idf_i[:])]
        for ci, (dst, src) in enumerate(cl):
            dma(dst, src, io.get("const_wait") if ci == 0 else None)
        if fused:
            cw_ = io["const_wait"]
            VE.wait_sem(cw_[0], cw_[1])
            SE.wait_sem(cw_[0], cw_[1])
            TE.wait_sem(cw_[0], cw_[1])
        VE.do(lambda v: v.memset(onesf[:], 1.0))
        vv_ones = VE.tick()
        TE.wait(VE, vv_ones)
        for kc in range(8):
            gdma(wu1s[:, kc * RR:(kc + 1) * RR], wu1_i[kc * 128:(kc + 1) * 128, :])
        gdma(wu2s[:], wu2_i[:])
        for kc in range(4):
            gdma(wos[:, kc * D:(kc + 1) * D], wo_i[kc * 128:(kc + 1) * 128, :])
        SE.wait_sem(ld, dma_n[0])
        SE.do(lambda s: s.activation(b2cc[:], b2cc[:], Act.Identity, scale=0.5))
        sv_b2h = SE.tick()

        lastuse = {}
        sfree = {}

        def bank_wait(eng, bank):
            lu = lastuse.get(id(bank))
            if lu is not None:
                eng.wait(lu[0], lu[1])

        for half in range(2):
            hb = half * TH2
            wv_h = TE.n
            vv_h = VE.n
            for kc in range(8):
                dma(xnb[:, kc * TH2:(kc + 1) * TH2],
                    xn_i[kc * 128:(kc + 1) * 128, hb:hb + TH2], (tks, wv_h))
                dma(l2b[:, kc * TH2:(kc + 1) * TH2],
                    l2_i[kc * 128:(kc + 1) * 128, hb:hb + TH2], (tks, wv_h))
            for kc in range(4):
                dma(zb[:, kc * TH2:(kc + 1) * TH2],
                    y_i[kc * 128:(kc + 1) * 128, hb:hb + TH2], (tks, wv_h))
            TE.wait_sem(ld, dma_n[0])
            TE.wait_sem(ldg, gdma_n[0])
            TE.wait(VE, vv_h)

            for nb in range(NB2):
                def fg1(t, nb=nb):
                    for kc in range(8):
                        t.matmul(pm[0:64, 0:512], wu1s[:, kc * RR:(kc + 1) * RR],
                                 xnb[:, kc * TH2 + nb * 512:kc * TH2 + (nb + 1) * 512],
                                 start=(kc == 0), stop=(kc == 7), skip_group_check=True)
                TE.do(fg1)
                tv = TE.tick()
                SE.wait(TE, tv)
                def fs1(s, nb=nb):
                    s.activation(g1b[:, nb * 512:(nb + 1) * 512], pm[0:64, 0:512],
                                 Act.Silu, bias=bu1c[:], scale=1.0)
                SE.do(fs1)
                sv = SE.tick()
                TE.wait(SE, sv)

            it_g = 0
            for mc in range(8):
                for nb in range(NB2):
                    bnk = it_g % 2
                    pg = pm2 if bnk == 0 else pm2B
                    pw = pm if bnk == 0 else pmB
                    gt = gatet if bnk == 0 else gatetB
                    bank_wait(TE, pg)
                    def fg2(t, mc=mc, nb=nb, pg=pg):
                        t.matmul(pg[:, 0:512], wu2s[:, mc * 128:(mc + 1) * 128],
                                 g1b[:, nb * 512:(nb + 1) * 512],
                                 start=True, stop=True, skip_group_check=True)
                    TE.do(fg2)
                    tv = TE.tick()
                    SE.wait(TE, tv)
                    bank_wait(SE, gt)
                    def fs2(s, mc=mc, pg=pg, gt=gt):
                        s.activation(gt[:], pg[:, 0:512], Act.Sigmoid,
                                     bias=bu2c[:, mc:mc + 1], scale=1.0)
                    SE.do(fs2)
                    sv = SE.tick()
                    lastuse[id(pg)] = (SE, sv)
                    bank_wait(TE, pw)
                    def fwo(t, mc=mc, nb=nb, pw=pw):
                        for kc in range(4):
                            t.matmul(pw[:, 0:512],
                                     wos[:, kc * D + mc * 128:kc * D + (mc + 1) * 128],
                                     zb[:, kc * TH2 + nb * 512:kc * TH2 + (nb + 1) * 512],
                                     start=(kc == 0), stop=(kc == 3),
                                     skip_group_check=True)
                    TE.do(fwo)
                    tv = TE.tick()
                    VE.wait(TE, tv)
                    VE.wait(SE, sv)
                    def fya(v, mc=mc, nb=nb, pw=pw, gt=gt):
                        sl = slice(mc * TH2 + nb * 512, mc * TH2 + (nb + 1) * 512)
                        v.tensor_tensor(accb[:, sl], pw[:, 0:512], gt[:], Alu.mult)
                        v.drain()
                        v.scalar_tensor_tensor(accb[:, sl], xnb[:, sl], 2.0,
                                               accb[:, sl], Alu.mult, Alu.add)
                    VE.do(fya)
                    vv = VE.tick()
                    lastuse[id(pw)] = (VE, vv)
                    lastuse[id(gt)] = (VE, vv)
                    it_g += 1

            for nb in range(NB2):
                VE.wait(SE, SE.n)
                for kc in range(8):
                    def fyasq(v, kc=kc, nb=nb):
                        sl = slice(kc * TH2 + nb * 512, kc * TH2 + (nb + 1) * 512)
                        v.tensor_tensor(sq_[:, 0:512], accb[:, sl], xnb[:, sl],
                                        Alu.subtract)
                    VE.do(fyasq)
                    vv = VE.tick()
                    SE.wait(VE, vv)
                    SE.do(lambda s: s.activation(sq_[:, 512:1024], sq_[:, 0:512],
                                                 Act.Square))
                    sv = SE.tick()
                    TE.wait(SE, sv)
                    def fsum(t, kc=kc):
                        t.matmul(pbc[0:1, 0:512], onesf[:, 0:1], sq_[:, 512:1024],
                                 start=(kc == 0), stop=(kc == 7),
                                 skip_group_check=True)
                    TE.do(fsum)
                    tv = TE.tick()
                    VE.wait(TE, tv)
                SE.wait(TE, TE.n)
                SE.do(lambda s: s.copy(sq_[0:1, 512:1024], pbc[0:1, 0:512]))
                sv_cp = SE.tick()
                VE.wait(SE, sv_cp)
                VE.do(lambda v: v.tensor_scalar(sq_[0:1, 512:1024],
                                                sq_[0:1, 512:1024], 1.0 / D, 1e-6,
                                                Alu.mult, Alu.add))
                vv_eps = VE.tick()
                SE.wait(VE, vv_eps)
                SE.do(lambda s: s.activation(sq_[0:1, 0:512], sq_[0:1, 512:1024],
                                             Act.Sqrt))
                sv_sq = SE.tick()
                TE.wait(SE, sv_sq)
                def fbc(t):
                    t.matmul(pbc[:, 0:512], onesf[0:1, 0:128], sq_[0:1, 0:512],
                             start=True, stop=True, skip_group_check=True)
                TE.do(fbc)
                tv_bc = TE.tick()
                SE.wait(TE, tv_bc)
                SE.do(lambda s: s.copy(sq_[:, 1024:1536], pbc[:, 0:512]))
                gv_z0 = SE.tick()
                VE.wait(SE, gv_z0)
                VE.do(lambda v: v.reciprocal(sq_[:, 1024:1536], sq_[:, 1024:1536]))
                sv_rz = VE.tick()
                GE.wait(VE, sv_rz)
                TE.wait(VE, sv_rz)
                for kc in range(8):
                    GE.wait(SE, SE.n)
                    def fz(g_, kc=kc, nb=nb):
                        sl = slice(kc * TH2 + nb * 512, kc * TH2 + (nb + 1) * 512)
                        g_.tensor_tensor(sq_[:, 0:512], accb[:, sl], xnb[:, sl],
                                         Alu.subtract)
                        g_.drain()
                        g_.tensor_tensor(sq_[:, 0:512], sq_[:, 0:512],
                                         sq_[:, 1024:1536], Alu.mult)
                        g_.drain()
                    GE.do(fz)
                    gv = GE.tick()
                    SE.wait(GE, gv)
                    def fzw(s, kc=kc, nb=nb):
                        sl = slice(kc * TH2 + nb * 512, kc * TH2 + (nb + 1) * 512)
                        s.activation(zb[:, sl], sq_[:, 0:512], Act.Identity,
                                     scale=fnwc[:, kc:kc + 1])
                    SE.do(fzw)
                    sv = SE.tick()
                    GE.wait(SE, sv)

            TE.wait(SE, SE.n)
            for sb_ in range(NSLAB):
                par = sb_ % 2
                wslabA, wslabB, w2slab = (wslabA_P[par], wslabB_P[par],
                                          w2slab_P[par])
                for kc in range(8):
                    gdma(wslabA[:, kc * SLAB:(kc + 1) * SLAB],
                         wf1_i[kc * 128:(kc + 1) * 128, sb_ * SLAB:(sb_ + 1) * SLAB],
                         sfree.get(id(wslabA)))
                    gdma(wslabB[:, kc * SLAB:(kc + 1) * SLAB],
                         wf3_i[kc * 128:(kc + 1) * 128, sb_ * SLAB:(sb_ + 1) * SLAB],
                         sfree.get(id(wslabB)))
                for kc in range(MTS):
                    gdma(w2slab[:, kc * D:(kc + 1) * D],
                         wf2_i[sb_ * SLAB + kc * 128:sb_ * SLAB + (kc + 1) * 128, :],
                         sfree.get(id(w2slab)))
                TE.wait_sem(ldg, gdma_n[0])
                it_f = 0
                for mt in range(MTS):
                    for nb in range(NB2):
                        p1 = pm if it_f % 2 == 0 else pmB
                        p3 = pm2 if it_f % 2 == 0 else pm2B
                        bank_wait(TE, p1)
                        def ff1(t, mt=mt, nb=nb, p1=p1, wslabA=wslabA):
                            for kc in range(8):
                                t.matmul(p1[:, 0:512],
                                         wslabA[:, kc * SLAB + mt * 128:kc * SLAB + (mt + 1) * 128],
                                         zb[:, kc * TH2 + nb * 512:kc * TH2 + (nb + 1) * 512],
                                         start=(kc == 0), stop=(kc == 7),
                                         skip_group_check=True)
                        TE.do(ff1)
                        tv = TE.tick()
                        SE.wait(TE, tv)
                        def fsl(s, mt=mt, nb=nb, p1=p1):
                            s.activation(uslab[:, mt * TH2 + nb * 512:mt * TH2 + (nb + 1) * 512],
                                         p1[:, 0:512], Act.Silu)
                        SE.do(fsl)
                        sv = SE.tick()
                        lastuse[id(p1)] = (SE, sv)
                        bank_wait(TE, p3)
                        def ff3(t, mt=mt, nb=nb, p3=p3, wslabB=wslabB):
                            for kc in range(8):
                                t.matmul(p3[:, 0:512],
                                         wslabB[:, kc * SLAB + mt * 128:kc * SLAB + (mt + 1) * 128],
                                         zb[:, kc * TH2 + nb * 512:kc * TH2 + (nb + 1) * 512],
                                         start=(kc == 0), stop=(kc == 7),
                                         skip_group_check=True)
                        TE.do(ff3)
                        tv3 = TE.tick()
                        VE.wait(TE, tv3)
                        VE.wait(SE, sv)
                        def fmu(v, mt=mt, nb=nb, p3=p3):
                            sl = slice(mt * TH2 + nb * 512, mt * TH2 + (nb + 1) * 512)
                            v.tensor_tensor(uslab[:, sl], p3[:, 0:512], uslab[:, sl],
                                            Alu.mult)
                        VE.do(fmu)
                        vv = VE.tick()
                        lastuse[id(p3)] = (VE, vv)
                        it_f += 1
                TE.wait(VE, VE.n)
                it_f = 0
                for mc in range(8):
                    for nb in range(NB2):
                        p2 = pm if it_f % 2 == 0 else pmB
                        bank_wait(TE, p2)
                        def ff2(t, mc=mc, nb=nb, p2=p2, w2slab=w2slab):
                            for kc in range(MTS):
                                t.matmul(p2[:, 0:512],
                                         w2slab[:, kc * D + mc * 128:kc * D + (mc + 1) * 128],
                                         uslab[:, kc * TH2 + nb * 512:kc * TH2 + (nb + 1) * 512],
                                         start=(kc == 0), stop=(kc == MTS - 1),
                                         skip_group_check=True)
                        TE.do(ff2)
                        tv = TE.tick()
                        VE.wait(TE, tv)
                        def facc(v, mc=mc, nb=nb, p2=p2):
                            sl = slice(mc * TH2 + nb * 512, mc * TH2 + (nb + 1) * 512)
                            v.tensor_tensor(accb[:, sl], accb[:, sl], p2[:, 0:512],
                                            Alu.add)
                        VE.do(facc)
                        vv = VE.tick()
                        lastuse[id(p2)] = (VE, vv)
                        it_f += 1
                sfree[id(wslabA)] = (tks, TE.n)
                sfree[id(wslabB)] = (tks, TE.n)
                sfree[id(w2slab)] = (tks, TE.n)

            for sb_ in range(NSLAB):
                par = sb_ % 2
                wslabA, w2slab = wslabA_P[par], w2slab_P[par]
                qslab, sslab = qslab_P[par], sslab_P[par]
                for kc in range(8):
                    gdma(qslab[:, kc * SLAB:(kc + 1) * SLAB],
                         q1_i[kc * 128:(kc + 1) * 128, sb_ * SLAB:(sb_ + 1) * SLAB],
                         sfree.get(id(qslab)))
                    gdma(sslab[:, kc * SLAB:(kc + 1) * SLAB],
                         s1_i[kc * 128:(kc + 1) * 128, sb_ * SLAB:(sb_ + 1) * SLAB],
                         sfree.get(id(sslab)))
                VE.wait_sem(ldg, gdma_n[0])
                wfa = sfree.get(id(wslabA))
                if wfa is not None:
                    VE.wait_sem(wfa[0], wfa[1])
                def fdq(v, wslabA=wslabA, qslab=qslab, sslab=sslab):
                    v.tensor_scalar(wslabA[:], qslab[:], -128.0, None, Alu.add)
                    v.drain()
                    v.tensor_tensor(wslabA[:], wslabA[:], sslab[:], Alu.mult)
                VE.do(fdq)
                vv_dq = VE.tick()
                TE.wait(VE, vv_dq)
                for kc in range(MTS):
                    gdma(qslab[:, kc * D:(kc + 1) * D],
                         q2_i[sb_ * SLAB + kc * 128:sb_ * SLAB + (kc + 1) * 128, :],
                         (vks, vv_dq))
                    gdma(sslab[:, kc * D:(kc + 1) * D],
                         s2_i[(sb_ * MTS + kc) * 128:(sb_ * MTS + kc + 1) * 128, :],
                         (vks, vv_dq))
                VE.wait_sem(ldg, gdma_n[0])
                w2f = sfree.get(id(w2slab))
                if w2f is not None:
                    VE.wait_sem(w2f[0], w2f[1])
                def fdq2(v, w2slab=w2slab, qslab=qslab, sslab=sslab):
                    nw2 = MTS * D
                    v.tensor_scalar(w2slab[:, 0:nw2], qslab[:, 0:nw2], -128.0,
                                    None, Alu.add)
                    v.drain()
                    v.tensor_tensor(w2slab[:, 0:nw2], w2slab[:, 0:nw2],
                                    sslab[:, 0:nw2], Alu.mult)
                VE.do(fdq2)
                vv_dq2 = VE.tick()
                TE.wait(VE, vv_dq2)
                sfree[id(qslab)] = (vks, vv_dq2)
                sfree[id(sslab)] = (vks, vv_dq2)
                it_f = 0
                for mt in range(MTS):
                    for nb in range(NB2):
                        p1 = pm if it_f % 2 == 0 else pmB
                        bank_wait(TE, p1)
                        def fq1(t, mt=mt, nb=nb, p1=p1, wslabA=wslabA):
                            for kc in range(8):
                                t.matmul(p1[:, 0:512],
                                         wslabA[:, kc * SLAB + mt * 128:kc * SLAB + (mt + 1) * 128],
                                         l2b[:, kc * TH2 + nb * 512:kc * TH2 + (nb + 1) * 512],
                                         start=(kc == 0), stop=(kc == 7),
                                         skip_group_check=True)
                        TE.do(fq1)
                        tv = TE.tick()
                        SE.wait(TE, tv)
                        def fsq1(s, mt=mt, nb=nb, sb_=sb_, p1=p1):
                            col = (sb_ * SLAB + mt * 128) // 128
                            s.activation(uslab[:, mt * TH2 + nb * 512:mt * TH2 + (nb + 1) * 512],
                                         p1[:, 0:512], Act.Silu,
                                         bias=b1c[:, col:col + 1], scale=1.0)
                        SE.do(fsq1)
                        sv = SE.tick()
                        lastuse[id(p1)] = (SE, sv)
                        it_f += 1
                TE.wait(SE, SE.n)
                it_f = 0
                for mc in range(8):
                    for nb in range(NB2):
                        p2 = pm if it_f % 2 == 0 else pmB
                        bank_wait(TE, p2)
                        def fq2(t, mc=mc, nb=nb, p2=p2, w2slab=w2slab):
                            for kc in range(MTS):
                                t.matmul(p2[:, 0:512],
                                         w2slab[:, kc * D + mc * 128:kc * D + (mc + 1) * 128],
                                         uslab[:, kc * TH2 + nb * 512:kc * TH2 + (nb + 1) * 512],
                                         start=(kc == 0), stop=(kc == MTS - 1),
                                         skip_group_check=True)
                        TE.do(fq2)
                        tv = TE.tick()
                        VE.wait(TE, tv)
                        def facc2(v, mc=mc, nb=nb, p2=p2):
                            sl = slice(mc * TH2 + nb * 512, mc * TH2 + (nb + 1) * 512)
                            v.scalar_tensor_tensor(accb[:, sl], p2[:, 0:512], 0.5,
                                                   accb[:, sl], Alu.mult, Alu.add)
                        VE.do(facc2)
                        vv = VE.tick()
                        lastuse[id(p2)] = (VE, vv)
                        it_f += 1
                sfree[id(wslabA)] = (tks, TE.n)
                sfree[id(w2slab)] = (tks, TE.n)

            VE.wait(SE, sv_b2h)
            for mc in range(8):
                def fbias(v, mc=mc):
                    v.tensor_scalar(accb[:, mc * TH2:(mc + 1) * TH2],
                                    accb[:, mc * TH2:(mc + 1) * TH2],
                                    b2cc[:, mc:mc + 1], None, Alu.add)
                VE.do(fbias)
            vv_b = VE.tick()
            TE.wait(VE, vv_b)

            prev_vv = VE.n
            prev_store = dma_n[0]
            for it in range(TH2 // 128):
                dma(xtile[:], xs_i[hb + it * 128:hb + (it + 1) * 128, :],
                    (vks, prev_vv))
                ldx = dma_n[0]
                VE.wait_sem(ld, ldx)
                for kc in range(8):
                    pt = ptrp if kc % 2 == 0 else ptrpB
                    bank_wait(TE, pt)
                    def ftr(t, it=it, kc=kc, pt=pt):
                        t.transpose(pt[:, 0:128],
                                    accb[:, kc * TH2 + it * 128:kc * TH2 + (it + 1) * 128],
                                    idf[:])
                    TE.do(ftr)
                    tv = TE.tick()
                    VE.wait(TE, tv)
                    def fadd(v, kc=kc, pt=pt):
                        v.tensor_tensor(sq_[:, kc * 128:(kc + 1) * 128], pt[:, 0:128],
                                        xtile[:, kc * 128:(kc + 1) * 128], Alu.add)
                    VE.do(fadd)
                    vv = VE.tick()
                    lastuse[id(pt)] = (VE, vv)
                prev_vv = VE.n
                dma(out_o[hb + it * 128:hb + (it + 1) * 128, :], sq_[:, 0:1024],
                    (vks, prev_vv))
                prev_store = dma_n[0]
                VE.wait_sem(ld, prev_store - 16)
    return es


def build_C():
    nc = bass.Bass("TRN2", target_bir_lowering=False, detect_race_conditions=False)
    io = dict(
        xn_i=nc.dram_tensor("xn_fm", [D, TSL], BF16, kind="ExternalInput").ap(),
        l2_i=nc.dram_tensor("ln2h_fm", [D, TSL], BF16, kind="ExternalInput").ap(),
        y_i=nc.dram_tensor("yin", [H * DV, TSL], BF16, kind="ExternalInput").ap(),
        xs_i=nc.dram_tensor("xs", [TSL, D], F32, kind="ExternalInput").ap(),
        wu1_i=nc.dram_tensor("wu1", [D, RR], F32, kind="ExternalInput").ap(),
        bu1_i=nc.dram_tensor("bu1_c", [64, 1], F32, kind="ExternalInput").ap(),
        wu2_i=nc.dram_tensor("wu2", [RR, D], F32, kind="ExternalInput").ap(),
        bu2_i=nc.dram_tensor("bu2_c", [128, 8], F32, kind="ExternalInput").ap(),
        wo_i=nc.dram_tensor("wo", [H * DV, D], F32, kind="ExternalInput").ap(),
        fnw_i=nc.dram_tensor("fnw_c", [128, 8], F32, kind="ExternalInput").ap(),
        wf1_i=nc.dram_tensor("wff1", [D, DFF], F32, kind="ExternalInput").ap(),
        wf3_i=nc.dram_tensor("wff3", [D, DFF], F32, kind="ExternalInput").ap(),
        wf2_i=nc.dram_tensor("wff2", [DFF, D], F32, kind="ExternalInput").ap(),
        q1_i=nc.dram_tensor("fc1q", [D, DFF], I32, kind="ExternalInput").ap(),
        s1_i=nc.dram_tensor("fc1s_bc", [8 * 128, DFF], F32, kind="ExternalInput").ap(),
        b1_i=nc.dram_tensor("fc1b_c", [128, 32], F32, kind="ExternalInput").ap(),
        q2_i=nc.dram_tensor("fc2q", [DFF, D], I32, kind="ExternalInput").ap(),
        s2_i=nc.dram_tensor("fc2s_bc", [32 * 128, D], F32, kind="ExternalInput").ap(),
        b2c_i=nc.dram_tensor("fc2b_c", [128, 8], F32, kind="ExternalInput").ap(),
        idf_i=nc.dram_tensor("identf", [128, 128], F32, kind="ExternalInput").ap(),
        out_o=nc.dram_tensor("outp", [TSL, D], F32, kind="ExternalOutput").ap(),
    )
    es0 = ExitStack()
    E = ECtx(nc, es0)
    es_c = emit_C(nc, E, io, fused=False)
    E.GE.wait_sem(E.ld, E.dma_n[0])
    blk = es0.enter_context(nc.Block())
    with es0, es_c:
        finalize_engs(nc, E, blk)
    return nc


def stage_C_inputs(inputs, A_out, B_out):
    maps = []
    identf = np.eye(128, dtype=np.float32)
    x = np.ascontiguousarray(inputs["L0_attn_x"])
    s1bc = np.repeat(np.ascontiguousarray(inputs["L0_ffn_fc1_scales"], np.float32),
                     128, axis=0)
    s2bc = np.repeat(np.ascontiguousarray(inputs["L0_ffn_fc2_scales"], np.float32),
                     128, axis=0)
    b1c = host_colvec(inputs["L0_ffn_fc1_bias"])
    b2c = host_colvec(inputs["L0_ffn_fc2_bias"])
    fnw = host_colvec(inputs["L0_attn_ff_norm_w"])
    bu2 = host_colvec(inputs["L0_attn_bu2"])
    for c in range(8):
        b, sl_i = c // 4, c % 4
        sl = slice(sl_i * TSL, (sl_i + 1) * TSL)
        yin = np.concatenate([B_out[b * 4 + hp]["yout"][:, sl] for hp in range(4)],
                             axis=0)
        maps.append(dict(
            xn_fm=A_out[c]["xn_fm"], ln2h_fm=A_out[c]["ln2h_fm"], yin=yin,
            xs=np.ascontiguousarray(x[b, sl], np.float32),
            wu1=np.ascontiguousarray(inputs["L0_attn_Wu1"], np.float32),
            bu1_c=np.ascontiguousarray(inputs["L0_attn_bu1"], np.float32)[:, None],
            wu2=np.ascontiguousarray(inputs["L0_attn_Wu2"], np.float32),
            bu2_c=bu2, wo=np.ascontiguousarray(inputs["L0_attn_Wo"], np.float32),
            fnw_c=fnw,
            wff1=np.ascontiguousarray(inputs["L0_attn_Wff1"], np.float32),
            wff3=np.ascontiguousarray(inputs["L0_attn_Wff3"], np.float32),
            wff2=np.ascontiguousarray(inputs["L0_attn_Wff2"], np.float32),
            fc1q=np.ascontiguousarray(inputs["L0_ffn_fc1_q"], np.int32),
            fc1s_bc=s1bc, fc1b_c=b1c,
            fc2q=np.ascontiguousarray(inputs["L0_ffn_fc2_q"], np.int32),
            fc2s_bc=s2bc, fc2b_c=b2c, identf=identf))
    return maps



RG4 = [[0, 1, 2, 3], [4, 5, 6, 7]]


def build_F():
    nc = bass.Bass("TRN2", target_bir_lowering=False, detect_race_conditions=False)

    def ein(n, sh, dt):
        return nc.dram_tensor(n, sh, dt, kind="ExternalInput").ap()

    def loc(n, sh, dt):
        return nc.dram_tensor(n, sh, dt, kind="Internal", addr_space="Local").ap()

    xs_i = ein("xs", [TSL, D], F32)
    ident_i = ein("ident", [128, 128], BF16)
    ioA = dict(
        xs=xs_i, xh=ein("xh", [128, D], F32), nw_c=ein("nw_c", [128, 8], F32),
        cb_c=ein("cb_c", [128, 8], F32), cw_c=ein("cw_c", [128, 32], F32),
        g2_c=ein("g2_c", [128, 8], F32), b2_c=ein("b2_c", [128, 8], F32),
        wau_i=ein("wau", [D, RR], F32), bau_c=ein("bau_c", [64, 1], F32),
        ident=ident_i,
        h_au=loc("hau_loc", [D + RR, TSL], BF16),
        xn_fm_o=loc("xnf_loc", [D, TSL], BF16),
        ln2_fm_o=loc("l2f_loc", [D, TSL], BF16),
    )
    hall = loc("hall", [4 * (D + RR), TSL], BF16)
    yrs = loc("yrs", [4 * 512, TSL], BF16)
    ysl = loc("ysl", [512, TSL], BF16)

    es0 = ExitStack()
    E = ECtx(nc, es0)
    cgs = es0.enter_context(nc.semaphore("cgs"))

    fnsA, esA, (sdA, nstA) = make_A(nc, ioA)
    esA.close()

    E.GE.wait_sem(sdA, nstA)
    E.GE.do(lambda g: g.collective_compute(
        "AllGather", Alu.bypass, ins=[ioA["h_au"]], outs=[hall],
        replica_groups=RG4).then_inc(cgs, 1))

    ioB = dict(
        wq_i=ein("wq_s", [D, 128], F32), wk_i=ein("wk_s", [D, 128], F32),
        wv_i=ein("wv_s", [D, 128], F32), wsp_i=ein("wsp_s", [D, 128], F32),
        wad_i=ein("wad_s", [RR, 128], F32), bad_i=ein("bad_s", [128, 1], F32),
        wbe_i=ein("wbe_s", [D, 2], F32), bbe_i=ein("bbe_s", [128, 2], F32),
        asp_i=ein("asp_c", [128, 1], F32), bsp_i=ein("bsp_c", [128, 2], F32),
        hnw_i=ein("hnw_t", [128, 64], F32),
        oexp_i=ein("oexp", [128, NSHIFT * 128], F32),
        wts_i=ein("wts", [128, NPOS * SBLK], F32),
        invc_i=ein("invc512", [128, GRAN], F32),
        ipow_i=ein("ipow512", [128, GRAN], F32),
        hpow_i=ein("hpow512", [128, GRAN], F32),
        ident_i=ident_i, mau_i=ein("maskUS", [128, 64], U8),
        mui_i=ein("maskUI", [128, 64], U8),
        hall=hall, yrs=yrs, hsel_i=ein("hsel", [128, 4], F32),
        const_wait=(sdA, nstA), g0_wait=(cgs, 1),
    )
    esB = emit_B(nc, E, ioB, Ttot=T, fused=True)
    esB.close()

    E.GE.wait_sem(E.ld, E.dma_n[0])
    E.GE.do(lambda g: g.collective_compute(
        "ReduceScatter", Alu.add, ins=[yrs], outs=[ysl],
        replica_groups=RG4).then_inc(cgs, 1))

    ioC = dict(
        xn_i=ioA["xn_fm_o"], l2_i=ioA["ln2_fm_o"], y_i=ysl, xs_i=xs_i,
        wu1_i=ein("wu1", [D, RR], F32), bu1_i=ein("bu1_c", [64, 1], F32),
        wu2_i=ein("wu2", [RR, D], F32), bu2_i=ein("bu2_c", [128, 8], F32),
        wo_i=ein("wo", [H * DV, D], F32), fnw_i=ein("fnw_c", [128, 8], F32),
        wf1_i=ein("wff1", [D, DFF], F32), wf3_i=ein("wff3", [D, DFF], F32),
        wf2_i=ein("wff2", [DFF, D], F32),
        q1_i=ein("fc1q", [D, DFF], I32),
        s1_i=ein("fc1s_bc", [8 * 128, DFF], F32),
        b1_i=ein("fc1b_c", [128, 32], F32),
        q2_i=ein("fc2q", [DFF, D], I32),
        s2_i=ein("fc2s_bc", [32 * 128, D], F32),
        b2c_i=ein("fc2b_c", [128, 8], F32),
        idf_i=ein("identf", [128, 128], F32),
        out_o=nc.dram_tensor("outp", [TSL, D], F32, kind="ExternalOutput").ap(),
        const_wait=(cgs, 2),
    )
    esC = emit_C(nc, E, ioC, fused=True)
    E.GE.wait_sem(E.ld, E.dma_n[0])

    blk = es0.enter_context(nc.Block())
    with es0, esC:
        @blk.sync
        def _(sy):
            fnsA["sync"](sy)
            for wait, fn in E.dmas:
                if wait is not None:
                    sy.wait_ge(wait[0], wait[1])
                fn(sy).then_inc(E.ld, 16)

        @blk.tensor
        def _(t):
            fnsA["tensor"](t)
            E.TE.run(t)

        @blk.scalar
        def _(s):
            fnsA["scalar"](s)
            E.SE.run(s)

        @blk.vector
        def _(v):
            fnsA["vector"](v)
            E.VE.run(v)

        @blk.gpsimd
        def _(g_):
            fnsA["gpsimd"](g_)
            E.GE.run(g_)
    return nc


def stage_F_inputs(inputs):
    mapsA = stage_A_inputs(inputs)
    mus, mui = host_masks()
    oexp = host_shift_exp()
    wtsh = host_win_weights()
    invc = host_invc()
    ipow = host_ipow()
    hpow = host_hpow()
    identf = np.eye(128, dtype=np.float32)
    x = np.ascontiguousarray(inputs["L0_attn_x"])
    s1bc = np.repeat(np.ascontiguousarray(inputs["L0_ffn_fc1_scales"], np.float32),
                     128, axis=0)
    s2bc = np.repeat(np.ascontiguousarray(inputs["L0_ffn_fc2_scales"], np.float32),
                     128, axis=0)
    b1c = host_colvec(inputs["L0_ffn_fc1_bias"])
    b2c = host_colvec(inputs["L0_ffn_fc2_bias"])
    fnw = host_colvec(inputs["L0_attn_ff_norm_w"])
    bu2 = host_colvec(inputs["L0_attn_bu2"])
    maps = []
    for c in range(8):
        b, hp = c // 4, c % 4
        h0, h1 = 2 * hp, 2 * hp + 1
        cols = slice(h0 * 64, h0 * 64 + 128)
        m = dict(mapsA[c])
        wbe_s = np.ascontiguousarray(inputs["L0_attn_Wbeta"][:, h0:h0 + 2],
                                     np.float32)
        bbe = np.zeros((128, 2), np.float32)
        bbe[:, 0] = inputs["L0_attn_bbeta"][h0]
        bbe[:, 1] = inputs["L0_attn_bbeta"][h1]
        asp = np.concatenate([inputs["L0_attn_alpha_spike"][0, h0],
                              inputs["L0_attn_alpha_spike"][0, h1]]
                             ).astype(np.float32)[:, None]
        bsp = np.zeros((128, 2), np.float32)
        bsp[0:64, 0] = inputs["L0_attn_beta_spike"][0, h0]
        bsp[64:128, 0] = inputs["L0_attn_beta_spike"][0, h1]
        bsp[:, 1] = 1.0
        hnw = np.zeros((128, 64), np.float32)
        hnw[0:64] = inputs["L0_attn_head_norm_w"][h0][None, :]
        hnw[64:128] = inputs["L0_attn_head_norm_w"][h1][None, :]
        hsel = np.zeros((128, 4), np.float32)
        hsel[:, hp] = 1.0
        m.update(
            wq_s=np.ascontiguousarray(inputs["L0_attn_Wq"][:, cols], np.float32),
            wk_s=np.ascontiguousarray(inputs["L0_attn_Wk"][:, cols], np.float32),
            wv_s=np.ascontiguousarray(inputs["L0_attn_Wv"][:, cols], np.float32),
            wsp_s=np.ascontiguousarray(inputs["L0_attn_Wspike"][:, cols], np.float32),
            wad_s=np.ascontiguousarray(inputs["L0_attn_Wad"][:, cols], np.float32),
            bad_s=np.ascontiguousarray(inputs["L0_attn_bad"][cols],
                                       np.float32)[:, None],
            wbe_s=wbe_s, bbe_s=bbe, asp_c=asp, bsp_c=bsp, hnw_t=hnw,
            oexp=oexp, wts=wtsh, invc512=invc, ipow512=ipow, hpow512=hpow,
            maskUS=mus, maskUI=mui, hsel=hsel,
            wu1=np.ascontiguousarray(inputs["L0_attn_Wu1"], np.float32),
            bu1_c=np.ascontiguousarray(inputs["L0_attn_bu1"], np.float32)[:, None],
            wu2=np.ascontiguousarray(inputs["L0_attn_Wu2"], np.float32),
            bu2_c=bu2,
            wo=np.ascontiguousarray(inputs["L0_attn_Wo"], np.float32),
            fnw_c=fnw,
            wff1=np.ascontiguousarray(inputs["L0_attn_Wff1"], np.float32),
            wff3=np.ascontiguousarray(inputs["L0_attn_Wff3"], np.float32),
            wff2=np.ascontiguousarray(inputs["L0_attn_Wff2"], np.float32),
            fc1q=np.ascontiguousarray(inputs["L0_ffn_fc1_q"], np.int32),
            fc1s_bc=s1bc, fc1b_c=b1c,
            fc2q=np.ascontiguousarray(inputs["L0_ffn_fc2_q"], np.int32),
            fc2s_bc=s2bc, fc2b_c=b2c, identf=identf)
        maps.append(m)
    return maps


def kernel(**inputs):
    nc = build_F()
    maps = stage_F_inputs(inputs)
    res = run_bass_kernel_spmd(nc, maps, core_ids=list(range(8)))
    out = np.empty((B, T, D), np.float32)
    for c in range(8):
        b, s = c // 4, c % 4
        out[b, s * TSL:(s + 1) * TSL] = res.results[c]["outp"]
    return out


def kernel3(**inputs):
    nc_a = build_A()
    maps_a = stage_A_inputs(inputs)
    res_a = run_bass_kernel_spmd(nc_a, maps_a, core_ids=list(range(8)))
    A_out = res_a.results

    nc_b = build_B(T)
    maps_b = stage_B_inputs(inputs, A_out, Ttot=T)
    res_b = run_bass_kernel_spmd(nc_b, maps_b, core_ids=list(range(8)))
    B_out = res_b.results

    nc_c = build_C()
    maps_c = stage_C_inputs(inputs, A_out, B_out)
    res_c = run_bass_kernel_spmd(nc_c, maps_c, core_ids=list(range(8)))

    out = np.empty((B, T, D), np.float32)
    for c in range(8):
        b, s = c // 4, c % 4
        out[b, s * TSL:(s + 1) * TSL] = res_c.results[c]["outp"]
    return out
